# version 06
# baseline (speedup 1.0000x reference)
import contextlib
import numpy as np
import ml_dtypes

import concourse.bass as bass
import concourse.mybir as mybir
from concourse.bass_utils import run_bass_kernel_spmd

F32 = mybir.dt.float32
BF16 = mybir.dt.bfloat16
I32 = mybir.dt.int32
AF = mybir.ActivationFunctionType
ALU = mybir.AluOpType
AX = mybir.AxisListType

B, S, DM = 2, 8192, 2048
H, D, BS = 32, 64, 256
FF = 5632
F2 = 2 * FF
NCORES = 8
TOK = (B * S) // NCORES
NBLK = TOK // BS
NMT = TOK // 128
EPS = 1e-5
N1 = F2 * DM
N2 = DM * FF

ENGS = ("sync", "vector", "scalar", "tensor", "gpsimd")


def _decay():
    h = np.arange(H, dtype=np.float64)
    base = 1.0 / 2.0 ** (8.0 / H)
    slope = (base ** (h + 1.0)) * (1.0 - 0.0 / (32 - 1 + 1e-05) + 1e-05)
    r = np.arange(BS, dtype=np.float64) + 1.0
    qd = np.exp(-slope[:, None] * r[None, :]).astype(np.float32)
    kd = np.exp(-slope[:, None] * (BS - r[None, :])).astype(np.float32)
    bd = np.exp(-slope * BS).astype(np.float32)
    i_ = np.arange(128, dtype=np.float64)
    dif = i_[None, :] - i_[:, None]
    DD0 = np.zeros((128, H * 256), np.float32)
    QDB = np.zeros((128, (H // 2) * 256), np.float32)
    KDC = np.zeros((128, H * 2), np.float32)
    for hh in range(H):
        t0 = np.where(dif >= 0, np.exp(-slope[hh] * dif), 0.0)
        t1 = np.exp(-slope[hh] * (128.0 + dif))
        DD0[:, hh * 256:hh * 256 + 128] = t0.astype(np.float32)
        DD0[:, hh * 256 + 128:hh * 256 + 256] = t1.astype(np.float32)
        p0 = 64 * (hh % 2)
        QDB[p0:p0 + 64, (hh // 2) * 256:(hh // 2) * 256 + 256] = qd[hh][None, :]
        KDC[:, hh * 2] = kd[hh, 0:128]
        KDC[:, hh * 2 + 1] = kd[hh, 128:256]
    return qd, kd, bd, DD0, QDB, KDC


_QD, _KD, _BD, _DD0, _QDB, _KDC = _decay()


def _t_pred(core):
    out = np.zeros((8, 64, H * D), np.float32)
    g, j = core // 4, core % 4
    for cp in range(8):
        if cp // 4 != g or cp % 4 >= j:
            continue
        w = _BD.astype(np.float64) ** ((j - 1 - (cp % 4)) * NBLK)
        for hh in range(H):
            out[cp, :, hh * D:(hh + 1) * D] = np.float32(w[hh])
    return out


def _perm_qkv():
    perm = np.zeros(3 * H * D, np.int64)
    for hh in range(H):
        for d in range(D):
            perm[0 * H * D + hh * D + d] = 192 * hh + d
            perm[1 * H * D + hh * D + d] = 192 * hh + 64 + d
            perm[2 * H * D + hh * D + d] = 192 * hh + 128 + d
    return perm


_CHAIN = ("vector", "scalar", "gpsimd")


class Prog:
    def __init__(self, nc, stack):
        self.nc, self.stack = nc, stack
        self.q = {e: [] for e in ENGS}
        self.h, self.cnt = {}, {}
        self.last = {e: None for e in ENGS}
        self.waited = {e: {} for e in ENGS}

    def sem(self, name):
        if name not in self.h:
            self.h[name] = self.stack.enter_context(self.nc.semaphore(name))
            self.cnt[name] = 0
        return self.h[name]

    def emit(self, eng, fn):
        self.q[eng].append(fn)

    def _wait_raw(self, eng, sem, v):
        if v <= 0:
            return
        if self.waited[eng].get(sem, 0) >= v:
            return
        self.waited[eng][sem] = v
        hh = self.sem(sem)
        self.emit(eng, lambda e, hh=hh, v=v: e.wait_ge(hh, v))

    def _chain_wait(self, eng):
        if self.last[eng] is not None:
            s, v = self.last[eng]
            self._wait_raw(eng, s, v)

    def dma(self, eng, out, in_, sem):
        self._chain_wait(eng)
        hh = self.sem(sem)
        self.cnt[sem] += 16
        self.emit(eng, lambda e, out=out, in_=in_, hh=hh:
                  e.dma_start(out=out, in_=in_).then_inc(hh, 16))
        return self.cnt[sem]

    def dmaT(self, eng, out, in_, sem):
        self._chain_wait(eng)
        hh = self.sem(sem)
        self.cnt[sem] += 16
        self.emit(eng, lambda e, out=out, in_=in_, hh=hh:
                  e.dma_start_transpose(out=out, in_=in_).then_inc(hh, 16))
        return self.cnt[sem]

    def wait(self, eng, sem, val=None):
        self.sem(sem)
        v = self.cnt[sem] if val is None else val
        self._wait_raw(eng, sem, v)

    def op(self, eng, fn, sem=None, k=1):
        if eng in _CHAIN:
            self._chain_wait(eng)
            if sem is None:
                sem = "_ch_" + eng
                k = 1
        if sem is not None:
            hh = self.sem(sem)
            self.cnt[sem] += k
            self.emit(eng, lambda e, fn=fn, hh=hh, k=k: fn(e).then_inc(hh, k))
            if eng in _CHAIN or eng == "tensor":
                self.last[eng] = (sem, self.cnt[sem])
            return self.cnt[sem]
        self.emit(eng, fn)
        return None

    def val(self, sem):
        self.sem(sem)
        return self.cnt[sem]

    def drain(self, eng, *sems):
        for s in sems:
            self.wait(eng, s)

    def barrier(self):
        hh = self.sem("phase")
        for eng in ENGS:
            if eng in _CHAIN:
                self._chain_wait(eng)
            elif self.last.get(eng) is not None:
                s, v = self.last[eng]
                self._wait_raw(eng, s, v)
            self.emit(eng, lambda e, hh=hh: e.nop().then_inc(hh, 1))
        self.cnt["phase"] += len(ENGS)
        v = self.cnt["phase"]
        for eng in ENGS:
            self._wait_raw(eng, "phase", v)

    def finalize(self, block):
        for eng in ENGS:
            fns = self.q[eng]

            def body(e, fns=fns):
                for fn in fns:
                    fn(e)
            getattr(block, eng)(body)


class Rot:

    def __init__(self, bufs):
        self.bufs = bufs
        self.gates = [None] * len(bufs)
        self.i = -1

    def next(self, P, producer_eng):
        self.i += 1
        j = self.i % len(self.bufs)
        if self.gates[j] is not None:
            s, v = self.gates[j]
            P.wait(producer_eng, s, v)
        return j, self.bufs[j]

    def gate(self, j, sem, val):
        self.gates[j] = (sem, val)


def build_nc(max_phase=99, taps=(), n_rg=NCORES):
    nc = bass.Bass("TRN2", target_bir_lowering=False)

    x_in = nc.dram_tensor("x_in", [TOK, DM], F32, kind="ExternalInput").ap()
    w_attn = nc.dram_tensor("w_attn", [4 * DM, DM], F32, kind="ExternalInput").ap()
    outw_in = nc.dram_tensor("outw_in", [DM, DM], F32, kind="ExternalInput").ap()
    w1_in = nc.dram_tensor("w1_in", [F2, DM], F32, kind="ExternalInput").ap()
    w2_in = nc.dram_tensor("w2_in", [DM, FF], F32, kind="ExternalInput").ap()
    w1_sl = nc.dram_tensor("w1_sl", [F2 // 8, DM], F32, kind="ExternalInput").ap()
    w2_sl = nc.dram_tensor("w2_sl", [DM // 8, FF], F32, kind="ExternalInput").ap()
    ffnw_b = nc.dram_tensor("ffnw_b", [128, DM], F32, kind="ExternalInput").ap()
    attnw_b = nc.dram_tensor("attnw_b", [128, DM], F32, kind="ExternalInput").ap()
    ident_in = nc.dram_tensor("ident_in", [128, 128], F32, kind="ExternalInput").ap()
    dd0_in = nc.dram_tensor("dd0_in", [128, H * 256], F32, kind="ExternalInput").ap()
    qdb_in = nc.dram_tensor("qdb_in", [128, (H // 2) * 256], F32, kind="ExternalInput").ap()
    kdc_in = nc.dram_tensor("kdc_in", [128, H * 2], F32, kind="ExternalInput").ap()
    tpred_in = nc.dram_tensor("tpred_in", [8, 64, H * D], F32, kind="ExternalInput").ap()
    y_out = nc.dram_tensor("y_out", [TOK, DM], F32, kind="ExternalOutput").ap()

    x1_scr = nc.dram_tensor("x1_scr", [TOK, DM], F32).ap()
    x2_scr = nc.dram_tensor("x2_scr", [TOK, DM], F32).ap()
    xi_scr = nc.dram_tensor("xi_scr", [TOK, DM], BF16).ap()
    w1q_scr = nc.dram_tensor("w1q_scr", [F2, DM], BF16).ap()
    w2q_scr = nc.dram_tensor("w2q_scr", [DM, FF], BF16).ap()
    hid_scr = nc.dram_tensor("hid_scr", [TOK, FF], F32).ap()
    hq_scr = nc.dram_tensor("hq_scr", [TOK, FF], BF16).ap()
    qkv_scr = nc.dram_tensor("qkv_scr", [TOK, 3 * DM], F32).ap()
    gate_scr = nc.dram_tensor("gate_scr", [TOK, DM], F32).ap()
    qT_scr = nc.dram_tensor("qT_scr", [DM, TOK], F32).ap()
    kT_scr = nc.dram_tensor("kT_scr", [DM, TOK], F32).ap()
    attn_scr = nc.dram_tensor("attn_scr", [TOK, DM], F32).ap()
    cc_sin = nc.dram_tensor("cc_sin", [64, H * D], F32).ap()
    cc_sout = nc.dram_tensor("cc_sout", [8 * 64, H * D], F32).ap()
    cc_min = nc.dram_tensor("cc_min", [1, 2], F32).ap()
    cc_mout = nc.dram_tensor("cc_mout", [1, 2], F32).ap()

    tap_src = {"x1": x1_scr, "x2": x2_scr, "qkv": qkv_scr, "gate": gate_scr,
               "qT": qT_scr, "kT": kT_scr, "attn": attn_scr, "sgath": cc_sout,
               "hid": hid_scr, "xi": xi_scr, "hq": hq_scr}
    tap_dt = {"xi": BF16, "hq": BF16}
    tap_outs = {}
    _STRIP_TAP = None
    _STRIP_A = "strips_a" in taps
    taps = [t for t in taps if t != "strips_a"]
    if "strips" in taps or _STRIP_A:
        taps = [t for t in taps if t != "strips"]
        _STRIP_TAP = nc.dram_tensor("dbg_strips", [512, NMT], F32,
                                    kind="ExternalOutput").ap()
        tap_outs["strips"] = None
    for t in taps:
        src = tap_src[t]
        tap_outs[t] = nc.dram_tensor("dbg_" + t, list(src.tensor.shape),
                                     tap_dt.get(t, F32), kind="ExternalOutput").ap()

    with contextlib.ExitStack() as big:
        P = Prog(nc, big)

        def sb(name, shape, dt=F32, st=None):
            return (st or big).enter_context(nc.sbuf_tensor(name, shape, dt))

        def pst(name, shape, st, dt=F32):
            return st.enter_context(nc.psum_tensor(name, shape, dt))

        ident = sb("ident", [128, 128])
        c384 = sb("c384", [128, 1])
        cm384 = sb("cm384", [128, 1])
        ceps = sb("ceps", [128, 1])
        cinvd = sb("cinvd", [128, 1])
        maski = sb("maski", [128, 1], I32)
        ones_c = sb("ones_c", [128, 1])
        ones_r = sb("ones_r", [1, 128])
        sw1s = sb("sw1s", [128, 1])
        nsw1s = sb("nsw1s", [128, 1])
        w1m = sb("w1m", [128, 1])
        sw2s = sb("sw2s", [128, 1])
        w2m = sb("w2m", [128, 1])
        kdc = sb("kdc", [128, H * 2])
        sinv_s = [[sb(f"sinv_s{a}_{m}", [128, 1]) for m in range(NMT)] for a in range(2)]
        amax_s = [[sb(f"amax_s{a}_{m}", [128, 1]) for m in range(NMT)] for a in range(2)]
        m2q_s = [[sb(f"m2q_s{a}_{m}", [128, 1]) for m in range(NMT)] for a in range(2)]
        fin_s = [[sb(f"fin_s{a}_{m}", [128, 1]) for m in range(NMT)] for a in range(2)]
        Sin_st = sb("Sin_st", [64, H * D])

        def phase0():
            for f in (lambda e: e.memset(c384[:], 384.5),
                      lambda e: e.memset(cm384[:], -384.0),
                      lambda e: e.memset(ceps[:], EPS),
                      lambda e: e.memset(cinvd[:], 1.0 / DM),
                      lambda e: e.memset(maski[:], -32768),
                      lambda e: e.memset(ones_c[:], 1.0),
                      lambda e: e.memset(ones_r[:], 1.0)):
                P.op("vector", f)
            P.dma("sync", ident[:], ident_in, "ld0")
            P.dma("sync", kdc[:], kdc_in, "ld0")
            with contextlib.ExitStack() as ar:
                wt = Rot([sb(f"p0w{i}", [128, FF], st=ar) for i in range(2)])
                acc = sb("p0acc", [128, 2], st=ar)
                acc1 = sb("p0acc1", [128, 1], st=ar)
                acc2 = sb("p0acc2", [128, 1], st=ar)
                red = sb("p0red", [128, 1], st=ar)
                sc2 = sb("p0sc", [1, 2], st=ar)
                pr = pst("p0ps", [1, 2], ar)
                pb = pst("p0pb", [128, 2], ar)
                P.op("vector", lambda e: e.memset(acc1[:], 0.0))
                P.op("vector", lambda e: e.memset(acc2[:], 0.0))
                for accb, tiles, srcap, ncols in ((acc1, (F2 // 8) // 128, w1_sl, DM),
                                                  (acc2, (DM // 8) // 128, w2_sl, FF)):
                    for i in range(tiles):
                        j, buf = wt.next(P, "sync")
                        lv = P.dma("sync", buf[:, 0:ncols],
                                   srcap[i * 128:(i + 1) * 128, :], f"ld1_{j}")
                        P.wait("vector", f"ld1_{j}", lv)
                        P.op("vector", lambda e, buf=buf, ncols=ncols: e.reduce_sum(
                            red[:], buf[:, 0:ncols], axis=AX.X, apply_absolute_value=True))
                        vv = P.op("vector", lambda e, accb=accb: e.tensor_tensor(
                            accb[:], accb[:], red[:], ALU.add), sem="v0")
                        wt.gate(j, "v0", vv)
                P.op("vector", lambda e: e.tensor_scalar(
                    acc[:, 0:1], acc1[:], 1.0, None, ALU.mult))
                P.op("vector", lambda e: e.tensor_scalar(
                    acc[:, 1:2], acc2[:], 1.0, None, ALU.mult), sem="v0")
                P.wait("tensor", "v0")
                P.op("tensor", lambda e: e.matmul(pr[:], ones_c[:], acc[:],
                                                  start=True, stop=True), sem="t0")
                P.wait("scalar", "t0")
                P.op("scalar", lambda e: e.copy(sc2[:], pr[:]), sem="s0")
                P.wait("scalar", "s0")
                P.dma("scalar", cc_min, sc2[:], "st0")
                P.wait("gpsimd", "st0")
                P.op("gpsimd", lambda e: e.collective_compute(
                    "AllReduce", ALU.add, replica_groups=[list(range(n_rg))],
                    ins=[cc_min.opt()], outs=[cc_mout.opt()]), sem="cc0")
                P.wait("sync", "cc0")
                P.dma("sync", sc2[:], cc_mout, "ld2")
                P.wait("vector", "ld2")
                P.op("vector", lambda e: e.tensor_scalar(
                    sc2[:, 0:1], sc2[:, 0:1], 1.0 / N1, 1e-5, ALU.mult, ALU.max))
                P.op("vector", lambda e: e.tensor_scalar(
                    sc2[:, 1:2], sc2[:, 1:2], 1.0 / N2, 1e-5, ALU.mult, ALU.max), sem="v0")
                P.wait("tensor", "v0")
                P.op("tensor", lambda e: e.matmul(pb[:], ones_r[:], sc2[:],
                                                  start=True, stop=True), sem="t0")
                P.wait("vector", "t0")
                P.op("vector", lambda e: e.tensor_copy(w1m[:], pb[:, 0:1]))
                P.op("vector", lambda e: e.tensor_copy(w2m[:], pb[:, 1:2]))
                P.op("vector", lambda e: e.reciprocal(sw1s[:], w1m[:]))
                P.op("vector", lambda e: e.reciprocal(sw2s[:], w2m[:]))
                P.op("vector", lambda e: e.tensor_scalar(
                    nsw1s[:], sw1s[:], -1.0, None, ALU.mult))
                P.wait("vector", "ld0")
                P.barrier()

        def emit_ln(xt_ap, yt_ap, sqt_ap, tmp, vs, ss):
            rs, mn, sd, iv = tmp
            P.op("vector", lambda e: e.reduce_sum(rs[:], xt_ap, axis=AX.X))
            P.op("vector", lambda e: e.tensor_scalar(
                mn[:], rs[:], 1.0 / DM, None, ALU.mult))
            P.op("vector", lambda e: e.tensor_scalar(
                yt_ap, xt_ap, mn[:, 0:1], None, ALU.subtract), sem=vs)
            P.wait("scalar", vs)
            P.op("scalar", lambda e: e.activation(sqt_ap, yt_ap, AF.Square), sem=ss)
            P.wait("vector", ss)
            P.op("vector", lambda e: e.reduce_sum(rs[:], sqt_ap, axis=AX.X), sem=vs)
            P.wait("scalar", vs)
            P.op("scalar", lambda e: e.activation(
                sd[:], rs[:], AF.Sqrt, scale=cinvd[:, 0:1], bias=ceps[:, 0:1]), sem=ss)
            P.wait("vector", ss)
            P.op("vector", lambda e: e.reciprocal(iv[:], sd[:]))
            P.op("vector", lambda e: e.tensor_scalar(
                yt_ap, yt_ap, iv[:, 0:1], None, ALU.mult), sem=vs)

        def phase_ffn(app, src, dst):
            tag = f"f{app}"
            ld, lw, tp, st, vs, ss, ts_, pf = [
                "f" + s_ for s_ in
                ("ld", "lw", "tp", "st", "vs", "ss", "ts", "pf")]
            sinv, amx, m2q, fin = sinv_s[app], amax_s[app], m2q_s[app], fin_s[app]

            with contextlib.ExitStack() as ar:
                xt = Rot([sb(f"{tag}xt{i}", [128, DM], st=ar) for i in range(2)])
                yt = [sb(f"{tag}yt{i}", [128, DM], st=ar) for i in range(2)]
                sqt = [sb(f"{tag}sq{i}", [128, DM], st=ar) for i in range(2)]
                xib = Rot([sb(f"{tag}xib{i}", [128, DM], BF16, st=ar) for i in range(2)])
                tmp = [sb(f"{tag}tmp{i}", [128, 1], st=ar) for i in range(4)]
                am = sb(f"{tag}am", [128, 1], st=ar)
                sc = sb(f"{tag}sc", [128, 1], st=ar)
                fwb = sb(f"{tag}fwb", [128, DM], st=ar)
                P.dma("sync", fwb[:], ffnw_b, ld + "f")
                ssx_base = P.val(ss + "x")
                P.wait("vector", ld + "f")
                for m in range(NMT):
                    j, xbuf = xt.next(P, "sync")
                    lv = P.dma("sync", xbuf[:], src[m * 128:(m + 1) * 128, :],
                               f"{ld}_{j}")
                    b = m % 2
                    P.wait("vector", f"{ld}_{j}", lv)
                    if m >= 2:
                        P.wait("vector", ss + "x", ssx_base + m - 1)
                    emit_ln(xbuf[:], yt[b][:], sqt[b][:], tmp, vs, ss)
                    P.wait("scalar", vs)
                    P.op("scalar", lambda e, b=b: e.activation(
                        sqt[b][:], yt[b][:], AF.Square), sem=ss)
                    P.wait("vector", ss)
                    P.op("vector", lambda e, b=b: e.reduce_sum(
                        tmp[0][:], sqt[b][:], axis=AX.X), sem=vs)
                    P.wait("scalar", vs)
                    P.op("scalar", lambda e: e.activation(
                        tmp[2][:], tmp[0][:], AF.Sqrt,
                        scale=cinvd[:, 0:1], bias=ceps[:, 0:1]), sem=ss)
                    P.wait("vector", ss)
                    P.op("vector", lambda e: e.reciprocal(tmp[3][:], tmp[2][:]))
                    P.op("vector", lambda e, b=b: e.tensor_scalar(
                        yt[b][:], yt[b][:], tmp[3][:, 0:1], None, ALU.mult))
                    P.op("vector", lambda e, b=b: e.tensor_tensor(
                        yt[b][:], yt[b][:], fwb[:], ALU.mult))
                    P.op("vector", lambda e, b=b: e.reduce_max(
                        am[:], yt[b][:], axis=AX.X, apply_absolute_value=True))
                    P.op("vector", lambda e: e.tensor_scalar(
                        am[:], am[:], 1e-5, None, ALU.max))
                    P.op("vector", lambda e: e.reciprocal(sc[:], am[:]))
                    P.op("vector", lambda e: e.tensor_scalar(
                        sc[:], sc[:], 127.0, None, ALU.mult))
                    P.op("vector", lambda e, m=m: e.tensor_scalar(
                        sinv[m][:], am[:], 1.0 / 127.0, None, ALU.mult))
                    P.op("vector", lambda e, b=b: e.tensor_scalar(
                        yt[b][:], yt[b][:], sc[:, 0:1], 384.5, ALU.mult, ALU.add))
                    xv = P.op("vector", lambda e, b=b: e.tensor_scalar(
                        yt[b][:].bitcast(I32), yt[b][:].bitcast(I32), maski[:, 0:1],
                        None, ALU.bitwise_and), sem=vs)
                    xt.gate(j, vs, xv)
                    jx, xbuf2 = xib.next(P, "scalar")
                    P.wait("scalar", vs, xv)
                    P.op("scalar", lambda e, b=b, xbuf2=xbuf2: e.activation(
                        xbuf2[:], yt[b][:], AF.Identity, bias=cm384[:, 0:1]), sem=ss + "x")
                    sv = P.dma("scalar", xi_scr[m * 128:(m + 1) * 128, :],
                               xbuf2[:], f"{st}_{jx}")
                    xib.gate(jx, f"{st}_{jx}", sv)
                if _STRIP_TAP is not None and app == 0 and _STRIP_A:
                    P.wait("sync", vs)
                    for m in range(NMT):
                        P.dma("sync", _STRIP_TAP[0:128, m:m + 1], sinv[m][:], "dbgs")
                P.drain("sync", f"{st}_0", f"{st}_1")
                P.barrier()

            with contextlib.ExitStack() as ar:
                xiT = sb(f"{tag}xiT", [128, NMT * TOK], BF16, st=ar)
                wf = Rot([sb(f"{tag}wf{i}", [128, DM], st=ar) for i in range(2)])
                wqa = sb(f"{tag}wqa", [128, DM], st=ar)
                wqb = sb(f"{tag}wqb", [128, DM], st=ar)
                wqo = Rot([sb(f"{tag}wqo{i}", [128, DM], BF16, st=ar) for i in range(2)])
                w1tg = [sb(f"{tag}w1tg{i}", [128, 16 * 512], BF16, st=ar) for i in range(2)]
                w1tu = [sb(f"{tag}w1tu{i}", [128, 16 * 512], BF16, st=ar) for i in range(2)]
                rel = [sb(f"{tag}rel{i}", [128, 512], st=ar) for i in range(2)]
                hid = Rot([sb(f"{tag}hid{i}", [128, 512], st=ar) for i in range(2)])
                red1 = sb(f"{tag}red1", [128, 1], st=ar)
                t1 = sb(f"{tag}t1", [128, 1], st=ar)
                ht = Rot([sb(f"{tag}ht0", [128, FF // 2], st=ar)])
                hb = Rot([sb(f"{tag}hb0", [128, FF // 2], BF16, st=ar)])
                pg = [pst(f"{tag}pg{i}", [128, 512], ar) for i in range(2)]
                pu = [pst(f"{tag}pu{i}", [128, 512], ar) for i in range(2)]
                for m in range(NMT):
                    P.op("vector", lambda e, m=m: e.memset(amx[m][:], 0.0))
                for ft in range(NMT):
                    P.dmaT("sync", xiT[:, ft * TOK:(ft + 1) * TOK],
                           xi_scr[:, ft * 128:(ft + 1) * 128], tp + "x")
                P.wait("tensor", tp + "x")
                nch = F2 // 1024
                mm_iter = 0
                pf_base = P.val(pf)
                vsr_base = P.val(vs + "r")
                hid_sv = [0] * NMT
                hq_sv = [0] * NMT

                w2_subs = [(mq, c0) for mq in range(NMT)
                           for c0 in (0, 2048, 4096)]

                def quant_w2_tiles(lo, hi):
                    for mq, c0 in w2_subs[lo:hi]:
                        cn = min(2048, FF - c0)
                        jw, wbuf = wf.next(P, "sync")
                        lv = P.dma("sync", wbuf[:, 0:cn],
                                   w2_in[mq * 128:(mq + 1) * 128, c0:c0 + cn],
                                   f"{lw}_{jw}")
                        jq, qbuf = wqo.next(P, "vector")
                        P.wait("vector", f"{lw}_{jw}", lv)
                        P.op("vector", lambda e, wbuf=wbuf, cn=cn: e.tensor_scalar(
                            wqa[:, 0:cn], wbuf[:, 0:cn], sw2s[:, 0:1], 0.5,
                            ALU.mult, ALU.is_gt))
                        vv = P.op("vector", lambda e, wbuf=wbuf, cn=cn: e.tensor_scalar(
                            wqb[:, 0:cn], wbuf[:, 0:cn], sw2s[:, 0:1], -0.5,
                            ALU.mult, ALU.is_lt), sem=vs + "w")
                        wf.gate(jw, vs + "w", vv)
                        qv = P.op("vector", lambda e, qbuf=qbuf, cn=cn: e.tensor_tensor(
                            qbuf[:, 0:cn], wqa[:, 0:cn], wqb[:, 0:cn], ALU.subtract),
                            sem=vs + "q")
                        P.wait("scalar", vs + "q", qv)
                        sv2 = P.dma("scalar",
                                    w2q_scr[mq * 128:(mq + 1) * 128, c0:c0 + cn],
                                    qbuf[:, 0:cn], f"{st}w{jq}")
                        wqo.gate(jq, f"{st}w{jq}", sv2)

                def prep(n):
                    bn = n % 2
                    if app == 0:
                        for rbase in (n * 512, FF + n * 512):
                            for s4 in range(4):
                                r0 = rbase + s4 * 128
                                jw, wbuf = wf.next(P, "sync")
                                lv = P.dma("sync", wbuf[:], w1_in[r0:r0 + 128, :],
                                           f"{lw}_{jw}")
                                P.wait("vector", f"{lw}_{jw}", lv)
                                P.op("vector", lambda e, wbuf=wbuf: e.tensor_scalar(
                                    wqa[:], wbuf[:], sw1s[:, 0:1], 0.5, ALU.mult,
                                    ALU.is_gt))
                                vv = P.op("vector", lambda e, wbuf=wbuf: e.tensor_scalar(
                                    wqb[:], wbuf[:], nsw1s[:, 0:1], 0.5, ALU.mult,
                                    ALU.is_gt), sem=vs + "w")
                                wf.gate(jw, vs + "w", vv)
                                jq, qbuf = wqo.next(P, "vector")
                                qv = P.op("vector", lambda e, qbuf=qbuf: e.tensor_tensor(
                                    qbuf[:], wqa[:], wqb[:], ALU.subtract), sem=vs + "q")
                                P.wait("scalar", vs + "q", qv)
                                sv = P.dma("scalar", w1q_scr[r0:r0 + 128, :], qbuf[:],
                                           f"{st}w{jq}")
                                wqo.gate(jq, f"{st}w{jq}", sv)
                        quant_w2_tiles(n * 5 if n < 9 else 45 + (n - 9) * 2,
                                       (n + 1) * 5 if n < 9 else 45 + (n - 8) * 2)
                        for jq2 in range(2):
                            P.wait("sync", f"{st}w{jq2}")
                    if n >= 2:
                        P.wait("sync", pf, pf_base + 16 * (n - 1))
                    for ft in range(16):
                        P.dmaT("sync", w1tg[bn][:, ft * 512:(ft + 1) * 512],
                               w1q_scr[n * 512:(n + 1) * 512, ft * 128:(ft + 1) * 128],
                               f"{tp}w{bn}")
                        P.dmaT("sync", w1tu[bn][:, ft * 512:(ft + 1) * 512],
                               w1q_scr[FF + n * 512:FF + (n + 1) * 512,
                                       ft * 128:(ft + 1) * 128], f"{tp}w{bn}")
                    return P.val(f"{tp}w{bn}")

                def emit_hidq(m):
                    P.op("vector", lambda e, m=m: e.tensor_scalar(
                        t1[:], sinv[m][:], w1m[:, 0:1], None, ALU.mult))
                    P.op("vector", lambda e, m=m: e.tensor_tensor(
                        m2q[m][:], t1[:], t1[:], ALU.mult))
                    P.op("vector", lambda e, m=m: e.tensor_tensor(
                        m2q[m][:], m2q[m][:], t1[:], ALU.mult))
                    P.op("vector", lambda e, m=m: e.tensor_tensor(
                        t1[:], amx[m][:], m2q[m][:], ALU.mult))
                    P.op("vector", lambda e: e.tensor_scalar(
                        t1[:], t1[:], 1e-5, None, ALU.max))
                    P.op("vector", lambda e, m=m: e.tensor_scalar(
                        fin[m][:], t1[:], 1.0 / 127.0, None, ALU.mult))
                    P.op("vector", lambda e, m=m: e.tensor_scalar(
                        fin[m][:], fin[m][:], w2m[:, 0:1], 0.5, ALU.mult, ALU.mult))
                    P.op("vector", lambda e: e.reciprocal(t1[:], t1[:]))
                    P.op("vector", lambda e: e.tensor_scalar(
                        t1[:], t1[:], 127.0, None, ALU.mult))
                    P.op("vector", lambda e, m=m: e.tensor_tensor(
                        m2q[m][:], m2q[m][:], t1[:], ALU.mult))
                    for hh in range(2):
                        c0 = hh * (FF // 2)
                        jh, hbuf = ht.next(P, "sync")
                        P.wait("sync", f"{st}h0")
                        P.wait("sync", f"{st}h1")
                        lv = P.dma("sync", hbuf[:],
                                   hid_scr[m * 128:(m + 1) * 128, c0:c0 + FF // 2],
                                   ld + "h")
                        P.wait("vector", ld + "h", lv)
                        P.op("vector", lambda e, hbuf=hbuf, m=m: e.tensor_scalar(
                            hbuf[:], hbuf[:], m2q[m][:, 0:1], 384.5, ALU.mult, ALU.add))
                        hv = P.op("vector", lambda e, hbuf=hbuf: e.tensor_scalar(
                            hbuf[:].bitcast(I32), hbuf[:].bitcast(I32), maski[:, 0:1],
                            None, ALU.bitwise_and), sem=vs)
                        jb, bbuf = hb.next(P, "scalar")
                        P.wait("scalar", vs, hv)
                        av = P.op("scalar", lambda e, hbuf=hbuf, bbuf=bbuf: e.activation(
                            bbuf[:], hbuf[:], AF.Identity, bias=cm384[:, 0:1]), sem=ss)
                        ht.gate(jh, ss, av)
                        sv = P.dma("scalar",
                                   hq_scr[m * 128:(m + 1) * 128, c0:c0 + FF // 2],
                                   bbuf[:], f"{st}q{jb}")
                        hb.gate(jb, f"{st}q{jb}", sv)
                        hq_sv[m] = sv

                tp_chunk = [0] * nch
                tp_chunk[0] = prep(0)
                if nch > 1:
                    tp_chunk[1] = prep(1)
                for n in range(nch):
                    bn = n % 2
                    if n + 2 < nch:
                        tp_chunk[n + 2] = prep(n + 2)
                    P.wait("tensor", f"{tp}w{bn}", tp_chunk[n])
                    for m in range(NMT):
                        bm = mm_iter % 2
                        if mm_iter >= 2:
                            P.wait("tensor", pf, pf_base + mm_iter - 1)
                        for kt in range(16):
                            P.op("tensor", lambda e, bn=bn, bm=bm, m=m, kt=kt: e.matmul(
                                pg[bm][:],
                                xiT[:, kt * TOK + m * 128: kt * TOK + m * 128 + 128],
                                w1tg[bn][:, kt * 512:(kt + 1) * 512],
                                start=(kt == 0), stop=(kt == 15)))
                        tvv = 0
                        for kt in range(16):
                            tvv = P.op("tensor", lambda e, bn=bn, bm=bm, m=m, kt=kt:
                                       e.matmul(
                                           pu[bm][:],
                                           xiT[:, kt * TOK + m * 128:
                                               kt * TOK + m * 128 + 128],
                                           w1tu[bn][:, kt * 512:(kt + 1) * 512],
                                           start=(kt == 0), stop=(kt == 15)),
                                       sem=(ts_ if kt == 15 else None))
                        if mm_iter >= 2:
                            P.wait("scalar", vs + "r", vsr_base + mm_iter - 1)
                        P.wait("scalar", ts_, tvv)
                        P.op("scalar", lambda e, bm=bm: e.activation(
                            rel[bm][:], pg[bm][:], AF.Relu), sem=ss)
                        P.wait("vector", ss)
                        P.op("vector", lambda e, bm=bm: e.tensor_tensor(
                            rel[bm][:], rel[bm][:], rel[bm][:], ALU.mult))
                        jh, hbuf = hid.next(P, "vector")
                        P.op("vector", lambda e, bm=bm, hbuf=hbuf: e.tensor_tensor(
                            hbuf[:], rel[bm][:], pu[bm][:], ALU.mult), sem=pf)
                        P.op("vector", lambda e, hbuf=hbuf: e.reduce_max(
                            red1[:], hbuf[:], axis=AX.X, apply_absolute_value=True))
                        hv = P.op("vector", lambda e, m=m: e.tensor_tensor(
                            amx[m][:], amx[m][:], red1[:], ALU.max), sem=vs + "r")
                        P.wait("scalar", vs + "r", hv)
                        sv = P.dma("scalar",
                                   hid_scr[m * 128:(m + 1) * 128,
                                           n * 512:(n + 1) * 512], hbuf[:],
                                   f"{st}h{jh}")
                        hid.gate(jh, f"{st}h{jh}", sv)
                        hid_sv[m] = sv
                        if n == nch - 1:
                            emit_hidq(m)
                        mm_iter += 1
                P.drain("sync", f"{st}h0", f"{st}h1", f"{st}q0", f"{st}q1")
                if app == 0:
                    P.drain("sync", f"{st}w0", f"{st}w1")
                P.barrier()

            with contextlib.ExitStack() as ar:
                NKT = FF // 128
                w2t = sb(f"{tag}w2t", [128, NKT * 512], BF16, st=ar)
                hqT = [sb(f"{tag}hqT{i}", [128, NKT * 512], BF16, st=ar) for i in range(2)]
                xres = Rot([sb(f"{tag}xr{i}", [128, 512], st=ar) for i in range(2)])
                stg = Rot([sb(f"{tag}sg{i}", [128, 512], st=ar) for i in range(2)])
                pd = [pst(f"{tag}pd{i}", [128, 512], ar) for i in range(4)]
                hq_gate = [None, None]
                gidx = 0
                pfd_base = P.val(pf)
                tsd_base = P.val(ts_)
                for mg in range(4):
                    bg = mg % 2
                    if hq_gate[bg] is not None:
                        P.wait("sync", hq_gate[bg][0], hq_gate[bg][1])
                    for ft in range(NKT):
                        P.dmaT("sync",
                               hqT[bg][:, ft * 512:(ft + 1) * 512],
                               hq_scr[mg * 512:(mg + 1) * 512, ft * 128:(ft + 1) * 128],
                               f"{tp}h{bg}")
                    hq_tpv = P.val(f"{tp}h{bg}")
                    for qi in range(4):
                        if gidx > 0:
                            P.wait("sync", ts_, tsd_base + gidx)
                        for ft in range(NKT):
                            P.dmaT("sync",
                                   w2t[:, ft * 512:(ft + 1) * 512],
                                   w2q_scr[qi * 512:(qi + 1) * 512,
                                           ft * 128:(ft + 1) * 128], tp + "2")
                        P.wait("tensor", f"{tp}h{bg}", hq_tpv)
                        P.wait("tensor", tp + "2")
                        if gidx > 0:
                            P.wait("tensor", pf, pfd_base + gidx * 4)
                        tvv = 0
                        for ft in range(NKT):
                            for mi in range(4):
                                last = (ft == NKT - 1 and mi == 3)
                                tvv = P.op("tensor",
                                           lambda e, bg=bg, ft=ft, mi=mi: e.matmul(
                                               pd[mi][:],
                                               hqT[bg][:, ft * 512 + mi * 128:
                                                        ft * 512 + mi * 128 + 128],
                                               w2t[:, ft * 512:(ft + 1) * 512],
                                               start=(ft == 0), stop=(ft == NKT - 1)),
                                           sem=(ts_ if last else None))
                        for mi in range(4):
                            m = mg * 4 + mi
                            jx, xbuf = xres.next(P, "sync")
                            lv = P.dma("sync", xbuf[:],
                                       src[m * 128:(m + 1) * 128,
                                           qi * 512:(qi + 1) * 512], f"{ld}x{jx}")
                            js, sbuf_ = stg.next(P, "vector")
                            P.wait("vector", ts_, tvv)
                            P.wait("vector", f"{ld}x{jx}", lv)
                            xv = P.op("vector",
                                      lambda e, mi=mi, m=m, xbuf=xbuf, sbuf_=sbuf_:
                                      e.scalar_tensor_tensor(
                                          sbuf_[:], pd[mi][:], fin[m][:, 0:1], xbuf[:],
                                          ALU.mult, ALU.add), sem=pf)
                            xres.gate(jx, pf, P.val(pf))
                            P.wait("scalar", pf, P.val(pf))
                            sv = P.dma("scalar",
                                       dst[m * 128:(m + 1) * 128, qi * 512:(qi + 1) * 512],
                                       sbuf_[:], f"{st}d{js}")
                            stg.gate(js, f"{st}d{js}", sv)
                        gidx += 1
                    hq_gate[bg] = (pf, pfd_base + gidx * 4)
                P.drain("sync", f"{st}d0", f"{st}d1")
                P.barrier()

        def phase2():
            with contextlib.ExitStack() as outer:
                yT = sb("p2yT", [128, NMT * TOK], st=outer)
                with contextlib.ExitStack() as ar:
                    xt = Rot([sb(f"p2xt{i}", [128, DM], st=ar) for i in range(2)])
                    yt = [sb(f"p2yt{i}", [128, DM], st=ar) for i in range(2)]
                    sqt = [sb(f"p2sq{i}", [128, DM], st=ar) for i in range(2)]
                    tmp = [sb(f"p2tmp{i}", [128, 1], st=ar) for i in range(4)]
                    tpp = [pst(f"p2tp{i}", [128, 512], ar) for i in range(2)]
                    yT_r = yT[:].rearrange("p (k t) -> p k t", k=NMT)
                    for m in range(NMT):
                        j, xbuf = xt.next(P, "sync")
                        lv = P.dma("sync", xbuf[:], x1_scr[m * 128:(m + 1) * 128, :],
                                   f"p2ld{j}")
                        b = m % 2
                        P.wait("vector", f"p2ld{j}", lv)
                        if m >= 2:
                            P.wait("vector", "p2tv", (m - 1) * 4)
                        emit_ln(xbuf[:], yt[b][:], sqt[b][:], tmp, "p2vs", "p2ss")
                        yv = P.op("vector", lambda e: e.nop(), sem="p2y")
                        xt.gate(j, "p2y", yv)
                        P.wait("tensor", "p2y", yv)
                        for g in range(4):
                            gi = m * 4 + g
                            bg = gi % 2
                            if gi >= 2:
                                P.wait("tensor", "p2tv", gi - 1)
                            for t4 in range(4):
                                ft = g * 4 + t4
                                P.op("tensor", lambda e, b=b, bg=bg, ft=ft, t4=t4: e.transpose(
                                    tpp[bg][:, t4 * 128:(t4 + 1) * 128],
                                    yt[b][:, ft * 128:(ft + 1) * 128], ident[:]),
                                    sem=("p2tt" if t4 == 3 else None))
                            P.wait("vector", "p2tt")
                            P.op("vector", lambda e, bg=bg, g=g, m=m: e.tensor_copy(
                                yT_r[:, g * 4:(g + 1) * 4, m * 128:(m + 1) * 128],
                                tpp[bg][:].rearrange("p (f t) -> p f t", f=4)), sem="p2tv")
                    P.barrier()
                with contextlib.ExitStack() as ar:
                    wf = Rot([sb(f"p2wf{i}", [128, DM], st=ar) for i in range(2)])
                    Wt = sb("p2Wt", [128, 16 * 512], st=ar)
                    stg = Rot([sb(f"p2sg{i}", [128, 512], st=ar) for i in range(2)])
                    st2 = Rot([sb(f"p2s2{i}", [128, 512], st=ar) for i in range(2)])
                    tpp = [pst(f"p2tq{i}", [128, 512], ar) for i in range(2)]
                    mmp = [pst(f"p2mm{i}", [128, 512], ar) for i in range(2)]
                    Wt_r = Wt[:].rearrange("p (k r) -> p k r", k=16)
                    qT_r = qT_scr.rearrange("(b p) t -> p b t", p=128)
                    kT_r = kT_scr.rearrange("(b p) t -> p b t", p=128)
                    tpi = 0
                    mmi = 0
                    for c in range(16):
                        if c > 0:
                            P.wait("vector", "p2mt", mmi)
                        wv = 0
                        for s4 in range(4):
                            j, wbuf = wf.next(P, "sync")
                            lv = P.dma("sync", wbuf[:],
                                       w_attn[c * 512 + s4 * 128:
                                              c * 512 + (s4 + 1) * 128, :], f"p2lw{j}")
                            P.wait("tensor", f"p2lw{j}", lv)
                            for g in range(4):
                                bg = tpi % 2
                                if tpi >= 2:
                                    P.wait("tensor", "p2wv", tpi - 1)
                                for t4 in range(4):
                                    kt = g * 4 + t4
                                    P.op("tensor",
                                         lambda e, wbuf=wbuf, bg=bg, kt=kt, t4=t4:
                                         e.transpose(tpp[bg][:, t4 * 128:(t4 + 1) * 128],
                                                     wbuf[:, kt * 128:(kt + 1) * 128],
                                                     ident[:]),
                                         sem=("p2wt" if t4 == 3 else None))
                                P.wait("vector", "p2wt")
                                wv = P.op("vector", lambda e, bg=bg, g=g, s4=s4: e.tensor_copy(
                                    Wt_r[:, g * 4:(g + 1) * 4, s4 * 128:(s4 + 1) * 128],
                                    tpp[bg][:].rearrange("p (f t) -> p f t", f=4)),
                                    sem="p2wv")
                                tpi += 1
                            wf.gate(j, "p2wv", wv)
                        P.wait("tensor", "p2wv", wv)
                        for m in range(NMT):
                            bm = mmi % 2
                            if mmi >= 2:
                                P.wait("tensor", "p2pf", mmi - 1)
                            tvv = 0
                            for kt in range(16):
                                tvv = P.op("tensor", lambda e, bm=bm, m=m, kt=kt: e.matmul(
                                    mmp[bm][:],
                                    yT[:, kt * TOK + m * 128: kt * TOK + m * 128 + 128],
                                    Wt[:, kt * 512:(kt + 1) * 512],
                                    start=(kt == 0), stop=(kt == 15)),
                                    sem=("p2mt" if kt == 15 else None))
                            mmi += 1
                            js, sbuf_ = stg.next(P, "scalar")
                            P.wait("scalar", "p2mt", tvv)
                            func = AF.Silu if c < 12 else AF.Sigmoid
                            fv = P.op("scalar", lambda e, bm=bm, sbuf_=sbuf_, func=func:
                                      e.activation(sbuf_[:], mmp[bm][:], func), sem="p2pf")
                            if c < 12:
                                dstap = qkv_scr[m * 128:(m + 1) * 128, c * 512:(c + 1) * 512]
                            else:
                                dstap = gate_scr[m * 128:(m + 1) * 128,
                                                 (c - 12) * 512:(c - 11) * 512]
                            sv = P.dma("scalar", dstap, sbuf_[:], f"p2st{js}")
                            if c < 8:
                                P.wait("tensor", "p2pf", fv)
                                bg = tpi % 2
                                if tpi >= 2:
                                    P.wait("tensor", "p2wv", tpi - 1)
                                for t4 in range(4):
                                    P.op("tensor", lambda e, sbuf_=sbuf_, bg=bg, t4=t4:
                                         e.transpose(tpp[bg][:, t4 * 128:(t4 + 1) * 128],
                                                     sbuf_[:, t4 * 128:(t4 + 1) * 128],
                                                     ident[:]),
                                         sem=("p2wt" if t4 == 3 else None))
                                qtv = P.val("p2wt")
                                j2, s2buf = st2.next(P, "vector")
                                P.wait("vector", "p2wt", qtv)
                                wv2 = P.op("vector", lambda e, bg=bg, s2buf=s2buf:
                                           e.tensor_copy(s2buf[:], tpp[bg][:]), sem="p2wv")
                                tpi += 1
                                P.wait("sync", "p2wv", wv2)
                                scr_r = qT_r if c < 4 else kT_r
                                cc = c if c < 4 else c - 4
                                sv2 = P.dma("sync",
                                            scr_r[:, cc * 4:(cc + 1) * 4,
                                                  m * 128:(m + 1) * 128],
                                            s2buf[:].rearrange("p (f t) -> p f t", f=4),
                                            f"p2s2{j2}")
                                st2.gate(j2, f"p2s2{j2}", sv2)
                                stg.gate(js, f"p2st{js}", sv)
                            else:
                                stg.gate(js, f"p2st{js}", sv)
                    P.drain("sync", "p2st0", "p2st1", "p2s20", "p2s21")
                    P.barrier()

        def phase3a():
            kv_r = lambda i, c0, c1: qkv_scr[i * 256:(i + 1) * 256, c0:c1].rearrange(
                "(j p) c -> p j c", p=128)
            d3 = lambda ap, n: ap.rearrange("p (j c) -> p j c", j=n)
            with contextlib.ExitStack() as ar:
                kb = Rot([sb(f"p3ak{i}", [128, 2 * DM], st=ar) for i in range(2)])
                vb = Rot([sb(f"p3av{i}", [128, 2 * DM], st=ar) for i in range(2)])
                kk = Rot([sb(f"p3akk{i}", [128, 64], st=ar) for i in range(2)])
                Sloc = sb("p3aS", [64, H * D], st=ar)
                gs = Rot([sb(f"p3gs{i}", [64, H * D], st=ar) for i in range(2)])
                tpb = Rot([sb(f"p3tp{i}", [64, H * D], st=ar) for i in range(2)])
                cps = [pst(f"p3acp{i}", [64, 64], ar) for i in range(2)]
                P.op("vector", lambda e: e.memset(Sloc[:], 0.0))
                n_sv = 0
                for i in range(NBLK):
                    jk, kbuf = kb.next(P, "sync")
                    kv1 = P.dma("sync", d3(kbuf[:], 2), kv_r(i, DM, 2 * DM),
                                f"p3ak{jk}")
                    jv, vbuf = vb.next(P, "sync")
                    kv2 = P.dma("sync", d3(vbuf[:], 2), kv_r(i, 2 * DM, 3 * DM),
                                f"p3av{jv}")
                    P.wait("vector", f"p3ak{jk}", kv1)
                    P.wait("vector", f"p3av{jv}", kv2)
                    P.wait("tensor", f"p3av{jv}", kv2)
                    for hh in range(H):
                        bp = hh % 2
                        if n_sv >= 2:
                            P.wait("tensor", "p3sv", n_sv - 1)
                        tv = 0
                        for jh in range(2):
                            jkk, kkb = kk.next(P, "vector")
                            kv_ = P.op("vector",
                                       lambda e, kbuf=kbuf, hh=hh, jh=jh, kkb=kkb:
                                       e.tensor_scalar(
                                           kkb[:],
                                           kbuf[:, jh * DM + hh * 64: jh * DM + hh * 64 + 64],
                                           kdc[:, hh * 2 + jh: hh * 2 + jh + 1],
                                           None, ALU.mult), sem="p3kv")
                            P.wait("tensor", "p3kv", kv_)
                            tv = P.op("tensor",
                                      lambda e, bp=bp, kkb=kkb, vbuf=vbuf, hh=hh, jh=jh:
                                      e.matmul(cps[bp][:], kkb[:],
                                               vbuf[:, jh * DM + hh * 64:
                                                    jh * DM + hh * 64 + 64],
                                               start=(jh == 0), stop=(jh == 1)),
                                      sem=("p3tc" if jh == 1 else None))
                            kk.gate(jkk, "p3tc", P.val("p3tc") if jh == 1 else
                                    P.val("p3tc") + 1)
                        P.wait("vector", "p3tc", tv)
                        sv_ = P.op("vector", lambda e, bp=bp, hh=hh: e.scalar_tensor_tensor(
                            Sloc[:, hh * 64:(hh + 1) * 64], Sloc[:, hh * 64:(hh + 1) * 64],
                            float(_BD[hh]), cps[bp][:], ALU.mult, ALU.add), sem="p3sv")
                        n_sv += 1
                    kb.gate(jk, "p3kv", P.val("p3kv"))
                    vb.gate(jv, "p3tc", P.val("p3tc"))
                P.wait("scalar", "p3sv")
                sv = P.dma("scalar", cc_sin, Sloc[:], "p3st")
                P.wait("gpsimd", "p3st", sv)
                P.op("gpsimd", lambda e: e.collective_compute(
                    "AllGather", ALU.bypass, replica_groups=[list(range(n_rg))],
                    ins=[cc_sin.opt()],
                    outs=[(cc_sout[0:64 * n_rg, :] if n_rg != NCORES else cc_sout).opt()]),
                    sem="p3cc")
                P.op("vector", lambda e: e.memset(Sin_st[:], 0.0))
                P.wait("sync", "p3cc")
                for cp in range(n_rg):
                    jg, gbuf = gs.next(P, "sync")
                    gv1 = P.dma("sync", gbuf[:], cc_sout[cp * 64:(cp + 1) * 64, :],
                                f"p3ag{jg}")
                    jt, tbuf = tpb.next(P, "sync")
                    gv2 = P.dma("sync", tbuf[:], tpred_in[cp], f"p3at{jt}")
                    P.wait("vector", f"p3ag{jg}", gv1)
                    P.wait("vector", f"p3at{jt}", gv2)
                    P.op("vector", lambda e, gbuf=gbuf, tbuf=tbuf: e.tensor_tensor(
                        gbuf[:], gbuf[:], tbuf[:], ALU.mult))
                    gv = P.op("vector", lambda e, gbuf=gbuf: e.tensor_tensor(
                        Sin_st[:], Sin_st[:], gbuf[:], ALU.add), sem="p3gv")
                    gs.gate(jg, "p3gv", gv)
                    tpb.gate(jt, "p3gv", gv)
                P.barrier()

        def phase3b():
            kv_r = lambda i, c0, c1: qkv_scr[i * 256:(i + 1) * 256, c0:c1].rearrange(
                "(j p) c -> p j c", p=128)
            qT_sl = lambda i: qT_scr[:, i * 256:(i + 1) * 256].rearrange(
                "(t p) c -> p t c", p=128)
            kT_sl = lambda i: kT_scr[:, i * 256:(i + 1) * 256].rearrange(
                "(t p) c -> p t c", p=128)
            d3 = lambda ap, n: ap.rearrange("p (j c) -> p j c", j=n)
            with contextlib.ExitStack() as ar:
                dd0 = sb("p3dd0", [128, H * 256], st=ar)
                qdb = sb("p3qdb", [128, (H // 2) * 256], st=ar)
                kb = Rot([sb("p3k0", [128, 2 * DM], st=ar)])
                vb = Rot([sb("p3v0", [128, 2 * DM], st=ar)])
                qs = Rot([sb("p3q0", [128, NMT * 256], st=ar)])
                ks = Rot([sb("p3kT0", [128, NMT * 256], st=ar)])
                kk = Rot([sb(f"p3kk{i}", [128, 64], st=ar) for i in range(2)])
                qq = Rot([sb(f"p3qq{i}", [64, 256], st=ar) for i in range(2)])
                aw0 = Rot([sb(f"p3a0{i}", [128, 256], st=ar) for i in range(2)])
                aw1 = Rot([sb(f"p3a1{i}", [128, 128], st=ar) for i in range(2)])
                asm = Rot([sb(f"p3asm{i}", [128, 2 * DM], st=ar) for i in range(2)])
                Scur = [sb(f"p3S{i}", [64, H * D], st=ar) for i in range(2)]
                pa0 = [pst(f"p3pa0{i}", [128, 256], ar) for i in range(2)]
                pa1 = [pst(f"p3pa1{i}", [128, 128], ar) for i in range(2)]
                pi0 = pst("p3pi0", [128, 64], ar)
                pi1 = pst("p3pi1", [128, 64], ar)
                cps = [pst(f"p3cp{i}", [64, 64], ar) for i in range(2)]
                P.dma("sync", dd0[:], dd0_in, "p3c")
                P.dma("sync", qdb[:], qdb_in, "p3c")
                P.wait("vector", "p3c")
                P.wait("vector", "p3gv", P.val("p3gv"))
                P.op("vector", lambda e: e.tensor_copy(Scur[0][:], Sin_st[:]), sem="p3sv2")
                awi = 0
                n_sv = P.val("p3sv2")
                n_ie = 0
                sv_prev_block = P.val("p3sv2")
                for i in range(NBLK):
                    Sa, Sb = Scur[i % 2], Scur[(i + 1) % 2]
                    jk, kbuf = kb.next(P, "sync")
                    b1 = P.dma("sync", d3(kbuf[:], 2), kv_r(i, DM, 2 * DM), "p3bk")
                    jv, vbuf = vb.next(P, "sync")
                    b2 = P.dma("sync", d3(vbuf[:], 2), kv_r(i, 2 * DM, 3 * DM), "p3bv")
                    jq, qsb = qs.next(P, "sync")
                    b3 = P.dma("sync", d3(qsb[:], NMT), qT_sl(i), "p3bq")
                    jkt, ksb = ks.next(P, "sync")
                    b4 = P.dma("sync", d3(ksb[:], NMT), kT_sl(i), "p3bt")
                    for s_, v_ in (("p3bk", b1), ("p3bv", b2), ("p3bq", b3),
                                   ("p3bt", b4)):
                        P.wait("tensor", s_, v_)
                        P.wait("vector", s_, v_)
                    P.wait("tensor", "p3sv2", sv_prev_block)
                    ja, abuf = asm.next(P, "vector")
                    for hh in range(H):
                        p0 = 64 * (hh % 2)
                        hcol = (hh // 2) * 256
                        ba = awi % 2
                        if awi >= 2:
                            P.wait("tensor", "p3de", awi - 1)
                        P.op("tensor", lambda e, ba=ba, ksb=ksb, qsb=qsb, p0=p0, hcol=hcol:
                             e.matmul(pa0[ba][:], ksb[p0:p0 + 64, hcol:hcol + 128],
                                      qsb[p0:p0 + 64, hcol:hcol + 256],
                                      start=True, stop=True))
                        tv = P.op("tensor", lambda e, ba=ba, ksb=ksb, qsb=qsb, p0=p0, hcol=hcol:
                                  e.matmul(pa1[ba][:], ksb[p0:p0 + 64, hcol + 128:hcol + 256],
                                           qsb[p0:p0 + 64, hcol + 128:hcol + 256],
                                           start=True, stop=True), sem="p3aw")
                        ja0, a0b = aw0.next(P, "vector")
                        ja1, a1b = aw1.next(P, "vector")
                        P.wait("vector", "p3aw", tv)
                        P.op("vector", lambda e, ba=ba, a0b=a0b, hh=hh: e.tensor_tensor(
                            a0b[:], pa0[ba][:], dd0[:, hh * 256:(hh + 1) * 256], ALU.mult))
                        av = P.op("vector", lambda e, ba=ba, a1b=a1b, hh=hh: e.tensor_tensor(
                            a1b[:], pa1[ba][:], dd0[:, hh * 256:hh * 256 + 128], ALU.mult),
                            sem="p3de")
                        jqq, qqb = qq.next(P, "vector")
                        qv = P.op("vector",
                                  lambda e, qqb=qqb, qsb=qsb, qdb=qdb, p0=p0, hcol=hcol:
                                  e.tensor_tensor(qqb[:], qsb[p0:p0 + 64, hcol:hcol + 256],
                                                  qdb[p0:p0 + 64, hcol:hcol + 256],
                                                  ALU.mult), sem="p3qe")
                        P.wait("tensor", "p3de", av)
                        P.wait("tensor", "p3qe", qv)
                        if n_ie >= 1:
                            P.wait("tensor", "p3ie", n_ie)
                        P.op("tensor", lambda e, a0b=a0b, vbuf=vbuf, hh=hh: e.matmul(
                            pi0[:], a0b[:, 0:128], vbuf[:, hh * 64: hh * 64 + 64],
                            start=True, stop=False))
                        P.op("tensor", lambda e, qqb=qqb, Sa=Sa, hh=hh: e.matmul(
                            pi0[:], qqb[:, 0:128], Sa[:, hh * 64:(hh + 1) * 64],
                            start=False, stop=True), sem="p3it")
                        P.op("tensor", lambda e, a0b=a0b, vbuf=vbuf, hh=hh: e.matmul(
                            pi1[:], a0b[:, 128:256], vbuf[:, hh * 64: hh * 64 + 64],
                            start=True, stop=False))
                        P.op("tensor", lambda e, a1b=a1b, vbuf=vbuf, hh=hh: e.matmul(
                            pi1[:], a1b[:], vbuf[:, DM + hh * 64: DM + hh * 64 + 64],
                            start=False, stop=False))
                        itv = P.op("tensor", lambda e, qqb=qqb, Sa=Sa, hh=hh: e.matmul(
                            pi1[:], qqb[:, 128:256], Sa[:, hh * 64:(hh + 1) * 64],
                            start=False, stop=True), sem="p3it")
                        aw0.gate(ja0, "p3it", itv)
                        aw1.gate(ja1, "p3it", itv)
                        qq.gate(jqq, "p3it", itv)
                        bp = hh % 2
                        if n_sv >= 2:
                            P.wait("tensor", "p3sv2", n_sv - 1)
                        ctv = 0
                        for jh in range(2):
                            jkk, kkb = kk.next(P, "vector")
                            kv_ = P.op("vector",
                                       lambda e, kbuf=kbuf, hh=hh, jh=jh, kkb=kkb:
                                       e.tensor_scalar(
                                           kkb[:],
                                           kbuf[:, jh * DM + hh * 64: jh * DM + hh * 64 + 64],
                                           kdc[:, hh * 2 + jh: hh * 2 + jh + 1],
                                           None, ALU.mult), sem="p3kv2")
                            P.wait("tensor", "p3kv2", kv_)
                            ctv = P.op("tensor",
                                       lambda e, bp=bp, kkb=kkb, vbuf=vbuf, hh=hh, jh=jh:
                                       e.matmul(cps[bp][:], kkb[:],
                                                vbuf[:, jh * DM + hh * 64:
                                                     jh * DM + hh * 64 + 64],
                                                start=(jh == 0), stop=(jh == 1)),
                                       sem=("p3tc2" if jh == 1 else None))
                            kk.gate(jkk, "p3tc2", P.val("p3tc2") if jh == 1 else
                                    P.val("p3tc2") + 1)
                        P.wait("vector", "p3it", itv)
                        P.op("vector", lambda e, abuf=abuf, hh=hh: e.tensor_copy(
                            abuf[:, hh * 64:(hh + 1) * 64], pi0[:]))
                        P.op("vector", lambda e, abuf=abuf, hh=hh: e.tensor_copy(
                            abuf[:, DM + hh * 64: DM + hh * 64 + 64], pi1[:]), sem="p3ie")
                        n_ie += 1
                        P.wait("vector", "p3tc2", ctv)
                        P.op("vector", lambda e, Sa=Sa, Sb=Sb, bp=bp, hh=hh:
                             e.scalar_tensor_tensor(
                                 Sb[:, hh * 64:(hh + 1) * 64], Sa[:, hh * 64:(hh + 1) * 64],
                                 float(_BD[hh]), cps[bp][:], ALU.mult, ALU.add), sem="p3sv2")
                        n_sv += 1
                        awi += 1
                    sv_prev_block = P.val("p3sv2")
                    kb.gate(jk, "p3kv2", P.val("p3kv2"))
                    vb.gate(jv, "p3tc2", P.val("p3tc2"))
                    qs.gate(jq, "p3qe", P.val("p3qe"))
                    ks.gate(jkt, "p3aw", P.val("p3aw"))
                    P.wait("scalar", "p3ie", P.val("p3ie"))
                    P.dma("scalar", attn_scr[i * 256:i * 256 + 128, :],
                          abuf[:, 0:DM], f"p3s{ja}")
                    sv2 = P.dma("scalar", attn_scr[i * 256 + 128:(i + 1) * 256, :],
                                abuf[:, DM:2 * DM], f"p3s{ja}")
                    asm.gate(ja, f"p3s{ja}", sv2)
                P.drain("sync", "p3s0", "p3s1")
                P.barrier()

        def phase4():
            b_tv = P.val("p2tv")
            b_z = P.val("p2z")
            b_wv = P.val("p2wv")
            b_mt = P.val("p2mt")
            b_pf = P.val("p2pf")
            with contextlib.ExitStack() as outer:
                zT = sb("p4zT", [128, NMT * TOK], st=outer)
                with contextlib.ExitStack() as ar:
                    at = Rot([sb("p4at0", [128, DM], st=ar)])
                    gt = Rot([sb("p4gt0", [128, DM], st=ar)])
                    awb = sb("p4awb", [128, DM], st=ar)
                    P.dma("sync", awb[:], attnw_b, "p2lw0")
                    P.wait("vector", "p2lw0")
                    sqt = [sb(f"p4sq{i}", [128, DM], st=ar) for i in range(2)]
                    zt = [sb(f"p4zt{i}", [128, DM], st=ar) for i in range(2)]
                    tmp = [sb(f"p4tmp{i}", [128, 1], st=ar) for i in range(4)]
                    tpp = [pst(f"p4tp{i}", [128, 512], ar) for i in range(2)]
                    zT_r = zT[:].rearrange("p (k t) -> p k t", k=NMT)
                    for m in range(NMT):
                        ja, abuf = at.next(P, "sync")
                        a1 = P.dma("sync", abuf[:], attn_scr[m * 128:(m + 1) * 128, :],
                                   "p2ld0")
                        jg, gbuf = gt.next(P, "sync")
                        g1 = P.dma("sync", gbuf[:], gate_scr[m * 128:(m + 1) * 128, :],
                                   "p2ld1")
                        b = m % 2
                        P.wait("scalar", "p2ld0", a1)
                        P.wait("vector", "p2ld1", g1)
                        if m >= 2:
                            P.wait("scalar", "p2z", b_z + m - 1)
                            P.wait("vector", "p2tv", b_tv + (m - 1) * 4)
                        P.op("scalar", lambda e, b=b, abuf=abuf: e.activation(
                            sqt[b][:], abuf[:], AF.Square), sem="p2ss")
                        P.wait("vector", "p2ss")
                        P.op("vector", lambda e, b=b: e.reduce_sum(
                            tmp[0][:], sqt[b][:], axis=AX.X), sem="p2vs")
                        P.wait("scalar", "p2vs")
                        P.op("scalar", lambda e: e.activation(
                            tmp[2][:], tmp[0][:], AF.Sqrt,
                            scale=cinvd[:, 0:1], bias=ceps[:, 0:1]), sem="p2ss")
                        P.wait("vector", "p2ss")
                        P.op("vector", lambda e: e.reciprocal(tmp[3][:], tmp[2][:]))
                        P.op("vector", lambda e, b=b, abuf=abuf: e.tensor_scalar(
                            zt[b][:], abuf[:], tmp[3][:, 0:1], None, ALU.mult))
                        av = P.op("vector", lambda e, b=b: e.tensor_tensor(
                            zt[b][:], zt[b][:], awb[:], ALU.mult), sem="p2y")
                        at.gate(ja, "p2y", av)
                        gv = P.op("vector", lambda e, b=b, gbuf=gbuf: e.tensor_tensor(
                            zt[b][:], zt[b][:], gbuf[:], ALU.mult), sem="p2z")
                        gt.gate(jg, "p2z", gv)
                        P.wait("tensor", "p2z", gv)
                        for g in range(4):
                            gi = m * 4 + g
                            bg = gi % 2
                            if gi >= 2:
                                P.wait("tensor", "p2tv", b_tv + gi - 1)
                            for t4 in range(4):
                                ft = g * 4 + t4
                                P.op("tensor", lambda e, b=b, bg=bg, ft=ft, t4=t4:
                                     e.transpose(tpp[bg][:, t4 * 128:(t4 + 1) * 128],
                                                 zt[b][:, ft * 128:(ft + 1) * 128],
                                                 ident[:]),
                                     sem=("p2tt" if t4 == 3 else None))
                            P.wait("vector", "p2tt")
                            P.op("vector", lambda e, bg=bg, g=g, m=m: e.tensor_copy(
                                zT_r[:, g * 4:(g + 1) * 4, m * 128:(m + 1) * 128],
                                tpp[bg][:].rearrange("p (f t) -> p f t", f=4)), sem="p2tv")
                    P.barrier()
                with contextlib.ExitStack() as ar:
                    wf = Rot([sb(f"p4wf{i}", [128, DM], st=ar) for i in range(2)])
                    Wt = sb("p4Wt", [128, 16 * 256], st=ar)
                    stg = Rot([sb(f"p4sg{i}", [128, 256], st=ar) for i in range(2)])
                    xr = Rot([sb(f"p4xr{i}", [128, 256], st=ar) for i in range(2)])
                    tpp = [pst(f"p4tq{i}", [128, 512], ar) for i in range(2)]
                    mmp = [pst(f"p4mm{i}", [128, 256], ar) for i in range(2)]
                    Wt_r = Wt[:].rearrange("p (k r) -> p k r", k=16)
                    tpi = 0
                    mmi = 0
                    for c in range(8):
                        if c > 0:
                            P.wait("vector", "p2mt", b_mt + mmi)
                        wv = 0
                        for s4 in range(2):
                            j, wbuf = wf.next(P, "sync")
                            lv = P.dma("sync", wbuf[:],
                                       outw_in[c * 256 + s4 * 128:
                                               c * 256 + (s4 + 1) * 128, :], f"p2lw{j}")
                            P.wait("tensor", f"p2lw{j}", lv)
                            for g in range(4):
                                bg = tpi % 2
                                if tpi >= 2:
                                    P.wait("tensor", "p2wv", b_wv + tpi - 1)
                                for t4 in range(4):
                                    kt = g * 4 + t4
                                    P.op("tensor",
                                         lambda e, wbuf=wbuf, bg=bg, kt=kt, t4=t4:
                                         e.transpose(tpp[bg][:, t4 * 128:(t4 + 1) * 128],
                                                     wbuf[:, kt * 128:(kt + 1) * 128],
                                                     ident[:]),
                                         sem=("p2wt" if t4 == 3 else None))
                                P.wait("vector", "p2wt")
                                wv = P.op("vector", lambda e, bg=bg, g=g, s4=s4:
                                          e.tensor_copy(
                                              Wt_r[:, g * 4:(g + 1) * 4,
                                                   s4 * 128:(s4 + 1) * 128],
                                              tpp[bg][:].rearrange("p (f t) -> p f t", f=4)),
                                          sem="p2wv")
                                tpi += 1
                            wf.gate(j, "p2wv", wv)
                        P.wait("tensor", "p2wv", wv)
                        for m in range(NMT):
                            bm = mmi % 2
                            if mmi >= 2:
                                P.wait("tensor", "p2pf", b_pf + mmi - 1)
                            tvv = 0
                            for kt in range(16):
                                tvv = P.op("tensor", lambda e, bm=bm, m=m, kt=kt: e.matmul(
                                    mmp[bm][:],
                                    zT[:, kt * TOK + m * 128: kt * TOK + m * 128 + 128],
                                    Wt[:, kt * 256:(kt + 1) * 256],
                                    start=(kt == 0), stop=(kt == 15)),
                                    sem=("p2mt" if kt == 15 else None))
                            mmi += 1
                            jx, xbuf = xr.next(P, "sync")
                            lv = P.dma("sync", xbuf[:],
                                       x1_scr[m * 128:(m + 1) * 128,
                                              c * 256:(c + 1) * 256], f"p2ld{jx}")
                            js, sbuf_ = stg.next(P, "vector")
                            P.wait("vector", "p2mt", tvv)
                            P.wait("vector", f"p2ld{jx}", lv)
                            xv = P.op("vector", lambda e, bm=bm, xbuf=xbuf, sbuf_=sbuf_:
                                      e.tensor_tensor(sbuf_[:], mmp[bm][:], xbuf[:],
                                                      ALU.add), sem="p2pf")
                            xr.gate(jx, "p2pf", xv)
                            P.wait("scalar", "p2pf", xv)
                            sv = P.dma("scalar",
                                       x2_scr[m * 128:(m + 1) * 128,
                                              c * 256:(c + 1) * 256], sbuf_[:],
                                       f"p2st{js}")
                            stg.gate(js, f"p2st{js}", sv)
                    P.drain("sync", "p2st0", "p2st1")
                    P.barrier()

        phase0()
        if max_phase >= 1:
            phase_ffn(0, x_in, x1_scr)
        if max_phase >= 2:
            phase2()
        if max_phase >= 3:
            phase3a()
            phase3b()
        if max_phase >= 4:
            phase4()
        if max_phase >= 5:
            phase_ffn(1, x2_scr, y_out)
        else:
            for m in range(NMT):
                P.dma("gpsimd", y_out[m * 128:(m + 1) * 128, :],
                      x1_scr[m * 128:(m + 1) * 128, :], "fcp")
            P.wait("gpsimd", "fcp")
        for t, ap in tap_outs.items():
            if ap is None:
                continue
            P.dma("gpsimd", ap, tap_src[t], "dbg")
        if P.val("dbg") if "dbg" in P.cnt else 0:
            P.wait("gpsimd", "dbg")
        P.barrier()

        block = big.enter_context(nc.Block())
        P.finalize(block)
    return nc


def _stage_inputs(inputs):
    x = np.ascontiguousarray(np.asarray(inputs["L0_attn_hidden_states"],
                                        dtype=np.float32).reshape(B * S, DM))
    qkv_w = np.asarray(inputs["L0_attn_qkv_w"], np.float32)
    out_w = np.ascontiguousarray(np.asarray(inputs["L0_attn_out_w"], np.float32))
    gate_w = np.asarray(inputs["L0_attn_gate_w"], np.float32)
    norm_w = np.asarray(inputs["L0_attn_norm_w"], np.float32)
    w1 = np.ascontiguousarray(np.asarray(inputs["L0_ffn_w_gate_up"], np.float32))
    w2 = np.ascontiguousarray(np.asarray(inputs["L0_ffn_w_down"], np.float32))
    fn = np.asarray(inputs["L0_ffn_norm_weight"], np.float32)

    wat = np.ascontiguousarray(np.concatenate([qkv_w[_perm_qkv()], gate_w], axis=0))
    ffnw_b = np.ascontiguousarray(np.broadcast_to(fn[None, :], (128, DM)))
    attnw_b = np.ascontiguousarray(np.broadcast_to(norm_w[None, :], (128, DM)))
    ident = np.eye(128, dtype=np.float32)

    in_maps = []
    for c in range(NCORES):
        in_maps.append({
            "x_in": np.ascontiguousarray(x[c * TOK:(c + 1) * TOK]),
            "w_attn": wat,
            "outw_in": out_w,
            "w1_in": w1,
            "w2_in": w2,
            "w1_sl": np.ascontiguousarray(w1[c * (F2 // 8):(c + 1) * (F2 // 8)]),
            "w2_sl": np.ascontiguousarray(w2[c * (DM // 8):(c + 1) * (DM // 8)]),
            "ffnw_b": ffnw_b,
            "attnw_b": attnw_b,
            "ident_in": ident,
            "dd0_in": _DD0,
            "qdb_in": _QDB,
            "kdc_in": _KDC,
            "tpred_in": _t_pred(c),
        })
    return in_maps


def kernel(**inputs):
    nc = build_nc()
    in_maps = _stage_inputs(inputs)
    res = run_bass_kernel_spmd(nc, in_maps, core_ids=list(range(NCORES)))
    out = np.concatenate([np.asarray(res.results[c]["y_out"], dtype=np.float32)
                          for c in range(NCORES)], axis=0)
    return out.reshape(B, S, DM)


# version 07
# speedup vs baseline: 1.0037x; 1.0037x over previous
import contextlib
import numpy as np
import ml_dtypes

import concourse.bass as bass
import concourse.mybir as mybir
from concourse.bass_utils import run_bass_kernel_spmd

F32 = mybir.dt.float32
BF16 = mybir.dt.bfloat16
I32 = mybir.dt.int32
AF = mybir.ActivationFunctionType
ALU = mybir.AluOpType
AX = mybir.AxisListType

B, S, DM = 2, 8192, 2048
H, D, BS = 32, 64, 256
FF = 5632
F2 = 2 * FF
NCORES = 8
TOK = (B * S) // NCORES
NBLK = TOK // BS
NMT = TOK // 128
EPS = 1e-5
N1 = F2 * DM
N2 = DM * FF

ENGS = ("sync", "vector", "scalar", "tensor", "gpsimd")


def _decay():
    h = np.arange(H, dtype=np.float64)
    base = 1.0 / 2.0 ** (8.0 / H)
    slope = (base ** (h + 1.0)) * (1.0 - 0.0 / (32 - 1 + 1e-05) + 1e-05)
    r = np.arange(BS, dtype=np.float64) + 1.0
    qd = np.exp(-slope[:, None] * r[None, :]).astype(np.float32)
    kd = np.exp(-slope[:, None] * (BS - r[None, :])).astype(np.float32)
    bd = np.exp(-slope * BS).astype(np.float32)
    i_ = np.arange(128, dtype=np.float64)
    dif = i_[None, :] - i_[:, None]
    DD0 = np.zeros((128, H * 256), np.float32)
    QDB = np.zeros((128, (H // 2) * 256), np.float32)
    KDC = np.zeros((128, H * 2), np.float32)
    for hh in range(H):
        t0 = np.where(dif >= 0, np.exp(-slope[hh] * dif), 0.0)
        t1 = np.exp(-slope[hh] * (128.0 + dif))
        DD0[:, hh * 256:hh * 256 + 128] = t0.astype(np.float32)
        DD0[:, hh * 256 + 128:hh * 256 + 256] = t1.astype(np.float32)
        p0 = 64 * (hh % 2)
        QDB[p0:p0 + 64, (hh // 2) * 256:(hh // 2) * 256 + 256] = qd[hh][None, :]
        KDC[:, hh * 2] = kd[hh, 0:128]
        KDC[:, hh * 2 + 1] = kd[hh, 128:256]
    return qd, kd, bd, DD0, QDB, KDC


_QD, _KD, _BD, _DD0, _QDB, _KDC = _decay()


def _t_pred(core):
    out = np.zeros((8, 64, H * D), np.float32)
    g, j = core // 4, core % 4
    for cp in range(8):
        if cp // 4 != g or cp % 4 >= j:
            continue
        w = _BD.astype(np.float64) ** ((j - 1 - (cp % 4)) * NBLK)
        for hh in range(H):
            out[cp, :, hh * D:(hh + 1) * D] = np.float32(w[hh])
    return out


def _perm_qkv():
    perm = np.zeros(3 * H * D, np.int64)
    for hh in range(H):
        for d in range(D):
            perm[0 * H * D + hh * D + d] = 192 * hh + d
            perm[1 * H * D + hh * D + d] = 192 * hh + 64 + d
            perm[2 * H * D + hh * D + d] = 192 * hh + 128 + d
    return perm


_CHAIN = ("vector", "scalar", "gpsimd")


class Prog:
    def __init__(self, nc, stack):
        self.nc, self.stack = nc, stack
        self.q = {e: [] for e in ENGS}
        self.h, self.cnt = {}, {}
        self.last = {e: None for e in ENGS}
        self.waited = {e: {} for e in ENGS}

    def sem(self, name):
        if name not in self.h:
            self.h[name] = self.stack.enter_context(self.nc.semaphore(name))
            self.cnt[name] = 0
        return self.h[name]

    def emit(self, eng, fn):
        self.q[eng].append(fn)

    def _wait_raw(self, eng, sem, v):
        if v <= 0:
            return
        if self.waited[eng].get(sem, 0) >= v:
            return
        self.waited[eng][sem] = v
        hh = self.sem(sem)
        self.emit(eng, lambda e, hh=hh, v=v: e.wait_ge(hh, v))

    def _chain_wait(self, eng):
        if self.last[eng] is not None:
            s, v = self.last[eng]
            self._wait_raw(eng, s, v)

    def dma(self, eng, out, in_, sem):
        self._chain_wait(eng)
        hh = self.sem(sem)
        self.cnt[sem] += 16
        self.emit(eng, lambda e, out=out, in_=in_, hh=hh:
                  e.dma_start(out=out, in_=in_).then_inc(hh, 16))
        return self.cnt[sem]

    def dmaT(self, eng, out, in_, sem):
        self._chain_wait(eng)
        hh = self.sem(sem)
        self.cnt[sem] += 16
        self.emit(eng, lambda e, out=out, in_=in_, hh=hh:
                  e.dma_start_transpose(out=out, in_=in_).then_inc(hh, 16))
        return self.cnt[sem]

    def wait(self, eng, sem, val=None):
        self.sem(sem)
        v = self.cnt[sem] if val is None else val
        self._wait_raw(eng, sem, v)

    def op(self, eng, fn, sem=None, k=1):
        if eng in _CHAIN:
            self._chain_wait(eng)
            if sem is None:
                sem = "_ch_" + eng
                k = 1
        if sem is not None:
            hh = self.sem(sem)
            self.cnt[sem] += k
            self.emit(eng, lambda e, fn=fn, hh=hh, k=k: fn(e).then_inc(hh, k))
            if eng in _CHAIN or eng == "tensor":
                self.last[eng] = (sem, self.cnt[sem])
            return self.cnt[sem]
        self.emit(eng, fn)
        return None

    def val(self, sem):
        self.sem(sem)
        return self.cnt[sem]

    def drain(self, eng, *sems):
        for s in sems:
            self.wait(eng, s)

    def barrier(self):
        hh = self.sem("phase")
        for eng in ENGS:
            if eng in _CHAIN:
                self._chain_wait(eng)
            elif self.last.get(eng) is not None:
                s, v = self.last[eng]
                self._wait_raw(eng, s, v)
            self.emit(eng, lambda e, hh=hh: e.nop().then_inc(hh, 1))
        self.cnt["phase"] += len(ENGS)
        v = self.cnt["phase"]
        for eng in ENGS:
            self._wait_raw(eng, "phase", v)

    def finalize(self, block):
        for eng in ENGS:
            fns = self.q[eng]

            def body(e, fns=fns):
                for fn in fns:
                    fn(e)
            getattr(block, eng)(body)


class Rot:

    def __init__(self, bufs):
        self.bufs = bufs
        self.gates = [None] * len(bufs)
        self.i = -1

    def next(self, P, producer_eng):
        self.i += 1
        j = self.i % len(self.bufs)
        if self.gates[j] is not None:
            s, v = self.gates[j]
            P.wait(producer_eng, s, v)
        return j, self.bufs[j]

    def gate(self, j, sem, val):
        self.gates[j] = (sem, val)


def build_nc(max_phase=99, taps=(), n_rg=NCORES):
    nc = bass.Bass("TRN2", target_bir_lowering=False)

    x_in = nc.dram_tensor("x_in", [TOK, DM], F32, kind="ExternalInput").ap()
    w_attn = nc.dram_tensor("w_attn", [4 * DM, DM], F32, kind="ExternalInput").ap()
    outw_in = nc.dram_tensor("outw_in", [DM, DM], F32, kind="ExternalInput").ap()
    w1_in = nc.dram_tensor("w1_in", [F2, DM], F32, kind="ExternalInput").ap()
    w2_in = nc.dram_tensor("w2_in", [DM, FF], F32, kind="ExternalInput").ap()
    w1_sl = nc.dram_tensor("w1_sl", [F2 // 8, DM], F32, kind="ExternalInput").ap()
    w2_sl = nc.dram_tensor("w2_sl", [DM // 8, FF], F32, kind="ExternalInput").ap()
    ffnw_b = nc.dram_tensor("ffnw_b", [128, DM], F32, kind="ExternalInput").ap()
    attnw_b = nc.dram_tensor("attnw_b", [128, DM], F32, kind="ExternalInput").ap()
    ident_in = nc.dram_tensor("ident_in", [128, 128], F32, kind="ExternalInput").ap()
    dd0_in = nc.dram_tensor("dd0_in", [128, H * 256], F32, kind="ExternalInput").ap()
    qdb_in = nc.dram_tensor("qdb_in", [128, (H // 2) * 256], F32, kind="ExternalInput").ap()
    kdc_in = nc.dram_tensor("kdc_in", [128, H * 2], F32, kind="ExternalInput").ap()
    tpred_in = nc.dram_tensor("tpred_in", [8, 64, H * D], F32, kind="ExternalInput").ap()
    y_out = nc.dram_tensor("y_out", [TOK, DM], F32, kind="ExternalOutput").ap()

    x1_scr = nc.dram_tensor("x1_scr", [TOK, DM], F32).ap()
    x2_scr = nc.dram_tensor("x2_scr", [TOK, DM], F32).ap()
    xi_scr = nc.dram_tensor("xi_scr", [TOK, DM], BF16).ap()
    w1q_scr = nc.dram_tensor("w1q_scr", [F2, DM], BF16).ap()
    w2q_scr = nc.dram_tensor("w2q_scr", [DM, FF], BF16).ap()
    hid_scr = nc.dram_tensor("hid_scr", [TOK, FF], F32).ap()
    hq_scr = nc.dram_tensor("hq_scr", [TOK, FF], BF16).ap()
    qkv_scr = nc.dram_tensor("qkv_scr", [TOK, 3 * DM], F32).ap()
    gate_scr = nc.dram_tensor("gate_scr", [TOK, DM], F32).ap()
    qT_scr = nc.dram_tensor("qT_scr", [DM, TOK], F32).ap()
    kT_scr = nc.dram_tensor("kT_scr", [DM, TOK], F32).ap()
    attn_scr = nc.dram_tensor("attn_scr", [TOK, DM], F32).ap()
    cc_sin = nc.dram_tensor("cc_sin", [64, H * D], F32).ap()
    cc_sout = nc.dram_tensor("cc_sout", [8 * 64, H * D], F32).ap()
    cc_min = nc.dram_tensor("cc_min", [1, 2], F32).ap()
    cc_mout = nc.dram_tensor("cc_mout", [1, 2], F32).ap()

    tap_src = {"x1": x1_scr, "x2": x2_scr, "qkv": qkv_scr, "gate": gate_scr,
               "qT": qT_scr, "kT": kT_scr, "attn": attn_scr, "sgath": cc_sout,
               "hid": hid_scr, "xi": xi_scr, "hq": hq_scr}
    tap_dt = {"xi": BF16, "hq": BF16}
    tap_outs = {}
    _STRIP_TAP = None
    _STRIP_A = "strips_a" in taps
    taps = [t for t in taps if t != "strips_a"]
    if "strips" in taps or _STRIP_A:
        taps = [t for t in taps if t != "strips"]
        _STRIP_TAP = nc.dram_tensor("dbg_strips", [512, NMT], F32,
                                    kind="ExternalOutput").ap()
        tap_outs["strips"] = None
    for t in taps:
        src = tap_src[t]
        tap_outs[t] = nc.dram_tensor("dbg_" + t, list(src.tensor.shape),
                                     tap_dt.get(t, F32), kind="ExternalOutput").ap()

    with contextlib.ExitStack() as big:
        P = Prog(nc, big)

        def sb(name, shape, dt=F32, st=None):
            return (st or big).enter_context(nc.sbuf_tensor(name, shape, dt))

        def pst(name, shape, st, dt=F32):
            return st.enter_context(nc.psum_tensor(name, shape, dt))

        ident = sb("ident", [128, 128])
        c384 = sb("c384", [128, 1])
        cm384 = sb("cm384", [128, 1])
        ceps = sb("ceps", [128, 1])
        cinvd = sb("cinvd", [128, 1])
        maski = sb("maski", [128, 1], I32)
        ones_c = sb("ones_c", [128, 1])
        ones_r = sb("ones_r", [1, 128])
        sw1s = sb("sw1s", [128, 1])
        nsw1s = sb("nsw1s", [128, 1])
        w1m = sb("w1m", [128, 1])
        sw2s = sb("sw2s", [128, 1])
        w2m = sb("w2m", [128, 1])
        kdc = sb("kdc", [128, H * 2])
        sinv_s = [[sb(f"sinv_s{a}_{m}", [128, 1]) for m in range(NMT)] for a in range(2)]
        amax_s = [[sb(f"amax_s{a}_{m}", [128, 1]) for m in range(NMT)] for a in range(2)]
        m2q_s = [[sb(f"m2q_s{a}_{m}", [128, 1]) for m in range(NMT)] for a in range(2)]
        fin_s = [[sb(f"fin_s{a}_{m}", [128, 1]) for m in range(NMT)] for a in range(2)]
        Sin_st = sb("Sin_st", [64, H * D])

        def phase0():
            for f in (lambda e: e.memset(c384[:], 384.5),
                      lambda e: e.memset(cm384[:], -384.0),
                      lambda e: e.memset(ceps[:], EPS),
                      lambda e: e.memset(cinvd[:], 1.0 / DM),
                      lambda e: e.memset(maski[:], -32768),
                      lambda e: e.memset(ones_c[:], 1.0),
                      lambda e: e.memset(ones_r[:], 1.0)):
                P.op("vector", f)
            P.dma("sync", ident[:], ident_in, "ld0")
            P.dma("sync", kdc[:], kdc_in, "ld0")
            with contextlib.ExitStack() as ar:
                wt = Rot([sb(f"p0w{i}", [128, FF], st=ar) for i in range(2)])
                acc = sb("p0acc", [128, 2], st=ar)
                acc1 = sb("p0acc1", [128, 1], st=ar)
                acc2 = sb("p0acc2", [128, 1], st=ar)
                red = sb("p0red", [128, 1], st=ar)
                sc2 = sb("p0sc", [1, 2], st=ar)
                pr = pst("p0ps", [1, 2], ar)
                pb = pst("p0pb", [128, 2], ar)
                P.op("vector", lambda e: e.memset(acc1[:], 0.0))
                P.op("vector", lambda e: e.memset(acc2[:], 0.0))
                for accb, tiles, srcap, ncols in ((acc1, (F2 // 8) // 128, w1_sl, DM),
                                                  (acc2, (DM // 8) // 128, w2_sl, FF)):
                    for i in range(tiles):
                        j, buf = wt.next(P, "sync")
                        lv = P.dma("sync", buf[:, 0:ncols],
                                   srcap[i * 128:(i + 1) * 128, :], f"ld1_{j}")
                        P.wait("vector", f"ld1_{j}", lv)
                        P.op("vector", lambda e, buf=buf, ncols=ncols: e.reduce_sum(
                            red[:], buf[:, 0:ncols], axis=AX.X, apply_absolute_value=True))
                        vv = P.op("vector", lambda e, accb=accb: e.tensor_tensor(
                            accb[:], accb[:], red[:], ALU.add), sem="v0")
                        wt.gate(j, "v0", vv)
                P.op("vector", lambda e: e.tensor_scalar(
                    acc[:, 0:1], acc1[:], 1.0, None, ALU.mult))
                P.op("vector", lambda e: e.tensor_scalar(
                    acc[:, 1:2], acc2[:], 1.0, None, ALU.mult), sem="v0")
                P.wait("tensor", "v0")
                P.op("tensor", lambda e: e.matmul(pr[:], ones_c[:], acc[:],
                                                  start=True, stop=True), sem="t0")
                P.wait("scalar", "t0")
                P.op("scalar", lambda e: e.copy(sc2[:], pr[:]), sem="s0")
                P.wait("scalar", "s0")
                P.dma("scalar", cc_min, sc2[:], "st0")
                P.wait("gpsimd", "st0")
                P.op("gpsimd", lambda e: e.collective_compute(
                    "AllReduce", ALU.add, replica_groups=[list(range(n_rg))],
                    ins=[cc_min.opt()], outs=[cc_mout.opt()]), sem="cc0")
                P.wait("sync", "cc0")
                P.dma("sync", sc2[:], cc_mout, "ld2")
                P.wait("vector", "ld2")
                P.op("vector", lambda e: e.tensor_scalar(
                    sc2[:, 0:1], sc2[:, 0:1], 1.0 / N1, 1e-5, ALU.mult, ALU.max))
                P.op("vector", lambda e: e.tensor_scalar(
                    sc2[:, 1:2], sc2[:, 1:2], 1.0 / N2, 1e-5, ALU.mult, ALU.max), sem="v0")
                P.wait("tensor", "v0")
                P.op("tensor", lambda e: e.matmul(pb[:], ones_r[:], sc2[:],
                                                  start=True, stop=True), sem="t0")
                P.wait("vector", "t0")
                P.op("vector", lambda e: e.tensor_copy(w1m[:], pb[:, 0:1]))
                P.op("vector", lambda e: e.tensor_copy(w2m[:], pb[:, 1:2]))
                P.op("vector", lambda e: e.reciprocal(sw1s[:], w1m[:]))
                P.op("vector", lambda e: e.reciprocal(sw2s[:], w2m[:]))
                P.op("vector", lambda e: e.tensor_scalar(
                    nsw1s[:], sw1s[:], -1.0, None, ALU.mult))
                P.wait("vector", "ld0")
                P.barrier()

        def emit_ln(xt_ap, yt_ap, sqt_ap, tmp, vs, ss):
            rs, mn, sd, iv = tmp
            P.op("vector", lambda e: e.reduce_sum(rs[:], xt_ap, axis=AX.X))
            P.op("vector", lambda e: e.tensor_scalar(
                mn[:], rs[:], 1.0 / DM, None, ALU.mult))
            P.op("vector", lambda e: e.tensor_scalar(
                yt_ap, xt_ap, mn[:, 0:1], None, ALU.subtract), sem=vs)
            P.wait("scalar", vs)
            P.op("scalar", lambda e: e.activation(sqt_ap, yt_ap, AF.Square), sem=ss)
            P.wait("vector", ss)
            P.op("vector", lambda e: e.reduce_sum(rs[:], sqt_ap, axis=AX.X), sem=vs)
            P.wait("scalar", vs)
            P.op("scalar", lambda e: e.activation(
                sd[:], rs[:], AF.Sqrt, scale=cinvd[:, 0:1], bias=ceps[:, 0:1]), sem=ss)
            P.wait("vector", ss)
            P.op("vector", lambda e: e.reciprocal(iv[:], sd[:]))
            P.op("vector", lambda e: e.tensor_scalar(
                yt_ap, yt_ap, iv[:, 0:1], None, ALU.mult), sem=vs)

        def phase_ffn(app, src, dst):
            tag = f"f{app}"
            ld, lw, tp, st, vs, ss, ts_, pf = [
                "f" + s_ for s_ in
                ("ld", "lw", "tp", "st", "vs", "ss", "ts", "pf")]
            sinv, amx, m2q, fin = sinv_s[app], amax_s[app], m2q_s[app], fin_s[app]

            with contextlib.ExitStack() as ar:
                xt = Rot([sb(f"{tag}xt{i}", [128, DM], st=ar) for i in range(2)])
                yt = [sb(f"{tag}yt{i}", [128, DM], st=ar) for i in range(2)]
                sqt = [sb(f"{tag}sq{i}", [128, DM], st=ar) for i in range(2)]
                xib = Rot([sb(f"{tag}xib{i}", [128, DM], BF16, st=ar) for i in range(2)])
                tmp = [sb(f"{tag}tmp{i}", [128, 1], st=ar) for i in range(4)]
                am = sb(f"{tag}am", [128, 1], st=ar)
                sc = sb(f"{tag}sc", [128, 1], st=ar)
                fwb = sb(f"{tag}fwb", [128, DM], st=ar)
                P.dma("sync", fwb[:], ffnw_b, ld + "f")
                ssx_base = P.val(ss + "x")
                P.wait("vector", ld + "f")
                for m in range(NMT):
                    j, xbuf = xt.next(P, "sync")
                    lv = P.dma("sync", xbuf[:], src[m * 128:(m + 1) * 128, :],
                               f"{ld}_{j}")
                    b = m % 2
                    P.wait("vector", f"{ld}_{j}", lv)
                    if m >= 2:
                        P.wait("vector", ss + "x", ssx_base + m - 1)
                    emit_ln(xbuf[:], yt[b][:], sqt[b][:], tmp, vs, ss)
                    P.wait("scalar", vs)
                    P.op("scalar", lambda e, b=b: e.activation(
                        sqt[b][:], yt[b][:], AF.Square), sem=ss)
                    P.wait("vector", ss)
                    P.op("vector", lambda e, b=b: e.reduce_sum(
                        tmp[0][:], sqt[b][:], axis=AX.X), sem=vs)
                    P.wait("scalar", vs)
                    P.op("scalar", lambda e: e.activation(
                        tmp[2][:], tmp[0][:], AF.Sqrt,
                        scale=cinvd[:, 0:1], bias=ceps[:, 0:1]), sem=ss)
                    P.wait("vector", ss)
                    P.op("vector", lambda e: e.reciprocal(tmp[3][:], tmp[2][:]))
                    P.op("vector", lambda e, b=b: e.tensor_scalar(
                        yt[b][:], yt[b][:], tmp[3][:, 0:1], None, ALU.mult))
                    P.op("vector", lambda e, b=b: e.tensor_tensor(
                        yt[b][:], yt[b][:], fwb[:], ALU.mult))
                    P.op("vector", lambda e, b=b: e.reduce_max(
                        am[:], yt[b][:], axis=AX.X, apply_absolute_value=True))
                    P.op("vector", lambda e: e.tensor_scalar(
                        am[:], am[:], 1e-5, None, ALU.max))
                    P.op("vector", lambda e: e.reciprocal(sc[:], am[:]))
                    P.op("vector", lambda e: e.tensor_scalar(
                        sc[:], sc[:], 127.0, None, ALU.mult))
                    P.op("vector", lambda e, m=m: e.tensor_scalar(
                        sinv[m][:], am[:], 1.0 / 127.0, None, ALU.mult))
                    P.op("vector", lambda e, b=b: e.tensor_scalar(
                        yt[b][:], yt[b][:], sc[:, 0:1], 384.5, ALU.mult, ALU.add))
                    xv = P.op("vector", lambda e, b=b: e.tensor_scalar(
                        yt[b][:].bitcast(I32), yt[b][:].bitcast(I32), maski[:, 0:1],
                        None, ALU.bitwise_and), sem=vs)
                    xt.gate(j, vs, xv)
                    jx, xbuf2 = xib.next(P, "scalar")
                    P.wait("scalar", vs, xv)
                    P.op("scalar", lambda e, b=b, xbuf2=xbuf2: e.activation(
                        xbuf2[:], yt[b][:], AF.Identity, bias=cm384[:, 0:1]), sem=ss + "x")
                    sv = P.dma("scalar", xi_scr[m * 128:(m + 1) * 128, :],
                               xbuf2[:], f"{st}_{jx}")
                    xib.gate(jx, f"{st}_{jx}", sv)
                if _STRIP_TAP is not None and app == 0 and _STRIP_A:
                    P.wait("sync", vs)
                    for m in range(NMT):
                        P.dma("sync", _STRIP_TAP[0:128, m:m + 1], sinv[m][:], "dbgs")
                P.drain("sync", f"{st}_0", f"{st}_1")
                P.barrier()

            with contextlib.ExitStack() as ar:
                xiT = sb(f"{tag}xiT", [128, NMT * TOK], BF16, st=ar)
                wf = Rot([sb(f"{tag}wf{i}", [128, DM], st=ar) for i in range(2)])
                wqa = sb(f"{tag}wqa", [128, DM], st=ar)
                wqb = sb(f"{tag}wqb", [128, DM], st=ar)
                wqo = Rot([sb(f"{tag}wqo{i}", [128, DM], BF16, st=ar) for i in range(2)])
                w1tg = [sb(f"{tag}w1tg{i}", [128, 16 * 512], BF16, st=ar) for i in range(2)]
                w1tu = [sb(f"{tag}w1tu{i}", [128, 16 * 512], BF16, st=ar) for i in range(2)]
                rel = [sb(f"{tag}rel{i}", [128, 512], st=ar) for i in range(2)]
                hid = Rot([sb(f"{tag}hid{i}", [128, 512], st=ar) for i in range(2)])
                red1 = sb(f"{tag}red1", [128, 1], st=ar)
                t1 = sb(f"{tag}t1", [128, 1], st=ar)
                ht = Rot([sb(f"{tag}ht0", [128, FF // 2], st=ar)])
                hb = Rot([sb(f"{tag}hb0", [128, FF // 2], BF16, st=ar)])
                pg = [pst(f"{tag}pg{i}", [128, 512], ar) for i in range(2)]
                pu = [pst(f"{tag}pu{i}", [128, 512], ar) for i in range(2)]
                for m in range(NMT):
                    P.op("vector", lambda e, m=m: e.memset(amx[m][:], 0.0))
                for ft in range(NMT):
                    P.dmaT("sync", xiT[:, ft * TOK:(ft + 1) * TOK],
                           xi_scr[:, ft * 128:(ft + 1) * 128], tp + "x")
                P.wait("tensor", tp + "x")
                nch = F2 // 1024
                mm_iter = 0
                pf_base = P.val(pf)
                vsr_base = P.val(vs + "r")
                hid_sv = [0] * NMT
                hq_sv = [0] * NMT

                w2_subs = [(mq, c0) for mq in range(NMT)
                           for c0 in (0, 2048, 4096)]

                def quant_w2_tiles(lo, hi):
                    for mq, c0 in w2_subs[lo:hi]:
                        cn = min(2048, FF - c0)
                        jw, wbuf = wf.next(P, "sync")
                        lv = P.dma("sync", wbuf[:, 0:cn],
                                   w2_in[mq * 128:(mq + 1) * 128, c0:c0 + cn],
                                   f"{lw}_{jw}")
                        jq, qbuf = wqo.next(P, "vector")
                        P.wait("vector", f"{lw}_{jw}", lv)
                        P.op("vector", lambda e, wbuf=wbuf, cn=cn: e.tensor_scalar(
                            wqa[:, 0:cn], wbuf[:, 0:cn], sw2s[:, 0:1], 0.5,
                            ALU.mult, ALU.is_gt))
                        vv = P.op("vector", lambda e, wbuf=wbuf, cn=cn: e.tensor_scalar(
                            wqb[:, 0:cn], wbuf[:, 0:cn], sw2s[:, 0:1], -0.5,
                            ALU.mult, ALU.is_lt), sem=vs + "w")
                        wf.gate(jw, vs + "w", vv)
                        qv = P.op("vector", lambda e, qbuf=qbuf, cn=cn: e.tensor_tensor(
                            qbuf[:, 0:cn], wqa[:, 0:cn], wqb[:, 0:cn], ALU.subtract),
                            sem=vs + "q")
                        P.wait("scalar", vs + "q", qv)
                        sv2 = P.dma("scalar",
                                    w2q_scr[mq * 128:(mq + 1) * 128, c0:c0 + cn],
                                    qbuf[:, 0:cn], f"{st}w{jq}")
                        wqo.gate(jq, f"{st}w{jq}", sv2)

                def prep(n):
                    bn = n % 2
                    if app == 0:
                        for rbase in (n * 512, FF + n * 512):
                            for s4 in range(4):
                                r0 = rbase + s4 * 128
                                jw, wbuf = wf.next(P, "sync")
                                lv = P.dma("sync", wbuf[:], w1_in[r0:r0 + 128, :],
                                           f"{lw}_{jw}")
                                P.wait("vector", f"{lw}_{jw}", lv)
                                P.op("vector", lambda e, wbuf=wbuf: e.tensor_scalar(
                                    wqa[:], wbuf[:], sw1s[:, 0:1], 0.5, ALU.mult,
                                    ALU.is_gt))
                                vv = P.op("vector", lambda e, wbuf=wbuf: e.tensor_scalar(
                                    wqb[:], wbuf[:], nsw1s[:, 0:1], 0.5, ALU.mult,
                                    ALU.is_gt), sem=vs + "w")
                                wf.gate(jw, vs + "w", vv)
                                jq, qbuf = wqo.next(P, "vector")
                                qv = P.op("vector", lambda e, qbuf=qbuf: e.tensor_tensor(
                                    qbuf[:], wqa[:], wqb[:], ALU.subtract), sem=vs + "q")
                                P.wait("scalar", vs + "q", qv)
                                sv = P.dma("scalar", w1q_scr[r0:r0 + 128, :], qbuf[:],
                                           f"{st}w{jq}")
                                wqo.gate(jq, f"{st}w{jq}", sv)
                        quant_w2_tiles(n * 5 if n < 9 else 45 + (n - 9) * 2,
                                       (n + 1) * 5 if n < 9 else 45 + (n - 8) * 2)
                        for jq2 in range(2):
                            P.wait("sync", f"{st}w{jq2}")
                    if n >= 2:
                        P.wait("sync", pf, pf_base + 16 * (n - 1))
                    for ft in range(16):
                        P.dmaT("sync", w1tg[bn][:, ft * 512:(ft + 1) * 512],
                               w1q_scr[n * 512:(n + 1) * 512, ft * 128:(ft + 1) * 128],
                               f"{tp}w{bn}")
                        P.dmaT("sync", w1tu[bn][:, ft * 512:(ft + 1) * 512],
                               w1q_scr[FF + n * 512:FF + (n + 1) * 512,
                                       ft * 128:(ft + 1) * 128], f"{tp}w{bn}")
                    return P.val(f"{tp}w{bn}")

                def emit_hidq(m):
                    P.op("vector", lambda e, m=m: e.tensor_scalar(
                        t1[:], sinv[m][:], w1m[:, 0:1], None, ALU.mult))
                    P.op("vector", lambda e, m=m: e.tensor_tensor(
                        m2q[m][:], t1[:], t1[:], ALU.mult))
                    P.op("vector", lambda e, m=m: e.tensor_tensor(
                        m2q[m][:], m2q[m][:], t1[:], ALU.mult))
                    P.op("vector", lambda e, m=m: e.tensor_tensor(
                        t1[:], amx[m][:], m2q[m][:], ALU.mult))
                    P.op("vector", lambda e: e.tensor_scalar(
                        t1[:], t1[:], 1e-5, None, ALU.max))
                    P.op("vector", lambda e, m=m: e.tensor_scalar(
                        fin[m][:], t1[:], 1.0 / 127.0, None, ALU.mult))
                    P.op("vector", lambda e, m=m: e.tensor_scalar(
                        fin[m][:], fin[m][:], w2m[:, 0:1], 0.5, ALU.mult, ALU.mult))
                    P.op("vector", lambda e: e.reciprocal(t1[:], t1[:]))
                    P.op("vector", lambda e: e.tensor_scalar(
                        t1[:], t1[:], 127.0, None, ALU.mult))
                    P.op("vector", lambda e, m=m: e.tensor_tensor(
                        m2q[m][:], m2q[m][:], t1[:], ALU.mult))
                    for hh in range(2):
                        c0 = hh * (FF // 2)
                        jh, hbuf = ht.next(P, "sync")
                        P.wait("sync", f"{st}h0")
                        P.wait("sync", f"{st}h1")
                        lv = P.dma("sync", hbuf[:],
                                   hid_scr[m * 128:(m + 1) * 128, c0:c0 + FF // 2],
                                   ld + "h")
                        P.wait("vector", ld + "h", lv)
                        P.op("vector", lambda e, hbuf=hbuf, m=m: e.tensor_scalar(
                            hbuf[:], hbuf[:], m2q[m][:, 0:1], 384.5, ALU.mult, ALU.add))
                        hv = P.op("vector", lambda e, hbuf=hbuf: e.tensor_scalar(
                            hbuf[:].bitcast(I32), hbuf[:].bitcast(I32), maski[:, 0:1],
                            None, ALU.bitwise_and), sem=vs)
                        jb, bbuf = hb.next(P, "scalar")
                        P.wait("scalar", vs, hv)
                        av = P.op("scalar", lambda e, hbuf=hbuf, bbuf=bbuf: e.activation(
                            bbuf[:], hbuf[:], AF.Identity, bias=cm384[:, 0:1]), sem=ss)
                        ht.gate(jh, ss, av)
                        sv = P.dma("scalar",
                                   hq_scr[m * 128:(m + 1) * 128, c0:c0 + FF // 2],
                                   bbuf[:], f"{st}q{jb}")
                        hb.gate(jb, f"{st}q{jb}", sv)
                        hq_sv[m] = sv

                tp_chunk = [0] * nch
                tp_chunk[0] = prep(0)
                if nch > 1:
                    tp_chunk[1] = prep(1)
                for n in range(nch):
                    bn = n % 2
                    if n + 2 < nch:
                        tp_chunk[n + 2] = prep(n + 2)
                    P.wait("tensor", f"{tp}w{bn}", tp_chunk[n])
                    for m in range(NMT):
                        bm = mm_iter % 2
                        if mm_iter >= 2:
                            P.wait("tensor", pf, pf_base + mm_iter - 1)
                        for kt in range(16):
                            P.op("tensor", lambda e, bn=bn, bm=bm, m=m, kt=kt: e.matmul(
                                pg[bm][:],
                                xiT[:, kt * TOK + m * 128: kt * TOK + m * 128 + 128],
                                w1tg[bn][:, kt * 512:(kt + 1) * 512],
                                start=(kt == 0), stop=(kt == 15)))
                        tvv = 0
                        for kt in range(16):
                            tvv = P.op("tensor", lambda e, bn=bn, bm=bm, m=m, kt=kt:
                                       e.matmul(
                                           pu[bm][:],
                                           xiT[:, kt * TOK + m * 128:
                                               kt * TOK + m * 128 + 128],
                                           w1tu[bn][:, kt * 512:(kt + 1) * 512],
                                           start=(kt == 0), stop=(kt == 15)),
                                       sem=(ts_ if kt == 15 else None))
                        if mm_iter >= 2:
                            P.wait("vector", vs + "r", vsr_base + mm_iter - 1)
                        P.wait("vector", ts_, tvv)
                        P.op("vector", lambda e, bm=bm: e.tensor_scalar(
                            rel[bm][:], pg[bm][:], 0.0, None, ALU.max))
                        P.op("vector", lambda e, bm=bm: e.tensor_tensor(
                            rel[bm][:], rel[bm][:], rel[bm][:], ALU.mult))
                        jh, hbuf = hid.next(P, "vector")
                        P.op("vector", lambda e, bm=bm, hbuf=hbuf: e.tensor_tensor(
                            hbuf[:], rel[bm][:], pu[bm][:], ALU.mult), sem=pf)
                        P.op("vector", lambda e, hbuf=hbuf: e.reduce_max(
                            red1[:], hbuf[:], axis=AX.X, apply_absolute_value=True))
                        hv = P.op("vector", lambda e, m=m: e.tensor_tensor(
                            amx[m][:], amx[m][:], red1[:], ALU.max), sem=vs + "r")
                        P.wait("scalar", vs + "r", hv)
                        sv = P.dma("scalar",
                                   hid_scr[m * 128:(m + 1) * 128,
                                           n * 512:(n + 1) * 512], hbuf[:],
                                   f"{st}h{jh}")
                        hid.gate(jh, f"{st}h{jh}", sv)
                        hid_sv[m] = sv
                        if n == nch - 1:
                            emit_hidq(m)
                        mm_iter += 1
                P.drain("sync", f"{st}h0", f"{st}h1", f"{st}q0", f"{st}q1")
                if app == 0:
                    P.drain("sync", f"{st}w0", f"{st}w1")
                P.barrier()

            with contextlib.ExitStack() as ar:
                NKT = FF // 128
                w2t = [sb(f"{tag}w2t{i}", [128, NKT * 512], BF16, st=ar)
                       for i in range(2)]
                hqT = [sb(f"{tag}hqT{i}", [128, NKT * 512], BF16, st=ar) for i in range(2)]
                xres = Rot([sb(f"{tag}xr{i}", [128, 512], st=ar) for i in range(2)])
                stg = Rot([sb(f"{tag}sg{i}", [128, 512], st=ar) for i in range(2)])
                pd = [pst(f"{tag}pd{i}", [128, 512], ar) for i in range(4)]
                hq_gate = [None, None]
                gidx = 0
                pfd_base = P.val(pf)
                tsd_base = P.val(ts_)
                for mg in range(4):
                    bg = mg % 2
                    if hq_gate[bg] is not None:
                        P.wait("sync", hq_gate[bg][0], hq_gate[bg][1])
                    for ft in range(NKT):
                        P.dmaT("sync",
                               hqT[bg][:, ft * 512:(ft + 1) * 512],
                               hq_scr[mg * 512:(mg + 1) * 512, ft * 128:(ft + 1) * 128],
                               f"{tp}h{bg}")
                    hq_tpv = P.val(f"{tp}h{bg}")
                    for qi in range(4):
                        bw = gidx % 2
                        if gidx > 1:
                            P.wait("sync", ts_, tsd_base + gidx - 1)
                        for ft in range(NKT):
                            P.dmaT("sync",
                                   w2t[bw][:, ft * 512:(ft + 1) * 512],
                                   w2q_scr[qi * 512:(qi + 1) * 512,
                                           ft * 128:(ft + 1) * 128], f"{tp}2{bw}")
                        P.wait("tensor", f"{tp}h{bg}", hq_tpv)
                        P.wait("tensor", f"{tp}2{bw}")
                        if gidx > 0:
                            P.wait("tensor", pf, pfd_base + gidx * 4)
                        tvv = 0
                        for ft in range(NKT):
                            for mi in range(4):
                                last = (ft == NKT - 1 and mi == 3)
                                tvv = P.op("tensor",
                                           lambda e, bg=bg, ft=ft, mi=mi, bw=bw: e.matmul(
                                               pd[mi][:],
                                               hqT[bg][:, ft * 512 + mi * 128:
                                                        ft * 512 + mi * 128 + 128],
                                               w2t[bw][:, ft * 512:(ft + 1) * 512],
                                               start=(ft == 0), stop=(ft == NKT - 1)),
                                           sem=(ts_ if last else None))
                        for mi in range(4):
                            m = mg * 4 + mi
                            jx, xbuf = xres.next(P, "sync")
                            lv = P.dma("sync", xbuf[:],
                                       src[m * 128:(m + 1) * 128,
                                           qi * 512:(qi + 1) * 512], f"{ld}x{jx}")
                            js, sbuf_ = stg.next(P, "vector")
                            P.wait("vector", ts_, tvv)
                            P.wait("vector", f"{ld}x{jx}", lv)
                            xv = P.op("vector",
                                      lambda e, mi=mi, m=m, xbuf=xbuf, sbuf_=sbuf_:
                                      e.scalar_tensor_tensor(
                                          sbuf_[:], pd[mi][:], fin[m][:, 0:1], xbuf[:],
                                          ALU.mult, ALU.add), sem=pf)
                            xres.gate(jx, pf, P.val(pf))
                            P.wait("scalar", pf, P.val(pf))
                            sv = P.dma("scalar",
                                       dst[m * 128:(m + 1) * 128, qi * 512:(qi + 1) * 512],
                                       sbuf_[:], f"{st}d{js}")
                            stg.gate(js, f"{st}d{js}", sv)
                        gidx += 1
                    hq_gate[bg] = (pf, pfd_base + gidx * 4)
                P.drain("sync", f"{st}d0", f"{st}d1")
                P.barrier()

        def phase2():
            with contextlib.ExitStack() as outer:
                yT = sb("p2yT", [128, NMT * TOK], st=outer)
                with contextlib.ExitStack() as ar:
                    xt = Rot([sb(f"p2xt{i}", [128, DM], st=ar) for i in range(2)])
                    yt = [sb(f"p2yt{i}", [128, DM], st=ar) for i in range(2)]
                    sqt = [sb(f"p2sq{i}", [128, DM], st=ar) for i in range(2)]
                    tmp = [sb(f"p2tmp{i}", [128, 1], st=ar) for i in range(4)]
                    tpp = [pst(f"p2tp{i}", [128, 512], ar) for i in range(2)]
                    yT_r = yT[:].rearrange("p (k t) -> p k t", k=NMT)
                    for m in range(NMT):
                        j, xbuf = xt.next(P, "sync")
                        lv = P.dma("sync", xbuf[:], x1_scr[m * 128:(m + 1) * 128, :],
                                   f"p2ld{j}")
                        b = m % 2
                        P.wait("vector", f"p2ld{j}", lv)
                        if m >= 2:
                            P.wait("vector", "p2tv", (m - 1) * 4)
                        emit_ln(xbuf[:], yt[b][:], sqt[b][:], tmp, "p2vs", "p2ss")
                        yv = P.op("vector", lambda e: e.nop(), sem="p2y")
                        xt.gate(j, "p2y", yv)
                        P.wait("tensor", "p2y", yv)
                        for g in range(4):
                            gi = m * 4 + g
                            bg = gi % 2
                            if gi >= 2:
                                P.wait("tensor", "p2tv", gi - 1)
                            for t4 in range(4):
                                ft = g * 4 + t4
                                P.op("tensor", lambda e, b=b, bg=bg, ft=ft, t4=t4: e.transpose(
                                    tpp[bg][:, t4 * 128:(t4 + 1) * 128],
                                    yt[b][:, ft * 128:(ft + 1) * 128], ident[:]),
                                    sem=("p2tt" if t4 == 3 else None))
                            P.wait("vector", "p2tt")
                            P.op("vector", lambda e, bg=bg, g=g, m=m: e.tensor_copy(
                                yT_r[:, g * 4:(g + 1) * 4, m * 128:(m + 1) * 128],
                                tpp[bg][:].rearrange("p (f t) -> p f t", f=4)), sem="p2tv")
                    P.barrier()
                with contextlib.ExitStack() as ar:
                    wf = Rot([sb(f"p2wf{i}", [128, DM], st=ar) for i in range(2)])
                    Wt = sb("p2Wt", [128, 16 * 512], st=ar)
                    stg = Rot([sb(f"p2sg{i}", [128, 512], st=ar) for i in range(2)])
                    st2 = Rot([sb(f"p2s2{i}", [128, 512], st=ar) for i in range(2)])
                    tpp = [pst(f"p2tq{i}", [128, 512], ar) for i in range(2)]
                    mmp = [pst(f"p2mm{i}", [128, 512], ar) for i in range(2)]
                    Wt_r = Wt[:].rearrange("p (k r) -> p k r", k=16)
                    qT_r = qT_scr.rearrange("(b p) t -> p b t", p=128)
                    kT_r = kT_scr.rearrange("(b p) t -> p b t", p=128)
                    tpi = 0
                    mmi = 0
                    for c in range(16):
                        if c > 0:
                            P.wait("vector", "p2mt", mmi)
                        wv = 0
                        for s4 in range(4):
                            j, wbuf = wf.next(P, "sync")
                            lv = P.dma("sync", wbuf[:],
                                       w_attn[c * 512 + s4 * 128:
                                              c * 512 + (s4 + 1) * 128, :], f"p2lw{j}")
                            P.wait("tensor", f"p2lw{j}", lv)
                            for g in range(4):
                                bg = tpi % 2
                                if tpi >= 2:
                                    P.wait("tensor", "p2wv", tpi - 1)
                                for t4 in range(4):
                                    kt = g * 4 + t4
                                    P.op("tensor",
                                         lambda e, wbuf=wbuf, bg=bg, kt=kt, t4=t4:
                                         e.transpose(tpp[bg][:, t4 * 128:(t4 + 1) * 128],
                                                     wbuf[:, kt * 128:(kt + 1) * 128],
                                                     ident[:]),
                                         sem=("p2wt" if t4 == 3 else None))
                                P.wait("vector", "p2wt")
                                wv = P.op("vector", lambda e, bg=bg, g=g, s4=s4: e.tensor_copy(
                                    Wt_r[:, g * 4:(g + 1) * 4, s4 * 128:(s4 + 1) * 128],
                                    tpp[bg][:].rearrange("p (f t) -> p f t", f=4)),
                                    sem="p2wv")
                                tpi += 1
                            wf.gate(j, "p2wv", wv)
                        P.wait("tensor", "p2wv", wv)
                        for m in range(NMT):
                            bm = mmi % 2
                            if mmi >= 2:
                                P.wait("tensor", "p2pf", mmi - 1)
                            tvv = 0
                            for kt in range(16):
                                tvv = P.op("tensor", lambda e, bm=bm, m=m, kt=kt: e.matmul(
                                    mmp[bm][:],
                                    yT[:, kt * TOK + m * 128: kt * TOK + m * 128 + 128],
                                    Wt[:, kt * 512:(kt + 1) * 512],
                                    start=(kt == 0), stop=(kt == 15)),
                                    sem=("p2mt" if kt == 15 else None))
                            mmi += 1
                            js, sbuf_ = stg.next(P, "scalar")
                            P.wait("scalar", "p2mt", tvv)
                            func = AF.Silu if c < 12 else AF.Sigmoid
                            fv = P.op("scalar", lambda e, bm=bm, sbuf_=sbuf_, func=func:
                                      e.activation(sbuf_[:], mmp[bm][:], func), sem="p2pf")
                            if c < 12:
                                dstap = qkv_scr[m * 128:(m + 1) * 128, c * 512:(c + 1) * 512]
                            else:
                                dstap = gate_scr[m * 128:(m + 1) * 128,
                                                 (c - 12) * 512:(c - 11) * 512]
                            sv = P.dma("scalar", dstap, sbuf_[:], f"p2st{js}")
                            if c < 8:
                                P.wait("tensor", "p2pf", fv)
                                bg = tpi % 2
                                if tpi >= 2:
                                    P.wait("tensor", "p2wv", tpi - 1)
                                for t4 in range(4):
                                    P.op("tensor", lambda e, sbuf_=sbuf_, bg=bg, t4=t4:
                                         e.transpose(tpp[bg][:, t4 * 128:(t4 + 1) * 128],
                                                     sbuf_[:, t4 * 128:(t4 + 1) * 128],
                                                     ident[:]),
                                         sem=("p2wt" if t4 == 3 else None))
                                qtv = P.val("p2wt")
                                j2, s2buf = st2.next(P, "vector")
                                P.wait("vector", "p2wt", qtv)
                                wv2 = P.op("vector", lambda e, bg=bg, s2buf=s2buf:
                                           e.tensor_copy(s2buf[:], tpp[bg][:]), sem="p2wv")
                                tpi += 1
                                P.wait("sync", "p2wv", wv2)
                                scr_r = qT_r if c < 4 else kT_r
                                cc = c if c < 4 else c - 4
                                sv2 = P.dma("sync",
                                            scr_r[:, cc * 4:(cc + 1) * 4,
                                                  m * 128:(m + 1) * 128],
                                            s2buf[:].rearrange("p (f t) -> p f t", f=4),
                                            f"p2s2{j2}")
                                st2.gate(j2, f"p2s2{j2}", sv2)
                                stg.gate(js, f"p2st{js}", sv)
                            else:
                                stg.gate(js, f"p2st{js}", sv)
                    P.drain("sync", "p2st0", "p2st1", "p2s20", "p2s21")
                    P.barrier()

        def phase3a():
            kv_r = lambda i, c0, c1: qkv_scr[i * 256:(i + 1) * 256, c0:c1].rearrange(
                "(j p) c -> p j c", p=128)
            d3 = lambda ap, n: ap.rearrange("p (j c) -> p j c", j=n)
            with contextlib.ExitStack() as ar:
                kb = Rot([sb(f"p3ak{i}", [128, 2 * DM], st=ar) for i in range(2)])
                vb = Rot([sb(f"p3av{i}", [128, 2 * DM], st=ar) for i in range(2)])
                kk = Rot([sb(f"p3akk{i}", [128, 64], st=ar) for i in range(2)])
                Sloc = sb("p3aS", [64, H * D], st=ar)
                gs = Rot([sb(f"p3gs{i}", [64, H * D], st=ar) for i in range(2)])
                tpb = Rot([sb(f"p3tp{i}", [64, H * D], st=ar) for i in range(2)])
                cps = [pst(f"p3acp{i}", [64, 64], ar) for i in range(2)]
                P.op("vector", lambda e: e.memset(Sloc[:], 0.0))
                n_sv = 0
                for i in range(NBLK):
                    jk, kbuf = kb.next(P, "sync")
                    kv1 = P.dma("sync", d3(kbuf[:], 2), kv_r(i, DM, 2 * DM),
                                f"p3ak{jk}")
                    jv, vbuf = vb.next(P, "sync")
                    kv2 = P.dma("sync", d3(vbuf[:], 2), kv_r(i, 2 * DM, 3 * DM),
                                f"p3av{jv}")
                    P.wait("vector", f"p3ak{jk}", kv1)
                    P.wait("vector", f"p3av{jv}", kv2)
                    P.wait("tensor", f"p3av{jv}", kv2)
                    for hh in range(H):
                        bp = hh % 2
                        if n_sv >= 2:
                            P.wait("tensor", "p3sv", n_sv - 1)
                        tv = 0
                        for jh in range(2):
                            jkk, kkb = kk.next(P, "vector")
                            kv_ = P.op("vector",
                                       lambda e, kbuf=kbuf, hh=hh, jh=jh, kkb=kkb:
                                       e.tensor_scalar(
                                           kkb[:],
                                           kbuf[:, jh * DM + hh * 64: jh * DM + hh * 64 + 64],
                                           kdc[:, hh * 2 + jh: hh * 2 + jh + 1],
                                           None, ALU.mult), sem="p3kv")
                            P.wait("tensor", "p3kv", kv_)
                            tv = P.op("tensor",
                                      lambda e, bp=bp, kkb=kkb, vbuf=vbuf, hh=hh, jh=jh:
                                      e.matmul(cps[bp][:], kkb[:],
                                               vbuf[:, jh * DM + hh * 64:
                                                    jh * DM + hh * 64 + 64],
                                               start=(jh == 0), stop=(jh == 1)),
                                      sem=("p3tc" if jh == 1 else None))
                            kk.gate(jkk, "p3tc", P.val("p3tc") if jh == 1 else
                                    P.val("p3tc") + 1)
                        P.wait("vector", "p3tc", tv)
                        sv_ = P.op("vector", lambda e, bp=bp, hh=hh: e.scalar_tensor_tensor(
                            Sloc[:, hh * 64:(hh + 1) * 64], Sloc[:, hh * 64:(hh + 1) * 64],
                            float(_BD[hh]), cps[bp][:], ALU.mult, ALU.add), sem="p3sv")
                        n_sv += 1
                    kb.gate(jk, "p3kv", P.val("p3kv"))
                    vb.gate(jv, "p3tc", P.val("p3tc"))
                P.wait("scalar", "p3sv")
                sv = P.dma("scalar", cc_sin, Sloc[:], "p3st")
                P.wait("gpsimd", "p3st", sv)
                P.op("gpsimd", lambda e: e.collective_compute(
                    "AllGather", ALU.bypass, replica_groups=[list(range(n_rg))],
                    ins=[cc_sin.opt()],
                    outs=[(cc_sout[0:64 * n_rg, :] if n_rg != NCORES else cc_sout).opt()]),
                    sem="p3cc")
                P.op("vector", lambda e: e.memset(Sin_st[:], 0.0))
                P.wait("sync", "p3cc")
                for cp in range(n_rg):
                    jg, gbuf = gs.next(P, "sync")
                    gv1 = P.dma("sync", gbuf[:], cc_sout[cp * 64:(cp + 1) * 64, :],
                                f"p3ag{jg}")
                    jt, tbuf = tpb.next(P, "sync")
                    gv2 = P.dma("sync", tbuf[:], tpred_in[cp], f"p3at{jt}")
                    P.wait("vector", f"p3ag{jg}", gv1)
                    P.wait("vector", f"p3at{jt}", gv2)
                    P.op("vector", lambda e, gbuf=gbuf, tbuf=tbuf: e.tensor_tensor(
                        gbuf[:], gbuf[:], tbuf[:], ALU.mult))
                    gv = P.op("vector", lambda e, gbuf=gbuf: e.tensor_tensor(
                        Sin_st[:], Sin_st[:], gbuf[:], ALU.add), sem="p3gv")
                    gs.gate(jg, "p3gv", gv)
                    tpb.gate(jt, "p3gv", gv)
                P.barrier()

        def phase3b():
            kv_r = lambda i, c0, c1: qkv_scr[i * 256:(i + 1) * 256, c0:c1].rearrange(
                "(j p) c -> p j c", p=128)
            qT_sl = lambda i: qT_scr[:, i * 256:(i + 1) * 256].rearrange(
                "(t p) c -> p t c", p=128)
            kT_sl = lambda i: kT_scr[:, i * 256:(i + 1) * 256].rearrange(
                "(t p) c -> p t c", p=128)
            d3 = lambda ap, n: ap.rearrange("p (j c) -> p j c", j=n)
            with contextlib.ExitStack() as ar:
                dd0 = sb("p3dd0", [128, H * 256], st=ar)
                qdb = sb("p3qdb", [128, (H // 2) * 256], st=ar)
                kb = Rot([sb("p3k0", [128, 2 * DM], st=ar)])
                vb = Rot([sb("p3v0", [128, 2 * DM], st=ar)])
                qs = Rot([sb("p3q0", [128, NMT * 256], st=ar)])
                ks = Rot([sb("p3kT0", [128, NMT * 256], st=ar)])
                kk = Rot([sb(f"p3kk{i}", [128, 64], st=ar) for i in range(2)])
                qq = Rot([sb(f"p3qq{i}", [64, 256], st=ar) for i in range(2)])
                aw0 = Rot([sb(f"p3a0{i}", [128, 256], st=ar) for i in range(2)])
                aw1 = Rot([sb(f"p3a1{i}", [128, 128], st=ar) for i in range(2)])
                asm = Rot([sb(f"p3asm{i}", [128, 2 * DM], st=ar) for i in range(2)])
                Scur = [sb(f"p3S{i}", [64, H * D], st=ar) for i in range(2)]
                pa0 = [pst(f"p3pa0{i}", [128, 256], ar) for i in range(2)]
                pa1 = [pst(f"p3pa1{i}", [128, 128], ar) for i in range(2)]
                pi0 = pst("p3pi0", [128, 64], ar)
                pi1 = pst("p3pi1", [128, 64], ar)
                cps = [pst(f"p3cp{i}", [64, 64], ar) for i in range(2)]
                P.dma("sync", dd0[:], dd0_in, "p3c")
                P.dma("sync", qdb[:], qdb_in, "p3c")
                P.wait("vector", "p3c")
                P.wait("vector", "p3gv", P.val("p3gv"))
                P.op("vector", lambda e: e.tensor_copy(Scur[0][:], Sin_st[:]), sem="p3sv2")
                awi = 0
                n_sv = P.val("p3sv2")
                n_ie = 0
                sv_prev_block = P.val("p3sv2")
                for i in range(NBLK):
                    Sa, Sb = Scur[i % 2], Scur[(i + 1) % 2]
                    jk, kbuf = kb.next(P, "sync")
                    b1 = P.dma("sync", d3(kbuf[:], 2), kv_r(i, DM, 2 * DM), "p3bk")
                    jv, vbuf = vb.next(P, "sync")
                    b2 = P.dma("sync", d3(vbuf[:], 2), kv_r(i, 2 * DM, 3 * DM), "p3bv")
                    jq, qsb = qs.next(P, "sync")
                    b3 = P.dma("sync", d3(qsb[:], NMT), qT_sl(i), "p3bq")
                    jkt, ksb = ks.next(P, "sync")
                    b4 = P.dma("sync", d3(ksb[:], NMT), kT_sl(i), "p3bt")
                    for s_, v_ in (("p3bk", b1), ("p3bv", b2), ("p3bq", b3),
                                   ("p3bt", b4)):
                        P.wait("tensor", s_, v_)
                        P.wait("vector", s_, v_)
                    P.wait("tensor", "p3sv2", sv_prev_block)
                    ja, abuf = asm.next(P, "vector")
                    for hh in range(H):
                        p0 = 64 * (hh % 2)
                        hcol = (hh // 2) * 256
                        ba = awi % 2
                        if awi >= 2:
                            P.wait("tensor", "p3de", awi - 1)
                        P.op("tensor", lambda e, ba=ba, ksb=ksb, qsb=qsb, p0=p0, hcol=hcol:
                             e.matmul(pa0[ba][:], ksb[p0:p0 + 64, hcol:hcol + 128],
                                      qsb[p0:p0 + 64, hcol:hcol + 256],
                                      start=True, stop=True))
                        tv = P.op("tensor", lambda e, ba=ba, ksb=ksb, qsb=qsb, p0=p0, hcol=hcol:
                                  e.matmul(pa1[ba][:], ksb[p0:p0 + 64, hcol + 128:hcol + 256],
                                           qsb[p0:p0 + 64, hcol + 128:hcol + 256],
                                           start=True, stop=True), sem="p3aw")
                        ja0, a0b = aw0.next(P, "vector")
                        ja1, a1b = aw1.next(P, "vector")
                        P.wait("vector", "p3aw", tv)
                        P.op("vector", lambda e, ba=ba, a0b=a0b, hh=hh: e.tensor_tensor(
                            a0b[:], pa0[ba][:], dd0[:, hh * 256:(hh + 1) * 256], ALU.mult))
                        av = P.op("vector", lambda e, ba=ba, a1b=a1b, hh=hh: e.tensor_tensor(
                            a1b[:], pa1[ba][:], dd0[:, hh * 256:hh * 256 + 128], ALU.mult),
                            sem="p3de")
                        jqq, qqb = qq.next(P, "vector")
                        qv = P.op("vector",
                                  lambda e, qqb=qqb, qsb=qsb, qdb=qdb, p0=p0, hcol=hcol:
                                  e.tensor_tensor(qqb[:], qsb[p0:p0 + 64, hcol:hcol + 256],
                                                  qdb[p0:p0 + 64, hcol:hcol + 256],
                                                  ALU.mult), sem="p3qe")
                        P.wait("tensor", "p3de", av)
                        P.wait("tensor", "p3qe", qv)
                        if n_ie >= 1:
                            P.wait("tensor", "p3ie", n_ie)
                        P.op("tensor", lambda e, a0b=a0b, vbuf=vbuf, hh=hh: e.matmul(
                            pi0[:], a0b[:, 0:128], vbuf[:, hh * 64: hh * 64 + 64],
                            start=True, stop=False))
                        P.op("tensor", lambda e, qqb=qqb, Sa=Sa, hh=hh: e.matmul(
                            pi0[:], qqb[:, 0:128], Sa[:, hh * 64:(hh + 1) * 64],
                            start=False, stop=True), sem="p3it")
                        P.op("tensor", lambda e, a0b=a0b, vbuf=vbuf, hh=hh: e.matmul(
                            pi1[:], a0b[:, 128:256], vbuf[:, hh * 64: hh * 64 + 64],
                            start=True, stop=False))
                        P.op("tensor", lambda e, a1b=a1b, vbuf=vbuf, hh=hh: e.matmul(
                            pi1[:], a1b[:], vbuf[:, DM + hh * 64: DM + hh * 64 + 64],
                            start=False, stop=False))
                        itv = P.op("tensor", lambda e, qqb=qqb, Sa=Sa, hh=hh: e.matmul(
                            pi1[:], qqb[:, 128:256], Sa[:, hh * 64:(hh + 1) * 64],
                            start=False, stop=True), sem="p3it")
                        aw0.gate(ja0, "p3it", itv)
                        aw1.gate(ja1, "p3it", itv)
                        qq.gate(jqq, "p3it", itv)
                        bp = hh % 2
                        if n_sv >= 2:
                            P.wait("tensor", "p3sv2", n_sv - 1)
                        ctv = 0
                        for jh in range(2):
                            jkk, kkb = kk.next(P, "vector")
                            kv_ = P.op("vector",
                                       lambda e, kbuf=kbuf, hh=hh, jh=jh, kkb=kkb:
                                       e.tensor_scalar(
                                           kkb[:],
                                           kbuf[:, jh * DM + hh * 64: jh * DM + hh * 64 + 64],
                                           kdc[:, hh * 2 + jh: hh * 2 + jh + 1],
                                           None, ALU.mult), sem="p3kv2")
                            P.wait("tensor", "p3kv2", kv_)
                            ctv = P.op("tensor",
                                       lambda e, bp=bp, kkb=kkb, vbuf=vbuf, hh=hh, jh=jh:
                                       e.matmul(cps[bp][:], kkb[:],
                                                vbuf[:, jh * DM + hh * 64:
                                                     jh * DM + hh * 64 + 64],
                                                start=(jh == 0), stop=(jh == 1)),
                                       sem=("p3tc2" if jh == 1 else None))
                            kk.gate(jkk, "p3tc2", P.val("p3tc2") if jh == 1 else
                                    P.val("p3tc2") + 1)
                        P.wait("vector", "p3it", itv)
                        P.op("vector", lambda e, abuf=abuf, hh=hh: e.tensor_copy(
                            abuf[:, hh * 64:(hh + 1) * 64], pi0[:]))
                        P.op("vector", lambda e, abuf=abuf, hh=hh: e.tensor_copy(
                            abuf[:, DM + hh * 64: DM + hh * 64 + 64], pi1[:]), sem="p3ie")
                        n_ie += 1
                        P.wait("vector", "p3tc2", ctv)
                        P.op("vector", lambda e, Sa=Sa, Sb=Sb, bp=bp, hh=hh:
                             e.scalar_tensor_tensor(
                                 Sb[:, hh * 64:(hh + 1) * 64], Sa[:, hh * 64:(hh + 1) * 64],
                                 float(_BD[hh]), cps[bp][:], ALU.mult, ALU.add), sem="p3sv2")
                        n_sv += 1
                        awi += 1
                    sv_prev_block = P.val("p3sv2")
                    kb.gate(jk, "p3kv2", P.val("p3kv2"))
                    vb.gate(jv, "p3tc2", P.val("p3tc2"))
                    qs.gate(jq, "p3qe", P.val("p3qe"))
                    ks.gate(jkt, "p3aw", P.val("p3aw"))
                    P.wait("scalar", "p3ie", P.val("p3ie"))
                    P.dma("scalar", attn_scr[i * 256:i * 256 + 128, :],
                          abuf[:, 0:DM], f"p3s{ja}")
                    sv2 = P.dma("scalar", attn_scr[i * 256 + 128:(i + 1) * 256, :],
                                abuf[:, DM:2 * DM], f"p3s{ja}")
                    asm.gate(ja, f"p3s{ja}", sv2)
                P.drain("sync", "p3s0", "p3s1")
                P.barrier()

        def phase4():
            b_tv = P.val("p2tv")
            b_z = P.val("p2z")
            b_wv = P.val("p2wv")
            b_mt = P.val("p2mt")
            b_pf = P.val("p2pf")
            with contextlib.ExitStack() as outer:
                zT = sb("p4zT", [128, NMT * TOK], st=outer)
                with contextlib.ExitStack() as ar:
                    at = Rot([sb("p4at0", [128, DM], st=ar)])
                    gt = Rot([sb("p4gt0", [128, DM], st=ar)])
                    awb = sb("p4awb", [128, DM], st=ar)
                    P.dma("sync", awb[:], attnw_b, "p2lw0")
                    P.wait("vector", "p2lw0")
                    sqt = [sb(f"p4sq{i}", [128, DM], st=ar) for i in range(2)]
                    zt = [sb(f"p4zt{i}", [128, DM], st=ar) for i in range(2)]
                    tmp = [sb(f"p4tmp{i}", [128, 1], st=ar) for i in range(4)]
                    tpp = [pst(f"p4tp{i}", [128, 512], ar) for i in range(2)]
                    zT_r = zT[:].rearrange("p (k t) -> p k t", k=NMT)
                    for m in range(NMT):
                        ja, abuf = at.next(P, "sync")
                        a1 = P.dma("sync", abuf[:], attn_scr[m * 128:(m + 1) * 128, :],
                                   "p2ld0")
                        jg, gbuf = gt.next(P, "sync")
                        g1 = P.dma("sync", gbuf[:], gate_scr[m * 128:(m + 1) * 128, :],
                                   "p2ld1")
                        b = m % 2
                        P.wait("scalar", "p2ld0", a1)
                        P.wait("vector", "p2ld1", g1)
                        if m >= 2:
                            P.wait("scalar", "p2z", b_z + m - 1)
                            P.wait("vector", "p2tv", b_tv + (m - 1) * 4)
                        P.op("scalar", lambda e, b=b, abuf=abuf: e.activation(
                            sqt[b][:], abuf[:], AF.Square), sem="p2ss")
                        P.wait("vector", "p2ss")
                        P.op("vector", lambda e, b=b: e.reduce_sum(
                            tmp[0][:], sqt[b][:], axis=AX.X), sem="p2vs")
                        P.wait("scalar", "p2vs")
                        P.op("scalar", lambda e: e.activation(
                            tmp[2][:], tmp[0][:], AF.Sqrt,
                            scale=cinvd[:, 0:1], bias=ceps[:, 0:1]), sem="p2ss")
                        P.wait("vector", "p2ss")
                        P.op("vector", lambda e: e.reciprocal(tmp[3][:], tmp[2][:]))
                        P.op("vector", lambda e, b=b, abuf=abuf: e.tensor_scalar(
                            zt[b][:], abuf[:], tmp[3][:, 0:1], None, ALU.mult))
                        av = P.op("vector", lambda e, b=b: e.tensor_tensor(
                            zt[b][:], zt[b][:], awb[:], ALU.mult), sem="p2y")
                        at.gate(ja, "p2y", av)
                        gv = P.op("vector", lambda e, b=b, gbuf=gbuf: e.tensor_tensor(
                            zt[b][:], zt[b][:], gbuf[:], ALU.mult), sem="p2z")
                        gt.gate(jg, "p2z", gv)
                        P.wait("tensor", "p2z", gv)
                        for g in range(4):
                            gi = m * 4 + g
                            bg = gi % 2
                            if gi >= 2:
                                P.wait("tensor", "p2tv", b_tv + gi - 1)
                            for t4 in range(4):
                                ft = g * 4 + t4
                                P.op("tensor", lambda e, b=b, bg=bg, ft=ft, t4=t4:
                                     e.transpose(tpp[bg][:, t4 * 128:(t4 + 1) * 128],
                                                 zt[b][:, ft * 128:(ft + 1) * 128],
                                                 ident[:]),
                                     sem=("p2tt" if t4 == 3 else None))
                            P.wait("vector", "p2tt")
                            P.op("vector", lambda e, bg=bg, g=g, m=m: e.tensor_copy(
                                zT_r[:, g * 4:(g + 1) * 4, m * 128:(m + 1) * 128],
                                tpp[bg][:].rearrange("p (f t) -> p f t", f=4)), sem="p2tv")
                    P.barrier()
                with contextlib.ExitStack() as ar:
                    wf = Rot([sb(f"p4wf{i}", [128, DM], st=ar) for i in range(2)])
                    Wt = sb("p4Wt", [128, 16 * 256], st=ar)
                    stg = Rot([sb(f"p4sg{i}", [128, 256], st=ar) for i in range(2)])
                    xr = Rot([sb(f"p4xr{i}", [128, 256], st=ar) for i in range(2)])
                    tpp = [pst(f"p4tq{i}", [128, 512], ar) for i in range(2)]
                    mmp = [pst(f"p4mm{i}", [128, 256], ar) for i in range(2)]
                    Wt_r = Wt[:].rearrange("p (k r) -> p k r", k=16)
                    tpi = 0
                    mmi = 0
                    for c in range(8):
                        if c > 0:
                            P.wait("vector", "p2mt", b_mt + mmi)
                        wv = 0
                        for s4 in range(2):
                            j, wbuf = wf.next(P, "sync")
                            lv = P.dma("sync", wbuf[:],
                                       outw_in[c * 256 + s4 * 128:
                                               c * 256 + (s4 + 1) * 128, :], f"p2lw{j}")
                            P.wait("tensor", f"p2lw{j}", lv)
                            for g in range(4):
                                bg = tpi % 2
                                if tpi >= 2:
                                    P.wait("tensor", "p2wv", b_wv + tpi - 1)
                                for t4 in range(4):
                                    kt = g * 4 + t4
                                    P.op("tensor",
                                         lambda e, wbuf=wbuf, bg=bg, kt=kt, t4=t4:
                                         e.transpose(tpp[bg][:, t4 * 128:(t4 + 1) * 128],
                                                     wbuf[:, kt * 128:(kt + 1) * 128],
                                                     ident[:]),
                                         sem=("p2wt" if t4 == 3 else None))
                                P.wait("vector", "p2wt")
                                wv = P.op("vector", lambda e, bg=bg, g=g, s4=s4:
                                          e.tensor_copy(
                                              Wt_r[:, g * 4:(g + 1) * 4,
                                                   s4 * 128:(s4 + 1) * 128],
                                              tpp[bg][:].rearrange("p (f t) -> p f t", f=4)),
                                          sem="p2wv")
                                tpi += 1
                            wf.gate(j, "p2wv", wv)
                        P.wait("tensor", "p2wv", wv)
                        for m in range(NMT):
                            bm = mmi % 2
                            if mmi >= 2:
                                P.wait("tensor", "p2pf", b_pf + mmi - 1)
                            tvv = 0
                            for kt in range(16):
                                tvv = P.op("tensor", lambda e, bm=bm, m=m, kt=kt: e.matmul(
                                    mmp[bm][:],
                                    zT[:, kt * TOK + m * 128: kt * TOK + m * 128 + 128],
                                    Wt[:, kt * 256:(kt + 1) * 256],
                                    start=(kt == 0), stop=(kt == 15)),
                                    sem=("p2mt" if kt == 15 else None))
                            mmi += 1
                            jx, xbuf = xr.next(P, "sync")
                            lv = P.dma("sync", xbuf[:],
                                       x1_scr[m * 128:(m + 1) * 128,
                                              c * 256:(c + 1) * 256], f"p2ld{jx}")
                            js, sbuf_ = stg.next(P, "vector")
                            P.wait("vector", "p2mt", tvv)
                            P.wait("vector", f"p2ld{jx}", lv)
                            xv = P.op("vector", lambda e, bm=bm, xbuf=xbuf, sbuf_=sbuf_:
                                      e.tensor_tensor(sbuf_[:], mmp[bm][:], xbuf[:],
                                                      ALU.add), sem="p2pf")
                            xr.gate(jx, "p2pf", xv)
                            P.wait("scalar", "p2pf", xv)
                            sv = P.dma("scalar",
                                       x2_scr[m * 128:(m + 1) * 128,
                                              c * 256:(c + 1) * 256], sbuf_[:],
                                       f"p2st{js}")
                            stg.gate(js, f"p2st{js}", sv)
                    P.drain("sync", "p2st0", "p2st1")
                    P.barrier()

        phase0()
        if max_phase >= 1:
            phase_ffn(0, x_in, x1_scr)
        if max_phase >= 2:
            phase2()
        if max_phase >= 3:
            phase3a()
            phase3b()
        if max_phase >= 4:
            phase4()
        if max_phase >= 5:
            phase_ffn(1, x2_scr, y_out)
        else:
            for m in range(NMT):
                P.dma("gpsimd", y_out[m * 128:(m + 1) * 128, :],
                      x1_scr[m * 128:(m + 1) * 128, :], "fcp")
            P.wait("gpsimd", "fcp")
        for t, ap in tap_outs.items():
            if ap is None:
                continue
            P.dma("gpsimd", ap, tap_src[t], "dbg")
        if P.val("dbg") if "dbg" in P.cnt else 0:
            P.wait("gpsimd", "dbg")
        P.barrier()

        block = big.enter_context(nc.Block())
        P.finalize(block)
    return nc


def _stage_inputs(inputs):
    x = np.ascontiguousarray(np.asarray(inputs["L0_attn_hidden_states"],
                                        dtype=np.float32).reshape(B * S, DM))
    qkv_w = np.asarray(inputs["L0_attn_qkv_w"], np.float32)
    out_w = np.ascontiguousarray(np.asarray(inputs["L0_attn_out_w"], np.float32))
    gate_w = np.asarray(inputs["L0_attn_gate_w"], np.float32)
    norm_w = np.asarray(inputs["L0_attn_norm_w"], np.float32)
    w1 = np.ascontiguousarray(np.asarray(inputs["L0_ffn_w_gate_up"], np.float32))
    w2 = np.ascontiguousarray(np.asarray(inputs["L0_ffn_w_down"], np.float32))
    fn = np.asarray(inputs["L0_ffn_norm_weight"], np.float32)

    wat = np.ascontiguousarray(np.concatenate([qkv_w[_perm_qkv()], gate_w], axis=0))
    ffnw_b = np.ascontiguousarray(np.broadcast_to(fn[None, :], (128, DM)))
    attnw_b = np.ascontiguousarray(np.broadcast_to(norm_w[None, :], (128, DM)))
    ident = np.eye(128, dtype=np.float32)

    in_maps = []
    for c in range(NCORES):
        in_maps.append({
            "x_in": np.ascontiguousarray(x[c * TOK:(c + 1) * TOK]),
            "w_attn": wat,
            "outw_in": out_w,
            "w1_in": w1,
            "w2_in": w2,
            "w1_sl": np.ascontiguousarray(w1[c * (F2 // 8):(c + 1) * (F2 // 8)]),
            "w2_sl": np.ascontiguousarray(w2[c * (DM // 8):(c + 1) * (DM // 8)]),
            "ffnw_b": ffnw_b,
            "attnw_b": attnw_b,
            "ident_in": ident,
            "dd0_in": _DD0,
            "qdb_in": _QDB,
            "kdc_in": _KDC,
            "tpred_in": _t_pred(c),
        })
    return in_maps


def kernel(**inputs):
    nc = build_nc()
    in_maps = _stage_inputs(inputs)
    res = run_bass_kernel_spmd(nc, in_maps, core_ids=list(range(NCORES)))
    out = np.concatenate([np.asarray(res.results[c]["y_out"], dtype=np.float32)
                          for c in range(NCORES)], axis=0)
    return out.reshape(B, S, DM)


# version 08
# speedup vs baseline: 1.0508x; 1.0469x over previous
import contextlib
import numpy as np
import ml_dtypes

import concourse.bass as bass
import concourse.mybir as mybir
from concourse.bass_utils import run_bass_kernel_spmd

F32 = mybir.dt.float32
BF16 = mybir.dt.bfloat16
I32 = mybir.dt.int32
AF = mybir.ActivationFunctionType
ALU = mybir.AluOpType
AX = mybir.AxisListType

B, S, DM = 2, 8192, 2048
H, D, BS = 32, 64, 256
FF = 5632
F2 = 2 * FF
NCORES = 8
TOK = (B * S) // NCORES
NBLK = TOK // BS
NMT = TOK // 128
EPS = 1e-5
_ACT_EVAC = True
_INTERLEAVE_Q = True
_DUMP_B = False
N1 = F2 * DM
N2 = DM * FF

ENGS = ("sync", "vector", "scalar", "tensor", "gpsimd")


def _decay():
    h = np.arange(H, dtype=np.float64)
    base = 1.0 / 2.0 ** (8.0 / H)
    slope = (base ** (h + 1.0)) * (1.0 - 0.0 / (32 - 1 + 1e-05) + 1e-05)
    r = np.arange(BS, dtype=np.float64) + 1.0
    qd = np.exp(-slope[:, None] * r[None, :]).astype(np.float32)
    kd = np.exp(-slope[:, None] * (BS - r[None, :])).astype(np.float32)
    bd = np.exp(-slope * BS).astype(np.float32)
    i_ = np.arange(128, dtype=np.float64)
    dif = i_[None, :] - i_[:, None]
    DD0 = np.zeros((128, H * 256), np.float32)
    QDB = np.zeros((128, (H // 2) * 256), np.float32)
    KDC = np.zeros((128, H * 2), np.float32)
    for hh in range(H):
        t0 = np.where(dif >= 0, np.exp(-slope[hh] * dif), 0.0)
        t1 = np.exp(-slope[hh] * (128.0 + dif))
        DD0[:, hh * 256:hh * 256 + 128] = t0.astype(np.float32)
        DD0[:, hh * 256 + 128:hh * 256 + 256] = t1.astype(np.float32)
        p0 = 64 * (hh % 2)
        QDB[p0:p0 + 64, (hh // 2) * 256:(hh // 2) * 256 + 256] = qd[hh][None, :]
        KDC[:, hh * 2] = kd[hh, 0:128]
        KDC[:, hh * 2 + 1] = kd[hh, 128:256]
    return qd, kd, bd, DD0, QDB, KDC


_QD, _KD, _BD, _DD0, _QDB, _KDC = _decay()


def _t_pred(core):
    out = np.zeros((8, 64, H * D), np.float32)
    g, j = core // 4, core % 4
    for cp in range(8):
        if cp // 4 != g or cp % 4 >= j:
            continue
        w = _BD.astype(np.float64) ** ((j - 1 - (cp % 4)) * NBLK)
        for hh in range(H):
            out[cp, :, hh * D:(hh + 1) * D] = np.float32(w[hh])
    return out


def _perm_qkv():
    perm = np.zeros(3 * H * D, np.int64)
    for hh in range(H):
        for d in range(D):
            perm[0 * H * D + hh * D + d] = 192 * hh + d
            perm[1 * H * D + hh * D + d] = 192 * hh + 64 + d
            perm[2 * H * D + hh * D + d] = 192 * hh + 128 + d
    return perm


_CHAIN = ("vector", "scalar", "gpsimd")


class Prog:
    def __init__(self, nc, stack):
        self.nc, self.stack = nc, stack
        self.q = {e: [] for e in ENGS}
        self.h, self.cnt = {}, {}
        self.last = {e: None for e in ENGS}
        self.waited = {e: {} for e in ENGS}

    def sem(self, name):
        if name not in self.h:
            self.h[name] = self.stack.enter_context(self.nc.semaphore(name))
            self.cnt[name] = 0
        return self.h[name]

    def emit(self, eng, fn):
        self.q[eng].append(fn)

    def _wait_raw(self, eng, sem, v):
        if v <= 0:
            return
        if self.waited[eng].get(sem, 0) >= v:
            return
        self.waited[eng][sem] = v
        hh = self.sem(sem)
        self.emit(eng, lambda e, hh=hh, v=v: e.wait_ge(hh, v))

    def _chain_wait(self, eng):
        if self.last[eng] is not None:
            s, v = self.last[eng]
            self._wait_raw(eng, s, v)

    def dma(self, eng, out, in_, sem):
        self._chain_wait(eng)
        hh = self.sem(sem)
        self.cnt[sem] += 16
        self.emit(eng, lambda e, out=out, in_=in_, hh=hh:
                  e.dma_start(out=out, in_=in_).then_inc(hh, 16))
        return self.cnt[sem]

    def dmaT(self, eng, out, in_, sem):
        self._chain_wait(eng)
        hh = self.sem(sem)
        self.cnt[sem] += 16
        self.emit(eng, lambda e, out=out, in_=in_, hh=hh:
                  e.dma_start_transpose(out=out, in_=in_).then_inc(hh, 16))
        return self.cnt[sem]

    def wait(self, eng, sem, val=None):
        self.sem(sem)
        v = self.cnt[sem] if val is None else val
        self._wait_raw(eng, sem, v)

    def op(self, eng, fn, sem=None, k=1):
        if eng in _CHAIN:
            self._chain_wait(eng)
            if sem is None:
                sem = "_ch_" + eng
                k = 1
        if sem is not None:
            hh = self.sem(sem)
            self.cnt[sem] += k
            self.emit(eng, lambda e, fn=fn, hh=hh, k=k: fn(e).then_inc(hh, k))
            if eng in _CHAIN or eng == "tensor":
                self.last[eng] = (sem, self.cnt[sem])
            return self.cnt[sem]
        self.emit(eng, fn)
        return None

    def val(self, sem):
        self.sem(sem)
        return self.cnt[sem]

    def drain(self, eng, *sems):
        for s in sems:
            self.wait(eng, s)

    def barrier(self):
        hh = self.sem("phase")
        for eng in ENGS:
            if eng in _CHAIN:
                self._chain_wait(eng)
            elif self.last.get(eng) is not None:
                s, v = self.last[eng]
                self._wait_raw(eng, s, v)
            self.emit(eng, lambda e, hh=hh: e.nop().then_inc(hh, 1))
        self.cnt["phase"] += len(ENGS)
        v = self.cnt["phase"]
        for eng in ENGS:
            self._wait_raw(eng, "phase", v)

    def finalize(self, block):
        for eng in ENGS:
            fns = self.q[eng]

            def body(e, fns=fns):
                for fn in fns:
                    fn(e)
            getattr(block, eng)(body)


class Rot:

    def __init__(self, bufs):
        self.bufs = bufs
        self.gates = [None] * len(bufs)
        self.i = -1

    def next(self, P, producer_eng):
        self.i += 1
        j = self.i % len(self.bufs)
        if self.gates[j] is not None:
            s, v = self.gates[j]
            P.wait(producer_eng, s, v)
        return j, self.bufs[j]

    def gate(self, j, sem, val):
        self.gates[j] = (sem, val)


def build_nc(max_phase=99, taps=(), n_rg=NCORES):
    nc = bass.Bass("TRN2", target_bir_lowering=False)

    x_in = nc.dram_tensor("x_in", [TOK, DM], F32, kind="ExternalInput").ap()
    w_attn = nc.dram_tensor("w_attn", [4 * DM, DM], F32, kind="ExternalInput").ap()
    outw_in = nc.dram_tensor("outw_in", [DM, DM], F32, kind="ExternalInput").ap()
    w1_in = nc.dram_tensor("w1_in", [F2, DM], F32, kind="ExternalInput").ap()
    w2_in = nc.dram_tensor("w2_in", [DM, FF], F32, kind="ExternalInput").ap()
    w1_sl = nc.dram_tensor("w1_sl", [F2 // 8, DM], F32, kind="ExternalInput").ap()
    w2_sl = nc.dram_tensor("w2_sl", [DM // 8, FF], F32, kind="ExternalInput").ap()
    ffnw_b = nc.dram_tensor("ffnw_b", [128, DM], F32, kind="ExternalInput").ap()
    attnw_b = nc.dram_tensor("attnw_b", [128, DM], F32, kind="ExternalInput").ap()
    ident_in = nc.dram_tensor("ident_in", [128, 128], F32, kind="ExternalInput").ap()
    dd0_in = nc.dram_tensor("dd0_in", [128, H * 256], F32, kind="ExternalInput").ap()
    qdb_in = nc.dram_tensor("qdb_in", [128, (H // 2) * 256], F32, kind="ExternalInput").ap()
    kdc_in = nc.dram_tensor("kdc_in", [128, H * 2], F32, kind="ExternalInput").ap()
    tpred_in = nc.dram_tensor("tpred_in", [8, 64, H * D], F32, kind="ExternalInput").ap()
    y_out = nc.dram_tensor("y_out", [TOK, DM], F32, kind="ExternalOutput").ap()

    x1_scr = nc.dram_tensor("x1_scr", [TOK, DM], F32).ap()
    x2_scr = nc.dram_tensor("x2_scr", [TOK, DM], F32).ap()
    xi_scr = nc.dram_tensor("xi_scr", [TOK, DM], BF16).ap()
    w1q_scr = nc.dram_tensor("w1q_scr", [F2, DM], BF16).ap()
    w2q_scr = nc.dram_tensor("w2q_scr", [DM, FF], BF16).ap()
    hid_scr = nc.dram_tensor("hid_scr", [TOK, FF], F32).ap()
    hq_scr = nc.dram_tensor("hq_scr", [TOK, FF], BF16).ap()
    qkv_scr = nc.dram_tensor("qkv_scr", [TOK, 3 * DM], F32).ap()
    gate_scr = nc.dram_tensor("gate_scr", [TOK, DM], F32).ap()
    qT_scr = nc.dram_tensor("qT_scr", [DM, TOK], F32).ap()
    kT_scr = nc.dram_tensor("kT_scr", [DM, TOK], F32).ap()
    attn_scr = nc.dram_tensor("attn_scr", [TOK, DM], F32).ap()
    cc_sin = nc.dram_tensor("cc_sin", [64, H * D], F32).ap()
    cc_sout = nc.dram_tensor("cc_sout", [8 * 64, H * D], F32).ap()
    cc_min = nc.dram_tensor("cc_min", [1, 2], F32).ap()
    cc_mout = nc.dram_tensor("cc_mout", [1, 2], F32).ap()

    tap_src = {"x1": x1_scr, "x2": x2_scr, "qkv": qkv_scr, "gate": gate_scr,
               "qT": qT_scr, "kT": kT_scr, "attn": attn_scr, "sgath": cc_sout,
               "hid": hid_scr, "xi": xi_scr, "hq": hq_scr,
               "w1q": w1q_scr, "w2q": w2q_scr}
    tap_dt = {"xi": BF16, "hq": BF16, "w1q": BF16, "w2q": BF16}
    tap_outs = {}
    _STRIP_TAP = None
    _STRIP_A = "strips_a" in taps
    taps = [t for t in taps if t != "strips_a"]
    if "strips" in taps or _STRIP_A:
        taps = [t for t in taps if t != "strips"]
        _STRIP_TAP = nc.dram_tensor("dbg_strips", [512, NMT], F32,
                                    kind="ExternalOutput").ap()
        tap_outs["strips"] = None
    for t in taps:
        src = tap_src[t]
        tap_outs[t] = nc.dram_tensor("dbg_" + t, list(src.tensor.shape),
                                     tap_dt.get(t, F32), kind="ExternalOutput").ap()

    with contextlib.ExitStack() as big:
        P = Prog(nc, big)

        def sb(name, shape, dt=F32, st=None):
            return (st or big).enter_context(nc.sbuf_tensor(name, shape, dt))

        def pst(name, shape, st, dt=F32):
            return st.enter_context(nc.psum_tensor(name, shape, dt))

        ident = sb("ident", [128, 128])
        c384 = sb("c384", [128, 1])
        cm384 = sb("cm384", [128, 1])
        ceps = sb("ceps", [128, 1])
        cinvd = sb("cinvd", [128, 1])
        maski = sb("maski", [128, 1], I32)
        ones_c = sb("ones_c", [128, 1])
        ones_r = sb("ones_r", [1, 128])
        sw1s = sb("sw1s", [128, 1])
        nsw1s = sb("nsw1s", [128, 1])
        w1m = sb("w1m", [128, 1])
        sw2s = sb("sw2s", [128, 1])
        w2m = sb("w2m", [128, 1])
        kdc = sb("kdc", [128, H * 2])
        sinv_s = [[sb(f"sinv_s{a}_{m}", [128, 1]) for m in range(NMT)] for a in range(2)]
        amax_s = [[sb(f"amax_s{a}_{m}", [128, 1]) for m in range(NMT)] for a in range(2)]
        m2q_s = [[sb(f"m2q_s{a}_{m}", [128, 1]) for m in range(NMT)] for a in range(2)]
        fin_s = [[sb(f"fin_s{a}_{m}", [128, 1]) for m in range(NMT)] for a in range(2)]
        Sin_st = sb("Sin_st", [64, H * D])

        def phase0():
            for f in (lambda e: e.memset(c384[:], 384.5),
                      lambda e: e.memset(cm384[:], -384.0),
                      lambda e: e.memset(ceps[:], EPS),
                      lambda e: e.memset(cinvd[:], 1.0 / DM),
                      lambda e: e.memset(maski[:], -32768),
                      lambda e: e.memset(ones_c[:], 1.0),
                      lambda e: e.memset(ones_r[:], 1.0)):
                P.op("vector", f)
            P.dma("sync", ident[:], ident_in, "ld0")
            P.dma("sync", kdc[:], kdc_in, "ld0")
            with contextlib.ExitStack() as ar:
                wt = Rot([sb(f"p0w{i}", [128, FF], st=ar) for i in range(2)])
                acc = sb("p0acc", [128, 2], st=ar)
                acc1 = sb("p0acc1", [128, 1], st=ar)
                acc2 = sb("p0acc2", [128, 1], st=ar)
                red = sb("p0red", [128, 1], st=ar)
                sc2 = sb("p0sc", [1, 2], st=ar)
                pr = pst("p0ps", [1, 2], ar)
                pb = pst("p0pb", [128, 2], ar)
                P.op("vector", lambda e: e.memset(acc1[:], 0.0))
                P.op("vector", lambda e: e.memset(acc2[:], 0.0))
                for accb, tiles, srcap, ncols in ((acc1, (F2 // 8) // 128, w1_sl, DM),
                                                  (acc2, (DM // 8) // 128, w2_sl, FF)):
                    for i in range(tiles):
                        j, buf = wt.next(P, "sync")
                        lv = P.dma("sync", buf[:, 0:ncols],
                                   srcap[i * 128:(i + 1) * 128, :], f"ld1_{j}")
                        P.wait("vector", f"ld1_{j}", lv)
                        P.op("vector", lambda e, buf=buf, ncols=ncols: e.reduce_sum(
                            red[:], buf[:, 0:ncols], axis=AX.X, apply_absolute_value=True))
                        vv = P.op("vector", lambda e, accb=accb: e.tensor_tensor(
                            accb[:], accb[:], red[:], ALU.add), sem="v0")
                        wt.gate(j, "v0", vv)
                P.op("vector", lambda e: e.tensor_scalar(
                    acc[:, 0:1], acc1[:], 1.0, None, ALU.mult))
                P.op("vector", lambda e: e.tensor_scalar(
                    acc[:, 1:2], acc2[:], 1.0, None, ALU.mult), sem="v0")
                P.wait("tensor", "v0")
                P.op("tensor", lambda e: e.matmul(pr[:], ones_c[:], acc[:],
                                                  start=True, stop=True), sem="t0")
                P.wait("scalar", "t0")
                P.op("scalar", lambda e: e.copy(sc2[:], pr[:]), sem="s0")
                P.wait("scalar", "s0")
                P.dma("scalar", cc_min, sc2[:], "st0")
                P.wait("gpsimd", "st0")
                P.op("gpsimd", lambda e: e.collective_compute(
                    "AllReduce", ALU.add, replica_groups=[list(range(n_rg))],
                    ins=[cc_min.opt()], outs=[cc_mout.opt()]), sem="cc0")
                P.wait("sync", "cc0")
                P.dma("sync", sc2[:], cc_mout, "ld2")
                P.wait("vector", "ld2")
                P.op("vector", lambda e: e.tensor_scalar(
                    sc2[:, 0:1], sc2[:, 0:1], 1.0 / N1, 1e-5, ALU.mult, ALU.max))
                P.op("vector", lambda e: e.tensor_scalar(
                    sc2[:, 1:2], sc2[:, 1:2], 1.0 / N2, 1e-5, ALU.mult, ALU.max), sem="v0")
                P.wait("tensor", "v0")
                P.op("tensor", lambda e: e.matmul(pb[:], ones_r[:], sc2[:],
                                                  start=True, stop=True), sem="t0")
                P.wait("vector", "t0")
                P.op("vector", lambda e: e.tensor_copy(w1m[:], pb[:, 0:1]))
                P.op("vector", lambda e: e.tensor_copy(w2m[:], pb[:, 1:2]))
                P.op("vector", lambda e: e.reciprocal(sw1s[:], w1m[:]))
                P.op("vector", lambda e: e.reciprocal(sw2s[:], w2m[:]))
                P.op("vector", lambda e: e.tensor_scalar(
                    nsw1s[:], sw1s[:], -1.0, None, ALU.mult))
                P.wait("vector", "ld0")
                P.barrier()

        def emit_ln(xt_ap, yt_ap, sqt_ap, tmp, vs, ss):
            rs, mn, sd, iv = tmp
            P.op("vector", lambda e: e.reduce_sum(rs[:], xt_ap, axis=AX.X))
            P.op("vector", lambda e: e.tensor_scalar(
                mn[:], rs[:], 1.0 / DM, None, ALU.mult))
            P.op("vector", lambda e: e.tensor_scalar(
                yt_ap, xt_ap, mn[:, 0:1], None, ALU.subtract), sem=vs)
            P.wait("scalar", vs)
            P.op("scalar", lambda e: e.activation(sqt_ap, yt_ap, AF.Square), sem=ss)
            P.wait("vector", ss)
            P.op("vector", lambda e: e.reduce_sum(rs[:], sqt_ap, axis=AX.X), sem=vs)
            P.wait("scalar", vs)
            P.op("scalar", lambda e: e.activation(
                sd[:], rs[:], AF.Sqrt, scale=cinvd[:, 0:1], bias=ceps[:, 0:1]), sem=ss)
            P.wait("vector", ss)
            P.op("vector", lambda e: e.reciprocal(iv[:], sd[:]))
            P.op("vector", lambda e: e.tensor_scalar(
                yt_ap, yt_ap, iv[:, 0:1], None, ALU.mult), sem=vs)

        def phase_ffn(app, src, dst):
            tag = f"f{app}"
            ld, lw, tp, st, vs, ss, ts_, pf = [
                "f" + s_ for s_ in
                ("ld", "lw", "tp", "st", "vs", "ss", "ts", "pf")]
            sinv, amx, m2q, fin = sinv_s[app], amax_s[app], m2q_s[app], fin_s[app]

            with contextlib.ExitStack() as ar:
                xt = Rot([sb(f"{tag}xt{i}", [128, DM], st=ar) for i in range(2)])
                yt = [sb(f"{tag}yt{i}", [128, DM], st=ar) for i in range(2)]
                sqt = [sb(f"{tag}sq{i}", [128, DM], st=ar) for i in range(2)]
                xib = Rot([sb(f"{tag}xib{i}", [128, DM], BF16, st=ar) for i in range(2)])
                tmp = [sb(f"{tag}tmp{i}", [128, 1], st=ar) for i in range(4)]
                am = sb(f"{tag}am", [128, 1], st=ar)
                sc = sb(f"{tag}sc", [128, 1], st=ar)
                fwb = sb(f"{tag}fwb", [128, DM], st=ar)
                P.dma("sync", fwb[:], ffnw_b, ld + "f")
                ssx_base = P.val(ss + "x")
                P.wait("vector", ld + "f")
                for m in range(NMT):
                    j, xbuf = xt.next(P, "sync")
                    lv = P.dma("sync", xbuf[:], src[m * 128:(m + 1) * 128, :],
                               f"{ld}_{j}")
                    b = m % 2
                    P.wait("vector", f"{ld}_{j}", lv)
                    if m >= 2:
                        P.wait("vector", ss + "x", ssx_base + m - 1)
                    emit_ln(xbuf[:], yt[b][:], sqt[b][:], tmp, vs, ss)
                    P.wait("scalar", vs)
                    P.op("scalar", lambda e, b=b: e.activation(
                        sqt[b][:], yt[b][:], AF.Square), sem=ss)
                    P.wait("vector", ss)
                    P.op("vector", lambda e, b=b: e.reduce_sum(
                        tmp[0][:], sqt[b][:], axis=AX.X), sem=vs)
                    P.wait("scalar", vs)
                    P.op("scalar", lambda e: e.activation(
                        tmp[2][:], tmp[0][:], AF.Sqrt,
                        scale=cinvd[:, 0:1], bias=ceps[:, 0:1]), sem=ss)
                    P.wait("vector", ss)
                    P.op("vector", lambda e: e.reciprocal(tmp[3][:], tmp[2][:]))
                    P.op("vector", lambda e, b=b: e.tensor_scalar(
                        yt[b][:], yt[b][:], tmp[3][:, 0:1], None, ALU.mult))
                    P.op("vector", lambda e, b=b: e.tensor_tensor(
                        yt[b][:], yt[b][:], fwb[:], ALU.mult))
                    P.op("vector", lambda e, b=b: e.reduce_max(
                        am[:], yt[b][:], axis=AX.X, apply_absolute_value=True))
                    P.op("vector", lambda e: e.tensor_scalar(
                        am[:], am[:], 1e-5, None, ALU.max))
                    P.op("vector", lambda e: e.reciprocal(sc[:], am[:]))
                    P.op("vector", lambda e: e.tensor_scalar(
                        sc[:], sc[:], 127.0, None, ALU.mult))
                    P.op("vector", lambda e, m=m: e.tensor_scalar(
                        sinv[m][:], am[:], 1.0 / 127.0, None, ALU.mult))
                    P.op("vector", lambda e, b=b: e.tensor_scalar(
                        yt[b][:], yt[b][:], sc[:, 0:1], 384.5, ALU.mult, ALU.add))
                    xv = P.op("vector", lambda e, b=b: e.tensor_scalar(
                        yt[b][:].bitcast(I32), yt[b][:].bitcast(I32), maski[:, 0:1],
                        None, ALU.bitwise_and), sem=vs)
                    xt.gate(j, vs, xv)
                    jx, xbuf2 = xib.next(P, "scalar")
                    P.wait("scalar", vs, xv)
                    P.op("scalar", lambda e, b=b, xbuf2=xbuf2: e.activation(
                        xbuf2[:], yt[b][:], AF.Identity, bias=cm384[:, 0:1]), sem=ss + "x")
                    sv = P.dma("scalar", xi_scr[m * 128:(m + 1) * 128, :],
                               xbuf2[:], f"{st}_{jx}")
                    xib.gate(jx, f"{st}_{jx}", sv)
                if _STRIP_TAP is not None and app == 0 and _STRIP_A:
                    P.wait("sync", vs)
                    for m in range(NMT):
                        P.dma("sync", _STRIP_TAP[0:128, m:m + 1], sinv[m][:], "dbgs")
                P.drain("sync", f"{st}_0", f"{st}_1")
                P.barrier()

            with contextlib.ExitStack() as ar:
                xiT = sb(f"{tag}xiT", [128, NMT * TOK], BF16, st=ar)
                wf = Rot([sb(f"{tag}wf{i}", [128, DM], st=ar) for i in range(2)])
                wqa = sb(f"{tag}wqa", [128, DM], BF16, st=ar)
                wqb = sb(f"{tag}wqb", [128, DM], BF16, st=ar)
                wqo = Rot([sb(f"{tag}wqo{i}", [128, DM], BF16, st=ar) for i in range(2)])
                w1tg = [sb(f"{tag}w1tg{i}", [128, 16 * 512], BF16, st=ar) for i in range(2)]
                w1tu = [sb(f"{tag}w1tu{i}", [128, 16 * 512], BF16, st=ar) for i in range(2)]
                rel = [sb(f"{tag}rel{i}", [128, 512], st=ar) for i in range(2)]
                rl2 = [sb(f"{tag}rl2{i}", [128, 512], st=ar) for i in range(2)]
                hid = Rot([sb(f"{tag}hid{i}", [128, 512], st=ar) for i in range(2)])
                red1 = sb(f"{tag}red1", [128, 1], st=ar)
                t1 = sb(f"{tag}t1", [128, 1], st=ar)
                ht = Rot([sb(f"{tag}ht0", [128, FF // 2], st=ar)])
                hb = Rot([sb(f"{tag}hb0", [128, FF // 2], BF16, st=ar)])
                pg = [pst(f"{tag}pg{i}", [128, 512], ar) for i in range(2)]
                pu = [pst(f"{tag}pu{i}", [128, 512], ar) for i in range(2)]
                for m in range(NMT):
                    P.op("vector", lambda e, m=m: e.memset(amx[m][:], 0.0))
                for ft in range(NMT):
                    P.dmaT("sync", xiT[:, ft * TOK:(ft + 1) * TOK],
                           xi_scr[:, ft * 128:(ft + 1) * 128], tp + "x")
                nch = F2 // 1024
                mm_iter = 0
                pf_base = P.val(pf)
                ts_base = P.val(ts_)
                ssx2_base = P.val(ss + "x")
                hid_sv = [0] * NMT
                hq_sv = [0] * NMT

                pending_loads = []
                unit_queue = []

                def issue_load(kind, meta):
                    jw, wbuf = wf.next(P, "gpsimd")
                    if kind == "w1":
                        lv = P.dma("gpsimd", wbuf[:], w1_in[meta:meta + 128, :],
                                   f"{lw}_{jw}")
                    else:
                        mq, c0 = meta
                        cn = min(2048, FF - c0)
                        lv = P.dma("gpsimd", wbuf[:, 0:cn],
                                   w2_in[mq * 128:(mq + 1) * 128, c0:c0 + cn],
                                   f"{lw}_{jw}")
                    pending_loads.append((kind, meta, jw, lv, wbuf))

                def emit_quant_one():
                    kind, meta, jw, lv, wbuf = pending_loads.pop(0)
                    cn = DM if kind == "w1" else min(2048, FF - meta[1])
                    P.wait("vector", f"{lw}_{jw}", lv)
                    if kind == "w1":
                        P.op("vector", lambda e, wbuf=wbuf: e.tensor_scalar(
                            wqa[:], wbuf[:], sw1s[:, 0:1], 0.5, ALU.mult,
                            ALU.is_gt))
                        vv = P.op("vector", lambda e, wbuf=wbuf: e.tensor_scalar(
                            wqb[:], wbuf[:], nsw1s[:, 0:1], 0.5, ALU.mult,
                            ALU.is_gt), sem=vs + "w")
                    else:
                        P.op("vector", lambda e, wbuf=wbuf, cn=cn: e.tensor_scalar(
                            wqa[:, 0:cn], wbuf[:, 0:cn], sw2s[:, 0:1], 0.5,
                            ALU.mult, ALU.is_gt))
                        vv = P.op("vector", lambda e, wbuf=wbuf, cn=cn: e.tensor_scalar(
                            wqb[:, 0:cn], wbuf[:, 0:cn], sw2s[:, 0:1], -0.5,
                            ALU.mult, ALU.is_lt), sem=vs + "w")
                    wf.gate(jw, vs + "w", vv)
                    jq, qbuf = wqo.next(P, "vector")
                    qv = P.op("vector", lambda e, qbuf=qbuf, cn=cn: e.tensor_tensor(
                        qbuf[:, 0:cn], wqa[:, 0:cn], wqb[:, 0:cn], ALU.subtract),
                        sem=vs + "q")
                    P.wait("gpsimd", vs + "q", qv)
                    if kind == "w1":
                        sv = P.dma("gpsimd", w1q_scr[meta:meta + 128, :], qbuf[:],
                                   f"{st}w{jq}")
                    else:
                        mq, c0 = meta
                        sv = P.dma("gpsimd",
                                   w2q_scr[mq * 128:(mq + 1) * 128, c0:c0 + cn],
                                   qbuf[:, 0:cn], f"{st}w{jq}")
                    wqo.gate(jq, f"{st}w{jq}", sv)
                    if unit_queue:
                        issue_load(*unit_queue.pop(0))

                def push_units(us):
                    unit_queue.extend(us)
                    while len(pending_loads) < 2 and unit_queue:
                        issue_load(*unit_queue.pop(0))

                if app == 0:
                    w1_units = [("w1", rbase + s4 * 128) for n in range(nch)
                                for rbase in (n * 512, FF + n * 512)
                                for s4 in range(4)]
                    w2_units = [("w2", (mq, c0)) for mq in range(NMT)
                                for c0 in (0, 2048, 4096)]
                    upfront = w1_units[0:16]
                    iter_units = []
                    for n in range(nch):
                        if n < 9:
                            w2b = (w2_units[n * 5:(n + 1) * 5] if n < 8
                                   else w2_units[40:48])
                            iter_units.append(
                                w1_units[16 + n * 8:24 + n * 8] + w2b)
                        else:
                            iter_units.append([])
                else:
                    upfront = []
                    iter_units = [[] for _ in range(nch)]

                def dmaT_batch(k):
                    bn = k % 2
                    if app == 0:
                        P.wait("sync", f"{st}w0")
                        P.wait("sync", f"{st}w1")
                    P.wait("sync", ts_, ts_base + 16 * (k - 1))
                    for ft in range(16):
                        P.dmaT("sync", w1tg[bn][:, ft * 512:(ft + 1) * 512],
                               w1q_scr[k * 512:(k + 1) * 512, ft * 128:(ft + 1) * 128],
                               f"{tp}w{bn}")
                        P.dmaT("sync", w1tu[bn][:, ft * 512:(ft + 1) * 512],
                               w1q_scr[FF + k * 512:FF + (k + 1) * 512,
                                       ft * 128:(ft + 1) * 128], f"{tp}w{bn}")
                    return P.val(f"{tp}w{bn}")

                def emit_hidq(m):
                    P.op("vector", lambda e, m=m: e.tensor_scalar(
                        t1[:], sinv[m][:], w1m[:, 0:1], None, ALU.mult))
                    P.op("vector", lambda e, m=m: e.tensor_tensor(
                        m2q[m][:], t1[:], t1[:], ALU.mult))
                    P.op("vector", lambda e, m=m: e.tensor_tensor(
                        m2q[m][:], m2q[m][:], t1[:], ALU.mult))
                    P.op("vector", lambda e, m=m: e.tensor_tensor(
                        t1[:], amx[m][:], m2q[m][:], ALU.mult))
                    P.op("vector", lambda e: e.tensor_scalar(
                        t1[:], t1[:], 1e-5, None, ALU.max))
                    P.op("vector", lambda e, m=m: e.tensor_scalar(
                        fin[m][:], t1[:], 1.0 / 127.0, None, ALU.mult))
                    P.op("vector", lambda e, m=m: e.tensor_scalar(
                        fin[m][:], fin[m][:], w2m[:, 0:1], 0.5, ALU.mult, ALU.mult))
                    P.op("vector", lambda e: e.reciprocal(t1[:], t1[:]))
                    P.op("vector", lambda e: e.tensor_scalar(
                        t1[:], t1[:], 127.0, None, ALU.mult))
                    P.op("vector", lambda e, m=m: e.tensor_tensor(
                        m2q[m][:], m2q[m][:], t1[:], ALU.mult))
                    for hh in range(2):
                        c0 = hh * (FF // 2)
                        jh, hbuf = ht.next(P, "sync")
                        P.wait("sync", f"{st}h0")
                        P.wait("sync", f"{st}h1")
                        lv = P.dma("sync", hbuf[:],
                                   hid_scr[m * 128:(m + 1) * 128, c0:c0 + FF // 2],
                                   ld + "h")
                        P.wait("vector", ld + "h", lv)
                        P.op("vector", lambda e, hbuf=hbuf, m=m: e.tensor_scalar(
                            hbuf[:], hbuf[:], m2q[m][:, 0:1], 384.5, ALU.mult, ALU.add))
                        hv = P.op("vector", lambda e, hbuf=hbuf: e.tensor_scalar(
                            hbuf[:].bitcast(I32), hbuf[:].bitcast(I32), maski[:, 0:1],
                            None, ALU.bitwise_and), sem=vs)
                        jb, bbuf = hb.next(P, "scalar")
                        P.wait("scalar", vs, hv)
                        av = P.op("scalar", lambda e, hbuf=hbuf, bbuf=bbuf: e.activation(
                            bbuf[:], hbuf[:], AF.Identity, bias=cm384[:, 0:1]), sem=ss)
                        ht.gate(jh, ss, av)
                        sv = P.dma("scalar",
                                   hq_scr[m * 128:(m + 1) * 128, c0:c0 + FF // 2],
                                   bbuf[:], f"{st}q{jb}")
                        hb.gate(jb, f"{st}q{jb}", sv)
                        hq_sv[m] = sv

                tp_chunk = [0] * nch
                push_units(upfront)
                for _ in upfront:
                    emit_quant_one()
                tp_chunk[0] = dmaT_batch(0)
                if nch > 1:
                    tp_chunk[1] = dmaT_batch(1)
                for n in range(nch):
                    bn = n % 2
                    units = iter_units[n]
                    push_units(units)
                    if not _INTERLEAVE_Q:
                        for _ in units:
                            emit_quant_one()
                    P.wait("tensor", f"{tp}w{bn}", tp_chunk[n])
                    if n == 0:
                        P.wait("tensor", tp + "x")
                    for m in range(NMT):
                        bm = mm_iter % 2
                        if mm_iter >= 2:
                            P.wait("tensor", pf, pf_base + mm_iter - 1)
                        for kt in range(16):
                            P.op("tensor", lambda e, bn=bn, bm=bm, m=m, kt=kt: e.matmul(
                                pg[bm][:],
                                xiT[:, kt * TOK + m * 128: kt * TOK + m * 128 + 128],
                                w1tg[bn][:, kt * 512:(kt + 1) * 512],
                                start=(kt == 0), stop=(kt == 15)))
                        tvv = 0
                        for kt in range(16):
                            tvv = P.op("tensor", lambda e, bn=bn, bm=bm, m=m, kt=kt:
                                       e.matmul(
                                           pu[bm][:],
                                           xiT[:, kt * TOK + m * 128:
                                               kt * TOK + m * 128 + 128],
                                           w1tu[bn][:, kt * 512:(kt + 1) * 512],
                                           start=(kt == 0), stop=(kt == 15)),
                                       sem=(ts_ if kt == 15 else None))
                        if _ACT_EVAC:
                            if mm_iter >= 2:
                                P.wait("scalar", pf, pf_base + mm_iter - 1)
                            P.wait("scalar", ts_, tvv)
                            P.op("scalar", lambda e, bm=bm: e.activation(
                                rel[bm][:], pg[bm][:], AF.Relu))
                            P.op("scalar", lambda e, bm=bm: e.activation(
                                rl2[bm][:], rel[bm][:], AF.Square), sem=ss + "x")
                            P.wait("vector", ss + "x", ssx2_base + mm_iter + 1)
                        else:
                            P.wait("vector", ts_, tvv)
                            P.op("vector", lambda e, bm=bm: e.tensor_scalar(
                                rel[bm][:], pg[bm][:], 0.0, None, ALU.max))
                            P.op("vector", lambda e, bm=bm: e.tensor_tensor(
                                rl2[bm][:], rel[bm][:], rel[bm][:], ALU.mult))
                        jh, hbuf = hid.next(P, "vector")
                        P.op("vector", lambda e, bm=bm, hbuf=hbuf: e.tensor_tensor(
                            hbuf[:], rl2[bm][:], pu[bm][:], ALU.mult), sem=pf)
                        P.op("vector", lambda e, hbuf=hbuf: e.reduce_max(
                            red1[:], hbuf[:], axis=AX.X, apply_absolute_value=True))
                        hv = P.op("vector", lambda e, m=m: e.tensor_tensor(
                            amx[m][:], amx[m][:], red1[:], ALU.max), sem=vs + "r")
                        P.wait("scalar", vs + "r", hv)
                        sv = P.dma("scalar",
                                   hid_scr[m * 128:(m + 1) * 128,
                                           n * 512:(n + 1) * 512], hbuf[:],
                                   f"{st}h{jh}")
                        hid.gate(jh, f"{st}h{jh}", sv)
                        hid_sv[m] = sv
                        if _INTERLEAVE_Q and m < len(units):
                            emit_quant_one()
                        if n == nch - 1:
                            emit_hidq(m)
                        mm_iter += 1
                    if n + 2 < nch:
                        tp_chunk[n + 2] = dmaT_batch(n + 2)
                if _DUMP_B and app == 0:
                    dxi = nc.dram_tensor("dbg_xiT", [128, NMT * TOK], BF16,
                                         kind="ExternalOutput").ap()
                    dwt = nc.dram_tensor("dbg_w1t", [128, 2 * 16 * 512], BF16,
                                         kind="ExternalOutput").ap()
                    P.wait("sync", ts_, ts_base + 16 * nch)
                    P.dma("sync", dxi, xiT[:], "dbgB")
                    P.dma("sync", dwt[:, 0:8192], w1tg[(nch - 1) % 2][:], "dbgB")
                    P.dma("sync", dwt[:, 8192:16384], w1tu[(nch - 1) % 2][:], "dbgB")
                    P.wait("sync", "dbgB")
                P.drain("sync", f"{st}h0", f"{st}h1", f"{st}q0", f"{st}q1")
                if app == 0:
                    P.drain("sync", f"{st}w0", f"{st}w1")
                P.barrier()

            with contextlib.ExitStack() as ar:
                NKT = FF // 128
                w2t = [sb(f"{tag}w2t{i}", [128, NKT * 512], BF16, st=ar)
                       for i in range(2)]
                hqT = [sb(f"{tag}hqT{i}", [128, NKT * 512], BF16, st=ar) for i in range(2)]
                xres = Rot([sb(f"{tag}xr{i}", [128, 512], st=ar) for i in range(2)])
                stg = Rot([sb(f"{tag}sg{i}", [128, 512], st=ar) for i in range(2)])
                pd = [pst(f"{tag}pd{i}", [128, 512], ar) for i in range(4)]
                pfd_base = P.val(pf)
                tsd_base = P.val(ts_)

                def w2t_load(g):
                    qi, bw = g % 4, g % 2
                    P.wait("sync", ts_, tsd_base + g - 1)
                    for ft in range(NKT):
                        P.dmaT("sync", w2t[bw][:, ft * 512:(ft + 1) * 512],
                               w2q_scr[qi * 512:(qi + 1) * 512,
                                       ft * 128:(ft + 1) * 128], f"{tp}2{bw}")
                    return P.val(f"{tp}2{bw}")

                def hqT_part(mg, part):
                    bg = mg % 2
                    P.wait("sync", ts_, tsd_base + 4 * mg - 4)
                    for ft in range(part * 11, part * 11 + 11):
                        P.dmaT("sync", hqT[bg][:, ft * 512:(ft + 1) * 512],
                               hq_scr[mg * 512:(mg + 1) * 512,
                                      ft * 128:(ft + 1) * 128], f"{tp}h{bg}")
                    return P.val(f"{tp}h{bg}")

                hq_tpv = [0] * 4
                w2_val = [0] * 16
                for part in range(4):
                    hq_tpv[0] = hqT_part(0, part)
                w2_val[0] = w2t_load(0)
                w2_val[1] = w2t_load(1)
                gidx = 0
                for mg in range(4):
                    bg = mg % 2
                    for qi in range(4):
                        bw = gidx % 2
                        if mg + 1 < 4:
                            hq_tpv[mg + 1] = hqT_part(mg + 1, qi)
                        if gidx + 2 < 16:
                            w2_val[gidx + 2] = w2t_load(gidx + 2)
                        P.wait("tensor", f"{tp}h{bg}", hq_tpv[mg])
                        P.wait("tensor", f"{tp}2{bw}", w2_val[gidx])
                        if gidx > 0:
                            P.wait("tensor", pf, pfd_base + gidx * 4)
                        tvv = 0
                        for ft in range(NKT):
                            for mi in range(4):
                                last = (ft == NKT - 1 and mi == 3)
                                tvv = P.op("tensor",
                                           lambda e, bg=bg, ft=ft, mi=mi, bw=bw: e.matmul(
                                               pd[mi][:],
                                               hqT[bg][:, ft * 512 + mi * 128:
                                                        ft * 512 + mi * 128 + 128],
                                               w2t[bw][:, ft * 512:(ft + 1) * 512],
                                               start=(ft == 0), stop=(ft == NKT - 1)),
                                           sem=(ts_ if last else None))
                        for mi in range(4):
                            m = mg * 4 + mi
                            jx, xbuf = xres.next(P, "sync")
                            lv = P.dma("sync", xbuf[:],
                                       src[m * 128:(m + 1) * 128,
                                           qi * 512:(qi + 1) * 512], f"{ld}x{jx}")
                            js, sbuf_ = stg.next(P, "vector")
                            P.wait("vector", ts_, tvv)
                            P.wait("vector", f"{ld}x{jx}", lv)
                            xv = P.op("vector",
                                      lambda e, mi=mi, m=m, xbuf=xbuf, sbuf_=sbuf_:
                                      e.scalar_tensor_tensor(
                                          sbuf_[:], pd[mi][:], fin[m][:, 0:1], xbuf[:],
                                          ALU.mult, ALU.add), sem=pf)
                            xres.gate(jx, pf, P.val(pf))
                            P.wait("scalar", pf, P.val(pf))
                            sv = P.dma("scalar",
                                       dst[m * 128:(m + 1) * 128, qi * 512:(qi + 1) * 512],
                                       sbuf_[:], f"{st}d{js}")
                            stg.gate(js, f"{st}d{js}", sv)
                        gidx += 1
                P.drain("sync", f"{st}d0", f"{st}d1")
                P.barrier()

        def phase2():
            with contextlib.ExitStack() as outer:
                yT = sb("p2yT", [128, NMT * TOK], st=outer)
                with contextlib.ExitStack() as ar:
                    xt = Rot([sb(f"p2xt{i}", [128, DM], st=ar) for i in range(2)])
                    yt = [sb(f"p2yt{i}", [128, DM], st=ar) for i in range(2)]
                    sqt = [sb(f"p2sq{i}", [128, DM], st=ar) for i in range(2)]
                    tmp = [sb(f"p2tmp{i}", [128, 1], st=ar) for i in range(4)]
                    tpp = [pst(f"p2tp{i}", [128, 512], ar) for i in range(2)]
                    yT_r = yT[:].rearrange("p (k t) -> p k t", k=NMT)
                    for m in range(NMT):
                        j, xbuf = xt.next(P, "sync")
                        lv = P.dma("sync", xbuf[:], x1_scr[m * 128:(m + 1) * 128, :],
                                   f"p2ld{j}")
                        b = m % 2
                        P.wait("vector", f"p2ld{j}", lv)
                        if m >= 2:
                            P.wait("vector", "p2tv", (m - 1) * 4)
                        emit_ln(xbuf[:], yt[b][:], sqt[b][:], tmp, "p2vs", "p2ss")
                        yv = P.op("vector", lambda e: e.nop(), sem="p2y")
                        xt.gate(j, "p2y", yv)
                        P.wait("tensor", "p2y", yv)
                        for g in range(4):
                            gi = m * 4 + g
                            bg = gi % 2
                            if gi >= 2:
                                P.wait("tensor", "p2tv", gi - 1)
                            for t4 in range(4):
                                ft = g * 4 + t4
                                P.op("tensor", lambda e, b=b, bg=bg, ft=ft, t4=t4: e.transpose(
                                    tpp[bg][:, t4 * 128:(t4 + 1) * 128],
                                    yt[b][:, ft * 128:(ft + 1) * 128], ident[:]),
                                    sem=("p2tt" if t4 == 3 else None))
                            P.wait("vector", "p2tt")
                            P.op("vector", lambda e, bg=bg, g=g, m=m: e.tensor_copy(
                                yT_r[:, g * 4:(g + 1) * 4, m * 128:(m + 1) * 128],
                                tpp[bg][:].rearrange("p (f t) -> p f t", f=4)), sem="p2tv")
                    P.barrier()
                with contextlib.ExitStack() as ar:
                    wf = Rot([sb(f"p2wf{i}", [128, DM], st=ar) for i in range(2)])
                    Wt = sb("p2Wt", [128, 16 * 512], st=ar)
                    stg = Rot([sb(f"p2sg{i}", [128, 512], st=ar) for i in range(2)])
                    st2 = Rot([sb(f"p2s2{i}", [128, 512], st=ar) for i in range(2)])
                    tpp = [pst(f"p2tq{i}", [128, 512], ar) for i in range(2)]
                    mmp = [pst(f"p2mm{i}", [128, 512], ar) for i in range(2)]
                    Wt_r = Wt[:].rearrange("p (k r) -> p k r", k=16)
                    qT_r = qT_scr.rearrange("(b p) t -> p b t", p=128)
                    kT_r = kT_scr.rearrange("(b p) t -> p b t", p=128)
                    tpi = 0
                    mmi = 0
                    for c in range(16):
                        if c > 0:
                            P.wait("vector", "p2mt", mmi)
                        wv = 0
                        for s4 in range(4):
                            j, wbuf = wf.next(P, "sync")
                            lv = P.dma("sync", wbuf[:],
                                       w_attn[c * 512 + s4 * 128:
                                              c * 512 + (s4 + 1) * 128, :], f"p2lw{j}")
                            P.wait("tensor", f"p2lw{j}", lv)
                            for g in range(4):
                                bg = tpi % 2
                                if tpi >= 2:
                                    P.wait("tensor", "p2wv", tpi - 1)
                                for t4 in range(4):
                                    kt = g * 4 + t4
                                    P.op("tensor",
                                         lambda e, wbuf=wbuf, bg=bg, kt=kt, t4=t4:
                                         e.transpose(tpp[bg][:, t4 * 128:(t4 + 1) * 128],
                                                     wbuf[:, kt * 128:(kt + 1) * 128],
                                                     ident[:]),
                                         sem=("p2wt" if t4 == 3 else None))
                                P.wait("vector", "p2wt")
                                wv = P.op("vector", lambda e, bg=bg, g=g, s4=s4: e.tensor_copy(
                                    Wt_r[:, g * 4:(g + 1) * 4, s4 * 128:(s4 + 1) * 128],
                                    tpp[bg][:].rearrange("p (f t) -> p f t", f=4)),
                                    sem="p2wv")
                                tpi += 1
                            wf.gate(j, "p2wv", wv)
                        P.wait("tensor", "p2wv", wv)
                        for m in range(NMT):
                            bm = mmi % 2
                            if mmi >= 2:
                                P.wait("tensor", "p2pf", mmi - 1)
                            tvv = 0
                            for kt in range(16):
                                tvv = P.op("tensor", lambda e, bm=bm, m=m, kt=kt: e.matmul(
                                    mmp[bm][:],
                                    yT[:, kt * TOK + m * 128: kt * TOK + m * 128 + 128],
                                    Wt[:, kt * 512:(kt + 1) * 512],
                                    start=(kt == 0), stop=(kt == 15)),
                                    sem=("p2mt" if kt == 15 else None))
                            mmi += 1
                            js, sbuf_ = stg.next(P, "scalar")
                            P.wait("scalar", "p2mt", tvv)
                            func = AF.Silu if c < 12 else AF.Sigmoid
                            fv = P.op("scalar", lambda e, bm=bm, sbuf_=sbuf_, func=func:
                                      e.activation(sbuf_[:], mmp[bm][:], func), sem="p2pf")
                            if c < 12:
                                dstap = qkv_scr[m * 128:(m + 1) * 128, c * 512:(c + 1) * 512]
                            else:
                                dstap = gate_scr[m * 128:(m + 1) * 128,
                                                 (c - 12) * 512:(c - 11) * 512]
                            sv = P.dma("scalar", dstap, sbuf_[:], f"p2st{js}")
                            if c < 8:
                                P.wait("tensor", "p2pf", fv)
                                bg = tpi % 2
                                if tpi >= 2:
                                    P.wait("tensor", "p2wv", tpi - 1)
                                for t4 in range(4):
                                    P.op("tensor", lambda e, sbuf_=sbuf_, bg=bg, t4=t4:
                                         e.transpose(tpp[bg][:, t4 * 128:(t4 + 1) * 128],
                                                     sbuf_[:, t4 * 128:(t4 + 1) * 128],
                                                     ident[:]),
                                         sem=("p2wt" if t4 == 3 else None))
                                qtv = P.val("p2wt")
                                j2, s2buf = st2.next(P, "vector")
                                P.wait("vector", "p2wt", qtv)
                                wv2 = P.op("vector", lambda e, bg=bg, s2buf=s2buf:
                                           e.tensor_copy(s2buf[:], tpp[bg][:]), sem="p2wv")
                                tpi += 1
                                P.wait("sync", "p2wv", wv2)
                                scr_r = qT_r if c < 4 else kT_r
                                cc = c if c < 4 else c - 4
                                sv2 = P.dma("sync",
                                            scr_r[:, cc * 4:(cc + 1) * 4,
                                                  m * 128:(m + 1) * 128],
                                            s2buf[:].rearrange("p (f t) -> p f t", f=4),
                                            f"p2s2{j2}")
                                st2.gate(j2, f"p2s2{j2}", sv2)
                                stg.gate(js, f"p2st{js}", sv)
                            else:
                                stg.gate(js, f"p2st{js}", sv)
                    P.drain("sync", "p2st0", "p2st1", "p2s20", "p2s21")
                    P.barrier()

        def phase3a():
            kv_r = lambda i, c0, c1: qkv_scr[i * 256:(i + 1) * 256, c0:c1].rearrange(
                "(j p) c -> p j c", p=128)
            d3 = lambda ap, n: ap.rearrange("p (j c) -> p j c", j=n)
            with contextlib.ExitStack() as ar:
                kb = Rot([sb(f"p3ak{i}", [128, 2 * DM], st=ar) for i in range(2)])
                vb = Rot([sb(f"p3av{i}", [128, 2 * DM], st=ar) for i in range(2)])
                kk = Rot([sb(f"p3akk{i}", [128, 64], st=ar) for i in range(2)])
                Sloc = sb("p3aS", [64, H * D], st=ar)
                gs = Rot([sb(f"p3gs{i}", [64, H * D], st=ar) for i in range(2)])
                tpb = Rot([sb(f"p3tp{i}", [64, H * D], st=ar) for i in range(2)])
                cps = [pst(f"p3acp{i}", [64, 64], ar) for i in range(2)]
                P.op("vector", lambda e: e.memset(Sloc[:], 0.0))
                n_sv = 0
                for i in range(NBLK):
                    jk, kbuf = kb.next(P, "sync")
                    kv1 = P.dma("sync", d3(kbuf[:], 2), kv_r(i, DM, 2 * DM),
                                f"p3ak{jk}")
                    jv, vbuf = vb.next(P, "sync")
                    kv2 = P.dma("sync", d3(vbuf[:], 2), kv_r(i, 2 * DM, 3 * DM),
                                f"p3av{jv}")
                    P.wait("vector", f"p3ak{jk}", kv1)
                    P.wait("vector", f"p3av{jv}", kv2)
                    P.wait("tensor", f"p3av{jv}", kv2)
                    for hh in range(H):
                        bp = hh % 2
                        if n_sv >= 2:
                            P.wait("tensor", "p3sv", n_sv - 1)
                        tv = 0
                        for jh in range(2):
                            jkk, kkb = kk.next(P, "vector")
                            kv_ = P.op("vector",
                                       lambda e, kbuf=kbuf, hh=hh, jh=jh, kkb=kkb:
                                       e.tensor_scalar(
                                           kkb[:],
                                           kbuf[:, jh * DM + hh * 64: jh * DM + hh * 64 + 64],
                                           kdc[:, hh * 2 + jh: hh * 2 + jh + 1],
                                           None, ALU.mult), sem="p3kv")
                            P.wait("tensor", "p3kv", kv_)
                            tv = P.op("tensor",
                                      lambda e, bp=bp, kkb=kkb, vbuf=vbuf, hh=hh, jh=jh:
                                      e.matmul(cps[bp][:], kkb[:],
                                               vbuf[:, jh * DM + hh * 64:
                                                    jh * DM + hh * 64 + 64],
                                               start=(jh == 0), stop=(jh == 1)),
                                      sem=("p3tc" if jh == 1 else None))
                            kk.gate(jkk, "p3tc", P.val("p3tc") if jh == 1 else
                                    P.val("p3tc") + 1)
                        P.wait("vector", "p3tc", tv)
                        sv_ = P.op("vector", lambda e, bp=bp, hh=hh: e.scalar_tensor_tensor(
                            Sloc[:, hh * 64:(hh + 1) * 64], Sloc[:, hh * 64:(hh + 1) * 64],
                            float(_BD[hh]), cps[bp][:], ALU.mult, ALU.add), sem="p3sv")
                        n_sv += 1
                    kb.gate(jk, "p3kv", P.val("p3kv"))
                    vb.gate(jv, "p3tc", P.val("p3tc"))
                P.wait("scalar", "p3sv")
                sv = P.dma("scalar", cc_sin, Sloc[:], "p3st")
                P.wait("gpsimd", "p3st", sv)
                P.op("gpsimd", lambda e: e.collective_compute(
                    "AllGather", ALU.bypass, replica_groups=[list(range(n_rg))],
                    ins=[cc_sin.opt()],
                    outs=[(cc_sout[0:64 * n_rg, :] if n_rg != NCORES else cc_sout).opt()]),
                    sem="p3cc")
                P.op("vector", lambda e: e.memset(Sin_st[:], 0.0))
                P.wait("sync", "p3cc")
                for cp in range(n_rg):
                    jg, gbuf = gs.next(P, "sync")
                    gv1 = P.dma("sync", gbuf[:], cc_sout[cp * 64:(cp + 1) * 64, :],
                                f"p3ag{jg}")
                    jt, tbuf = tpb.next(P, "sync")
                    gv2 = P.dma("sync", tbuf[:], tpred_in[cp], f"p3at{jt}")
                    P.wait("vector", f"p3ag{jg}", gv1)
                    P.wait("vector", f"p3at{jt}", gv2)
                    P.op("vector", lambda e, gbuf=gbuf, tbuf=tbuf: e.tensor_tensor(
                        gbuf[:], gbuf[:], tbuf[:], ALU.mult))
                    gv = P.op("vector", lambda e, gbuf=gbuf: e.tensor_tensor(
                        Sin_st[:], Sin_st[:], gbuf[:], ALU.add), sem="p3gv")
                    gs.gate(jg, "p3gv", gv)
                    tpb.gate(jt, "p3gv", gv)
                P.barrier()

        def phase3b():
            kv_r = lambda i, c0, c1: qkv_scr[i * 256:(i + 1) * 256, c0:c1].rearrange(
                "(j p) c -> p j c", p=128)
            qT_sl = lambda i: qT_scr[:, i * 256:(i + 1) * 256].rearrange(
                "(t p) c -> p t c", p=128)
            kT_sl = lambda i: kT_scr[:, i * 256:(i + 1) * 256].rearrange(
                "(t p) c -> p t c", p=128)
            d3 = lambda ap, n: ap.rearrange("p (j c) -> p j c", j=n)
            with contextlib.ExitStack() as ar:
                dd0 = sb("p3dd0", [128, H * 256], st=ar)
                qdb = sb("p3qdb", [128, (H // 2) * 256], st=ar)
                kb = Rot([sb("p3k0", [128, 2 * DM], st=ar)])
                vb = Rot([sb("p3v0", [128, 2 * DM], st=ar)])
                qs = Rot([sb("p3q0", [128, NMT * 256], st=ar)])
                ks = Rot([sb("p3kT0", [128, NMT * 256], st=ar)])
                kk = Rot([sb(f"p3kk{i}", [128, 64], st=ar) for i in range(2)])
                qq = Rot([sb(f"p3qq{i}", [64, 256], st=ar) for i in range(2)])
                aw0 = Rot([sb(f"p3a0{i}", [128, 256], st=ar) for i in range(2)])
                aw1 = Rot([sb(f"p3a1{i}", [128, 128], st=ar) for i in range(2)])
                asm = Rot([sb(f"p3asm{i}", [128, 2 * DM], st=ar) for i in range(2)])
                Scur = [sb(f"p3S{i}", [64, H * D], st=ar) for i in range(2)]
                pa0 = [pst(f"p3pa0{i}", [128, 256], ar) for i in range(2)]
                pa1 = [pst(f"p3pa1{i}", [128, 128], ar) for i in range(2)]
                pi0 = pst("p3pi0", [128, 64], ar)
                pi1 = pst("p3pi1", [128, 64], ar)
                cps = [pst(f"p3cp{i}", [64, 64], ar) for i in range(2)]
                P.dma("sync", dd0[:], dd0_in, "p3c")
                P.dma("sync", qdb[:], qdb_in, "p3c")
                P.wait("vector", "p3c")
                P.wait("vector", "p3gv", P.val("p3gv"))
                P.op("vector", lambda e: e.tensor_copy(Scur[0][:], Sin_st[:]), sem="p3sv2")
                awi = 0
                n_sv = P.val("p3sv2")
                n_ie = 0
                sv_prev_block = P.val("p3sv2")
                for i in range(NBLK):
                    Sa, Sb = Scur[i % 2], Scur[(i + 1) % 2]
                    jk, kbuf = kb.next(P, "sync")
                    b1 = P.dma("sync", d3(kbuf[:], 2), kv_r(i, DM, 2 * DM), "p3bk")
                    jv, vbuf = vb.next(P, "sync")
                    b2 = P.dma("sync", d3(vbuf[:], 2), kv_r(i, 2 * DM, 3 * DM), "p3bv")
                    jq, qsb = qs.next(P, "sync")
                    b3 = P.dma("sync", d3(qsb[:], NMT), qT_sl(i), "p3bq")
                    jkt, ksb = ks.next(P, "sync")
                    b4 = P.dma("sync", d3(ksb[:], NMT), kT_sl(i), "p3bt")
                    for s_, v_ in (("p3bk", b1), ("p3bv", b2), ("p3bq", b3),
                                   ("p3bt", b4)):
                        P.wait("tensor", s_, v_)
                        P.wait("vector", s_, v_)
                    P.wait("tensor", "p3sv2", sv_prev_block)
                    ja, abuf = asm.next(P, "vector")
                    for hh in range(H):
                        p0 = 64 * (hh % 2)
                        hcol = (hh // 2) * 256
                        ba = awi % 2
                        if awi >= 2:
                            P.wait("tensor", "p3de", awi - 1)
                        P.op("tensor", lambda e, ba=ba, ksb=ksb, qsb=qsb, p0=p0, hcol=hcol:
                             e.matmul(pa0[ba][:], ksb[p0:p0 + 64, hcol:hcol + 128],
                                      qsb[p0:p0 + 64, hcol:hcol + 256],
                                      start=True, stop=True))
                        tv = P.op("tensor", lambda e, ba=ba, ksb=ksb, qsb=qsb, p0=p0, hcol=hcol:
                                  e.matmul(pa1[ba][:], ksb[p0:p0 + 64, hcol + 128:hcol + 256],
                                           qsb[p0:p0 + 64, hcol + 128:hcol + 256],
                                           start=True, stop=True), sem="p3aw")
                        ja0, a0b = aw0.next(P, "vector")
                        ja1, a1b = aw1.next(P, "vector")
                        P.wait("vector", "p3aw", tv)
                        P.op("vector", lambda e, ba=ba, a0b=a0b, hh=hh: e.tensor_tensor(
                            a0b[:], pa0[ba][:], dd0[:, hh * 256:(hh + 1) * 256], ALU.mult))
                        av = P.op("vector", lambda e, ba=ba, a1b=a1b, hh=hh: e.tensor_tensor(
                            a1b[:], pa1[ba][:], dd0[:, hh * 256:hh * 256 + 128], ALU.mult),
                            sem="p3de")
                        jqq, qqb = qq.next(P, "vector")
                        qv = P.op("vector",
                                  lambda e, qqb=qqb, qsb=qsb, qdb=qdb, p0=p0, hcol=hcol:
                                  e.tensor_tensor(qqb[:], qsb[p0:p0 + 64, hcol:hcol + 256],
                                                  qdb[p0:p0 + 64, hcol:hcol + 256],
                                                  ALU.mult), sem="p3qe")
                        P.wait("tensor", "p3de", av)
                        P.wait("tensor", "p3qe", qv)
                        if n_ie >= 1:
                            P.wait("tensor", "p3ie", n_ie)
                        P.op("tensor", lambda e, a0b=a0b, vbuf=vbuf, hh=hh: e.matmul(
                            pi0[:], a0b[:, 0:128], vbuf[:, hh * 64: hh * 64 + 64],
                            start=True, stop=False))
                        P.op("tensor", lambda e, qqb=qqb, Sa=Sa, hh=hh: e.matmul(
                            pi0[:], qqb[:, 0:128], Sa[:, hh * 64:(hh + 1) * 64],
                            start=False, stop=True), sem="p3it")
                        P.op("tensor", lambda e, a0b=a0b, vbuf=vbuf, hh=hh: e.matmul(
                            pi1[:], a0b[:, 128:256], vbuf[:, hh * 64: hh * 64 + 64],
                            start=True, stop=False))
                        P.op("tensor", lambda e, a1b=a1b, vbuf=vbuf, hh=hh: e.matmul(
                            pi1[:], a1b[:], vbuf[:, DM + hh * 64: DM + hh * 64 + 64],
                            start=False, stop=False))
                        itv = P.op("tensor", lambda e, qqb=qqb, Sa=Sa, hh=hh: e.matmul(
                            pi1[:], qqb[:, 128:256], Sa[:, hh * 64:(hh + 1) * 64],
                            start=False, stop=True), sem="p3it")
                        aw0.gate(ja0, "p3it", itv)
                        aw1.gate(ja1, "p3it", itv)
                        qq.gate(jqq, "p3it", itv)
                        bp = hh % 2
                        if n_sv >= 2:
                            P.wait("tensor", "p3sv2", n_sv - 1)
                        ctv = 0
                        for jh in range(2):
                            jkk, kkb = kk.next(P, "vector")
                            kv_ = P.op("vector",
                                       lambda e, kbuf=kbuf, hh=hh, jh=jh, kkb=kkb:
                                       e.tensor_scalar(
                                           kkb[:],
                                           kbuf[:, jh * DM + hh * 64: jh * DM + hh * 64 + 64],
                                           kdc[:, hh * 2 + jh: hh * 2 + jh + 1],
                                           None, ALU.mult), sem="p3kv2")
                            P.wait("tensor", "p3kv2", kv_)
                            ctv = P.op("tensor",
                                       lambda e, bp=bp, kkb=kkb, vbuf=vbuf, hh=hh, jh=jh:
                                       e.matmul(cps[bp][:], kkb[:],
                                                vbuf[:, jh * DM + hh * 64:
                                                     jh * DM + hh * 64 + 64],
                                                start=(jh == 0), stop=(jh == 1)),
                                       sem=("p3tc2" if jh == 1 else None))
                            kk.gate(jkk, "p3tc2", P.val("p3tc2") if jh == 1 else
                                    P.val("p3tc2") + 1)
                        P.wait("vector", "p3it", itv)
                        P.op("vector", lambda e, abuf=abuf, hh=hh: e.tensor_copy(
                            abuf[:, hh * 64:(hh + 1) * 64], pi0[:]))
                        P.op("vector", lambda e, abuf=abuf, hh=hh: e.tensor_copy(
                            abuf[:, DM + hh * 64: DM + hh * 64 + 64], pi1[:]), sem="p3ie")
                        n_ie += 1
                        P.wait("vector", "p3tc2", ctv)
                        P.op("vector", lambda e, Sa=Sa, Sb=Sb, bp=bp, hh=hh:
                             e.scalar_tensor_tensor(
                                 Sb[:, hh * 64:(hh + 1) * 64], Sa[:, hh * 64:(hh + 1) * 64],
                                 float(_BD[hh]), cps[bp][:], ALU.mult, ALU.add), sem="p3sv2")
                        n_sv += 1
                        awi += 1
                    sv_prev_block = P.val("p3sv2")
                    kb.gate(jk, "p3kv2", P.val("p3kv2"))
                    vb.gate(jv, "p3tc2", P.val("p3tc2"))
                    qs.gate(jq, "p3qe", P.val("p3qe"))
                    ks.gate(jkt, "p3aw", P.val("p3aw"))
                    P.wait("scalar", "p3ie", P.val("p3ie"))
                    P.dma("scalar", attn_scr[i * 256:i * 256 + 128, :],
                          abuf[:, 0:DM], f"p3s{ja}")
                    sv2 = P.dma("scalar", attn_scr[i * 256 + 128:(i + 1) * 256, :],
                                abuf[:, DM:2 * DM], f"p3s{ja}")
                    asm.gate(ja, f"p3s{ja}", sv2)
                P.drain("sync", "p3s0", "p3s1")
                P.barrier()

        def phase4():
            b_tv = P.val("p2tv")
            b_z = P.val("p2z")
            b_wv = P.val("p2wv")
            b_mt = P.val("p2mt")
            b_pf = P.val("p2pf")
            with contextlib.ExitStack() as outer:
                zT = sb("p4zT", [128, NMT * TOK], st=outer)
                with contextlib.ExitStack() as ar:
                    at = Rot([sb("p4at0", [128, DM], st=ar)])
                    gt = Rot([sb("p4gt0", [128, DM], st=ar)])
                    awb = sb("p4awb", [128, DM], st=ar)
                    P.dma("sync", awb[:], attnw_b, "p2lw0")
                    P.wait("vector", "p2lw0")
                    sqt = [sb(f"p4sq{i}", [128, DM], st=ar) for i in range(2)]
                    zt = [sb(f"p4zt{i}", [128, DM], st=ar) for i in range(2)]
                    tmp = [sb(f"p4tmp{i}", [128, 1], st=ar) for i in range(4)]
                    tpp = [pst(f"p4tp{i}", [128, 512], ar) for i in range(2)]
                    zT_r = zT[:].rearrange("p (k t) -> p k t", k=NMT)
                    for m in range(NMT):
                        ja, abuf = at.next(P, "sync")
                        a1 = P.dma("sync", abuf[:], attn_scr[m * 128:(m + 1) * 128, :],
                                   "p2ld0")
                        jg, gbuf = gt.next(P, "sync")
                        g1 = P.dma("sync", gbuf[:], gate_scr[m * 128:(m + 1) * 128, :],
                                   "p2ld1")
                        b = m % 2
                        P.wait("scalar", "p2ld0", a1)
                        P.wait("vector", "p2ld1", g1)
                        if m >= 2:
                            P.wait("scalar", "p2z", b_z + m - 1)
                            P.wait("vector", "p2tv", b_tv + (m - 1) * 4)
                        P.op("scalar", lambda e, b=b, abuf=abuf: e.activation(
                            sqt[b][:], abuf[:], AF.Square), sem="p2ss")
                        P.wait("vector", "p2ss")
                        P.op("vector", lambda e, b=b: e.reduce_sum(
                            tmp[0][:], sqt[b][:], axis=AX.X), sem="p2vs")
                        P.wait("scalar", "p2vs")
                        P.op("scalar", lambda e: e.activation(
                            tmp[2][:], tmp[0][:], AF.Sqrt,
                            scale=cinvd[:, 0:1], bias=ceps[:, 0:1]), sem="p2ss")
                        P.wait("vector", "p2ss")
                        P.op("vector", lambda e: e.reciprocal(tmp[3][:], tmp[2][:]))
                        P.op("vector", lambda e, b=b, abuf=abuf: e.tensor_scalar(
                            zt[b][:], abuf[:], tmp[3][:, 0:1], None, ALU.mult))
                        av = P.op("vector", lambda e, b=b: e.tensor_tensor(
                            zt[b][:], zt[b][:], awb[:], ALU.mult), sem="p2y")
                        at.gate(ja, "p2y", av)
                        gv = P.op("vector", lambda e, b=b, gbuf=gbuf: e.tensor_tensor(
                            zt[b][:], zt[b][:], gbuf[:], ALU.mult), sem="p2z")
                        gt.gate(jg, "p2z", gv)
                        P.wait("tensor", "p2z", gv)
                        for g in range(4):
                            gi = m * 4 + g
                            bg = gi % 2
                            if gi >= 2:
                                P.wait("tensor", "p2tv", b_tv + gi - 1)
                            for t4 in range(4):
                                ft = g * 4 + t4
                                P.op("tensor", lambda e, b=b, bg=bg, ft=ft, t4=t4:
                                     e.transpose(tpp[bg][:, t4 * 128:(t4 + 1) * 128],
                                                 zt[b][:, ft * 128:(ft + 1) * 128],
                                                 ident[:]),
                                     sem=("p2tt" if t4 == 3 else None))
                            P.wait("vector", "p2tt")
                            P.op("vector", lambda e, bg=bg, g=g, m=m: e.tensor_copy(
                                zT_r[:, g * 4:(g + 1) * 4, m * 128:(m + 1) * 128],
                                tpp[bg][:].rearrange("p (f t) -> p f t", f=4)), sem="p2tv")
                    P.barrier()
                with contextlib.ExitStack() as ar:
                    wf = Rot([sb(f"p4wf{i}", [128, DM], st=ar) for i in range(2)])
                    Wt = sb("p4Wt", [128, 16 * 256], st=ar)
                    stg = Rot([sb(f"p4sg{i}", [128, 256], st=ar) for i in range(2)])
                    xr = Rot([sb(f"p4xr{i}", [128, 256], st=ar) for i in range(2)])
                    tpp = [pst(f"p4tq{i}", [128, 512], ar) for i in range(2)]
                    mmp = [pst(f"p4mm{i}", [128, 256], ar) for i in range(2)]
                    Wt_r = Wt[:].rearrange("p (k r) -> p k r", k=16)
                    tpi = 0
                    mmi = 0
                    for c in range(8):
                        if c > 0:
                            P.wait("vector", "p2mt", b_mt + mmi)
                        wv = 0
                        for s4 in range(2):
                            j, wbuf = wf.next(P, "sync")
                            lv = P.dma("sync", wbuf[:],
                                       outw_in[c * 256 + s4 * 128:
                                               c * 256 + (s4 + 1) * 128, :], f"p2lw{j}")
                            P.wait("tensor", f"p2lw{j}", lv)
                            for g in range(4):
                                bg = tpi % 2
                                if tpi >= 2:
                                    P.wait("tensor", "p2wv", b_wv + tpi - 1)
                                for t4 in range(4):
                                    kt = g * 4 + t4
                                    P.op("tensor",
                                         lambda e, wbuf=wbuf, bg=bg, kt=kt, t4=t4:
                                         e.transpose(tpp[bg][:, t4 * 128:(t4 + 1) * 128],
                                                     wbuf[:, kt * 128:(kt + 1) * 128],
                                                     ident[:]),
                                         sem=("p2wt" if t4 == 3 else None))
                                P.wait("vector", "p2wt")
                                wv = P.op("vector", lambda e, bg=bg, g=g, s4=s4:
                                          e.tensor_copy(
                                              Wt_r[:, g * 4:(g + 1) * 4,
                                                   s4 * 128:(s4 + 1) * 128],
                                              tpp[bg][:].rearrange("p (f t) -> p f t", f=4)),
                                          sem="p2wv")
                                tpi += 1
                            wf.gate(j, "p2wv", wv)
                        P.wait("tensor", "p2wv", wv)
                        for m in range(NMT):
                            bm = mmi % 2
                            if mmi >= 2:
                                P.wait("tensor", "p2pf", b_pf + mmi - 1)
                            tvv = 0
                            for kt in range(16):
                                tvv = P.op("tensor", lambda e, bm=bm, m=m, kt=kt: e.matmul(
                                    mmp[bm][:],
                                    zT[:, kt * TOK + m * 128: kt * TOK + m * 128 + 128],
                                    Wt[:, kt * 256:(kt + 1) * 256],
                                    start=(kt == 0), stop=(kt == 15)),
                                    sem=("p2mt" if kt == 15 else None))
                            mmi += 1
                            jx, xbuf = xr.next(P, "sync")
                            lv = P.dma("sync", xbuf[:],
                                       x1_scr[m * 128:(m + 1) * 128,
                                              c * 256:(c + 1) * 256], f"p2ld{jx}")
                            js, sbuf_ = stg.next(P, "vector")
                            P.wait("vector", "p2mt", tvv)
                            P.wait("vector", f"p2ld{jx}", lv)
                            xv = P.op("vector", lambda e, bm=bm, xbuf=xbuf, sbuf_=sbuf_:
                                      e.tensor_tensor(sbuf_[:], mmp[bm][:], xbuf[:],
                                                      ALU.add), sem="p2pf")
                            xr.gate(jx, "p2pf", xv)
                            P.wait("scalar", "p2pf", xv)
                            sv = P.dma("scalar",
                                       x2_scr[m * 128:(m + 1) * 128,
                                              c * 256:(c + 1) * 256], sbuf_[:],
                                       f"p2st{js}")
                            stg.gate(js, f"p2st{js}", sv)
                    P.drain("sync", "p2st0", "p2st1")
                    P.barrier()

        phase0()
        if max_phase >= 1:
            phase_ffn(0, x_in, x1_scr)
        if max_phase >= 2:
            phase2()
        if max_phase >= 3:
            phase3a()
            phase3b()
        if max_phase >= 4:
            phase4()
        if max_phase >= 5:
            phase_ffn(1, x2_scr, y_out)
        else:
            for m in range(NMT):
                P.dma("gpsimd", y_out[m * 128:(m + 1) * 128, :],
                      x1_scr[m * 128:(m + 1) * 128, :], "fcp")
            P.wait("gpsimd", "fcp")
        for t, ap in tap_outs.items():
            if ap is None:
                continue
            P.dma("gpsimd", ap, tap_src[t], "dbg")
        if P.val("dbg") if "dbg" in P.cnt else 0:
            P.wait("gpsimd", "dbg")
        P.barrier()

        block = big.enter_context(nc.Block())
        P.finalize(block)
    return nc


def _stage_inputs(inputs):
    x = np.ascontiguousarray(np.asarray(inputs["L0_attn_hidden_states"],
                                        dtype=np.float32).reshape(B * S, DM))
    qkv_w = np.asarray(inputs["L0_attn_qkv_w"], np.float32)
    out_w = np.ascontiguousarray(np.asarray(inputs["L0_attn_out_w"], np.float32))
    gate_w = np.asarray(inputs["L0_attn_gate_w"], np.float32)
    norm_w = np.asarray(inputs["L0_attn_norm_w"], np.float32)
    w1 = np.ascontiguousarray(np.asarray(inputs["L0_ffn_w_gate_up"], np.float32))
    w2 = np.ascontiguousarray(np.asarray(inputs["L0_ffn_w_down"], np.float32))
    fn = np.asarray(inputs["L0_ffn_norm_weight"], np.float32)

    wat = np.ascontiguousarray(np.concatenate([qkv_w[_perm_qkv()], gate_w], axis=0))
    ffnw_b = np.ascontiguousarray(np.broadcast_to(fn[None, :], (128, DM)))
    attnw_b = np.ascontiguousarray(np.broadcast_to(norm_w[None, :], (128, DM)))
    ident = np.eye(128, dtype=np.float32)

    in_maps = []
    for c in range(NCORES):
        in_maps.append({
            "x_in": np.ascontiguousarray(x[c * TOK:(c + 1) * TOK]),
            "w_attn": wat,
            "outw_in": out_w,
            "w1_in": w1,
            "w2_in": w2,
            "w1_sl": np.ascontiguousarray(w1[c * (F2 // 8):(c + 1) * (F2 // 8)]),
            "w2_sl": np.ascontiguousarray(w2[c * (DM // 8):(c + 1) * (DM // 8)]),
            "ffnw_b": ffnw_b,
            "attnw_b": attnw_b,
            "ident_in": ident,
            "dd0_in": _DD0,
            "qdb_in": _QDB,
            "kdc_in": _KDC,
            "tpred_in": _t_pred(c),
        })
    return in_maps


def kernel(**inputs):
    nc = build_nc()
    in_maps = _stage_inputs(inputs)
    res = run_bass_kernel_spmd(nc, in_maps, core_ids=list(range(NCORES)))
    out = np.concatenate([np.asarray(res.results[c]["y_out"], dtype=np.float32)
                          for c in range(NCORES)], axis=0)
    return out.reshape(B, S, DM)


# version 09
# speedup vs baseline: 1.1040x; 1.0506x over previous
import contextlib
import numpy as np
import ml_dtypes

import concourse.bass as bass
import concourse.mybir as mybir
from concourse.bass_utils import run_bass_kernel_spmd

F32 = mybir.dt.float32
BF16 = mybir.dt.bfloat16
I32 = mybir.dt.int32
AF = mybir.ActivationFunctionType
ALU = mybir.AluOpType
AX = mybir.AxisListType

B, S, DM = 2, 8192, 2048
H, D, BS = 32, 64, 256
FF = 5632
F2 = 2 * FF
NCORES = 8
TOK = (B * S) // NCORES
NBLK = TOK // BS
NMT = TOK // 128
EPS = 1e-5
_ACT_EVAC = True
_INTERLEAVE_Q = True
_DUMP_B = False
N1 = F2 * DM
N2 = DM * FF

ENGS = ("sync", "vector", "scalar", "tensor", "gpsimd")


def _decay():
    h = np.arange(H, dtype=np.float64)
    base = 1.0 / 2.0 ** (8.0 / H)
    slope = (base ** (h + 1.0)) * (1.0 - 0.0 / (32 - 1 + 1e-05) + 1e-05)
    r = np.arange(BS, dtype=np.float64) + 1.0
    qd = np.exp(-slope[:, None] * r[None, :]).astype(np.float32)
    kd = np.exp(-slope[:, None] * (BS - r[None, :])).astype(np.float32)
    bd = np.exp(-slope * BS).astype(np.float32)
    i_ = np.arange(128, dtype=np.float64)
    dif = i_[None, :] - i_[:, None]
    DD0 = np.zeros((128, H * 256), np.float32)
    QDB = np.zeros((128, (H // 2) * 256), np.float32)
    KDC = np.zeros((128, H * 2), np.float32)
    for hh in range(H):
        t0 = np.where(dif >= 0, np.exp(-slope[hh] * dif), 0.0)
        t1 = np.exp(-slope[hh] * (128.0 + dif))
        DD0[:, hh * 256:hh * 256 + 128] = t0.astype(np.float32)
        DD0[:, hh * 256 + 128:hh * 256 + 256] = t1.astype(np.float32)
        p0 = 64 * (hh % 2)
        QDB[p0:p0 + 64, (hh // 2) * 256:(hh // 2) * 256 + 256] = qd[hh][None, :]
        KDC[:, hh * 2] = kd[hh, 0:128]
        KDC[:, hh * 2 + 1] = kd[hh, 128:256]
    return qd, kd, bd, DD0, QDB, KDC


_QD, _KD, _BD, _DD0, _QDB, _KDC = _decay()


def _t_pred(core):
    out = np.zeros((8, 64, H * D), np.float32)
    g, j = core // 4, core % 4
    for cp in range(8):
        if cp // 4 != g or cp % 4 >= j:
            continue
        w = _BD.astype(np.float64) ** ((j - 1 - (cp % 4)) * NBLK)
        for hh in range(H):
            out[cp, :, hh * D:(hh + 1) * D] = np.float32(w[hh])
    return out


def _perm_qkv():
    perm = np.zeros(3 * H * D, np.int64)
    for hh in range(H):
        for d in range(D):
            perm[0 * H * D + hh * D + d] = 192 * hh + d
            perm[1 * H * D + hh * D + d] = 192 * hh + 64 + d
            perm[2 * H * D + hh * D + d] = 192 * hh + 128 + d
    return perm


_CHAIN = ("vector", "scalar", "gpsimd")


class Prog:
    def __init__(self, nc, stack):
        self.nc, self.stack = nc, stack
        self.q = {e: [] for e in ENGS}
        self.h, self.cnt = {}, {}
        self.last = {e: None for e in ENGS}
        self.waited = {e: {} for e in ENGS}

    def sem(self, name):
        if name not in self.h:
            self.h[name] = self.stack.enter_context(self.nc.semaphore(name))
            self.cnt[name] = 0
        return self.h[name]

    def emit(self, eng, fn):
        self.q[eng].append(fn)

    def _wait_raw(self, eng, sem, v):
        if v <= 0:
            return
        if self.waited[eng].get(sem, 0) >= v:
            return
        self.waited[eng][sem] = v
        hh = self.sem(sem)
        self.emit(eng, lambda e, hh=hh, v=v: e.wait_ge(hh, v))

    def _chain_wait(self, eng):
        if self.last[eng] is not None:
            s, v = self.last[eng]
            self._wait_raw(eng, s, v)

    def dma(self, eng, out, in_, sem):
        self._chain_wait(eng)
        hh = self.sem(sem)
        self.cnt[sem] += 16
        self.emit(eng, lambda e, out=out, in_=in_, hh=hh:
                  e.dma_start(out=out, in_=in_).then_inc(hh, 16))
        return self.cnt[sem]

    def dmaT(self, eng, out, in_, sem):
        self._chain_wait(eng)
        hh = self.sem(sem)
        self.cnt[sem] += 16
        self.emit(eng, lambda e, out=out, in_=in_, hh=hh:
                  e.dma_start_transpose(out=out, in_=in_).then_inc(hh, 16))
        return self.cnt[sem]

    def wait(self, eng, sem, val=None):
        self.sem(sem)
        v = self.cnt[sem] if val is None else val
        self._wait_raw(eng, sem, v)

    def op(self, eng, fn, sem=None, k=1):
        if eng in _CHAIN:
            self._chain_wait(eng)
            if sem is None:
                sem = "_ch_" + eng
                k = 1
        if sem is not None:
            hh = self.sem(sem)
            self.cnt[sem] += k
            self.emit(eng, lambda e, fn=fn, hh=hh, k=k: fn(e).then_inc(hh, k))
            if eng in _CHAIN or eng == "tensor":
                self.last[eng] = (sem, self.cnt[sem])
            return self.cnt[sem]
        self.emit(eng, fn)
        return None

    def val(self, sem):
        self.sem(sem)
        return self.cnt[sem]

    def drain(self, eng, *sems):
        for s in sems:
            self.wait(eng, s)

    def barrier(self):
        hh = self.sem("phase")
        for eng in ENGS:
            if eng in _CHAIN:
                self._chain_wait(eng)
            elif self.last.get(eng) is not None:
                s, v = self.last[eng]
                self._wait_raw(eng, s, v)
            self.emit(eng, lambda e, hh=hh: e.nop().then_inc(hh, 1))
        self.cnt["phase"] += len(ENGS)
        v = self.cnt["phase"]
        for eng in ENGS:
            self._wait_raw(eng, "phase", v)

    def finalize(self, block):
        for eng in ENGS:
            fns = self.q[eng]

            def body(e, fns=fns):
                for fn in fns:
                    fn(e)
            getattr(block, eng)(body)


class Rot:

    def __init__(self, bufs):
        self.bufs = bufs
        self.gates = [None] * len(bufs)
        self.i = -1

    def next(self, P, producer_eng):
        self.i += 1
        j = self.i % len(self.bufs)
        if self.gates[j] is not None:
            s, v = self.gates[j]
            P.wait(producer_eng, s, v)
        return j, self.bufs[j]

    def gate(self, j, sem, val):
        self.gates[j] = (sem, val)


def build_nc(max_phase=99, taps=(), n_rg=NCORES):
    nc = bass.Bass("TRN2", target_bir_lowering=False)

    x_in = nc.dram_tensor("x_in", [TOK, DM], F32, kind="ExternalInput").ap()
    w_attn = nc.dram_tensor("w_attn", [4 * DM, DM], F32, kind="ExternalInput").ap()
    outw_in = nc.dram_tensor("outw_in", [DM, DM], F32, kind="ExternalInput").ap()
    w1_in = nc.dram_tensor("w1_in", [F2, DM], F32, kind="ExternalInput").ap()
    w2_in = nc.dram_tensor("w2_in", [DM, FF], F32, kind="ExternalInput").ap()
    w1_sl = nc.dram_tensor("w1_sl", [F2 // 8, DM], F32, kind="ExternalInput").ap()
    w2_sl = nc.dram_tensor("w2_sl", [DM // 8, FF], F32, kind="ExternalInput").ap()
    ffnw_b = nc.dram_tensor("ffnw_b", [128, DM], F32, kind="ExternalInput").ap()
    attnw_b = nc.dram_tensor("attnw_b", [128, DM], F32, kind="ExternalInput").ap()
    ident_in = nc.dram_tensor("ident_in", [128, 128], F32, kind="ExternalInput").ap()
    dd0_in = nc.dram_tensor("dd0_in", [128, H * 256], F32, kind="ExternalInput").ap()
    qdb_in = nc.dram_tensor("qdb_in", [128, (H // 2) * 256], F32, kind="ExternalInput").ap()
    kdc_in = nc.dram_tensor("kdc_in", [128, H * 2], F32, kind="ExternalInput").ap()
    tpred_in = nc.dram_tensor("tpred_in", [8, 64, H * D], F32, kind="ExternalInput").ap()
    y_out = nc.dram_tensor("y_out", [TOK, DM], F32, kind="ExternalOutput").ap()

    x1_scr = nc.dram_tensor("x1_scr", [TOK, DM], F32).ap()
    x2_scr = nc.dram_tensor("x2_scr", [TOK, DM], F32).ap()
    xi_scr = nc.dram_tensor("xi_scr", [TOK, DM], BF16).ap()
    w1q_scr = nc.dram_tensor("w1q_scr", [F2, DM], BF16).ap()
    w2q_scr = nc.dram_tensor("w2q_scr", [DM, FF], BF16).ap()
    w2tT_scr = nc.dram_tensor("w2tT_scr", [FF, DM], BF16).ap()
    hid_scr = nc.dram_tensor("hid_scr", [TOK, FF], F32).ap()
    hq_scr = nc.dram_tensor("hq_scr", [TOK, FF], BF16).ap()
    qkv_scr = nc.dram_tensor("qkv_scr", [TOK, 3 * DM], F32).ap()
    gate_scr = nc.dram_tensor("gate_scr", [TOK, DM], F32).ap()
    qT_scr = nc.dram_tensor("qT_scr", [DM, TOK], F32).ap()
    kT_scr = nc.dram_tensor("kT_scr", [DM, TOK], F32).ap()
    attn_scr = nc.dram_tensor("attn_scr", [TOK, DM], F32).ap()
    cc_sin = nc.dram_tensor("cc_sin", [64, H * D], F32).ap()
    cc_sout = nc.dram_tensor("cc_sout", [8 * 64, H * D], F32).ap()
    cc_min = nc.dram_tensor("cc_min", [1, 2], F32).ap()
    cc_mout = nc.dram_tensor("cc_mout", [1, 2], F32).ap()

    tap_src = {"x1": x1_scr, "x2": x2_scr, "qkv": qkv_scr, "gate": gate_scr,
               "qT": qT_scr, "kT": kT_scr, "attn": attn_scr, "sgath": cc_sout,
               "hid": hid_scr, "xi": xi_scr, "hq": hq_scr,
               "w1q": w1q_scr, "w2q": w2q_scr, "w2tT": w2tT_scr}
    tap_dt = {"xi": BF16, "hq": BF16, "w1q": BF16, "w2q": BF16, "w2tT": BF16}
    tap_outs = {}
    _STRIP_TAP = None
    _STRIP_A = "strips_a" in taps
    taps = [t for t in taps if t != "strips_a"]
    if "strips" in taps or _STRIP_A:
        taps = [t for t in taps if t != "strips"]
        _STRIP_TAP = nc.dram_tensor("dbg_strips", [512, NMT], F32,
                                    kind="ExternalOutput").ap()
        tap_outs["strips"] = None
    for t in taps:
        src = tap_src[t]
        tap_outs[t] = nc.dram_tensor("dbg_" + t, list(src.tensor.shape),
                                     tap_dt.get(t, F32), kind="ExternalOutput").ap()

    with contextlib.ExitStack() as big:
        P = Prog(nc, big)

        def sb(name, shape, dt=F32, st=None):
            return (st or big).enter_context(nc.sbuf_tensor(name, shape, dt))

        def pst(name, shape, st, dt=F32):
            return st.enter_context(nc.psum_tensor(name, shape, dt))

        ident = sb("ident", [128, 128])
        identb = sb("identb", [128, 128], BF16)
        c384 = sb("c384", [128, 1])
        cm384 = sb("cm384", [128, 1])
        ceps = sb("ceps", [128, 1])
        cinvd = sb("cinvd", [128, 1])
        maski = sb("maski", [128, 1], I32)
        ones_c = sb("ones_c", [128, 1])
        ones_r = sb("ones_r", [1, 128])
        sw1s = sb("sw1s", [128, 1])
        nsw1s = sb("nsw1s", [128, 1])
        w1m = sb("w1m", [128, 1])
        sw2s = sb("sw2s", [128, 1])
        w2m = sb("w2m", [128, 1])
        kdc = sb("kdc", [128, H * 2])
        sinv_s = [[sb(f"sinv_s{a}_{m}", [128, 1]) for m in range(NMT)] for a in range(2)]
        amax_s = [[sb(f"amax_s{a}_{m}", [128, 1]) for m in range(NMT)] for a in range(2)]
        m2q_s = [[sb(f"m2q_s{a}_{m}", [128, 1]) for m in range(NMT)] for a in range(2)]
        fin_s = [[sb(f"fin_s{a}_{m}", [128, 1]) for m in range(NMT)] for a in range(2)]
        Sin_st = sb("Sin_st", [64, H * D])

        def phase0():
            for f in (lambda e: e.memset(c384[:], 384.5),
                      lambda e: e.memset(cm384[:], -384.0),
                      lambda e: e.memset(ceps[:], EPS),
                      lambda e: e.memset(cinvd[:], 1.0 / DM),
                      lambda e: e.memset(maski[:], -32768),
                      lambda e: e.memset(ones_c[:], 1.0),
                      lambda e: e.memset(ones_r[:], 1.0)):
                P.op("vector", f)
            P.dma("sync", ident[:], ident_in, "ld0")
            P.dma("sync", kdc[:], kdc_in, "ld0")
            with contextlib.ExitStack() as ar:
                wt = Rot([sb(f"p0w{i}", [128, FF], st=ar) for i in range(2)])
                acc = sb("p0acc", [128, 2], st=ar)
                acc1 = sb("p0acc1", [128, 1], st=ar)
                acc2 = sb("p0acc2", [128, 1], st=ar)
                red = sb("p0red", [128, 1], st=ar)
                sc2 = sb("p0sc", [1, 2], st=ar)
                pr = pst("p0ps", [1, 2], ar)
                pb = pst("p0pb", [128, 2], ar)
                P.op("vector", lambda e: e.memset(acc1[:], 0.0))
                P.op("vector", lambda e: e.memset(acc2[:], 0.0))
                for accb, tiles, srcap, ncols in ((acc1, (F2 // 8) // 128, w1_sl, DM),
                                                  (acc2, (DM // 8) // 128, w2_sl, FF)):
                    for i in range(tiles):
                        j, buf = wt.next(P, "sync")
                        lv = P.dma("sync", buf[:, 0:ncols],
                                   srcap[i * 128:(i + 1) * 128, :], f"ld1_{j}")
                        P.wait("vector", f"ld1_{j}", lv)
                        P.op("vector", lambda e, buf=buf, ncols=ncols: e.reduce_sum(
                            red[:], buf[:, 0:ncols], axis=AX.X, apply_absolute_value=True))
                        vv = P.op("vector", lambda e, accb=accb: e.tensor_tensor(
                            accb[:], accb[:], red[:], ALU.add), sem="v0")
                        wt.gate(j, "v0", vv)
                P.op("vector", lambda e: e.tensor_scalar(
                    acc[:, 0:1], acc1[:], 1.0, None, ALU.mult))
                P.op("vector", lambda e: e.tensor_scalar(
                    acc[:, 1:2], acc2[:], 1.0, None, ALU.mult), sem="v0")
                P.wait("tensor", "v0")
                P.op("tensor", lambda e: e.matmul(pr[:], ones_c[:], acc[:],
                                                  start=True, stop=True), sem="t0")
                P.wait("scalar", "t0")
                P.op("scalar", lambda e: e.copy(sc2[:], pr[:]), sem="s0")
                P.wait("scalar", "s0")
                P.dma("scalar", cc_min, sc2[:], "st0")
                P.wait("gpsimd", "st0")
                P.op("gpsimd", lambda e: e.collective_compute(
                    "AllReduce", ALU.add, replica_groups=[list(range(n_rg))],
                    ins=[cc_min.opt()], outs=[cc_mout.opt()]), sem="cc0")
                P.wait("sync", "cc0")
                P.dma("sync", sc2[:], cc_mout, "ld2")
                P.wait("vector", "ld2")
                P.op("vector", lambda e: e.tensor_scalar(
                    sc2[:, 0:1], sc2[:, 0:1], 1.0 / N1, 1e-5, ALU.mult, ALU.max))
                P.op("vector", lambda e: e.tensor_scalar(
                    sc2[:, 1:2], sc2[:, 1:2], 1.0 / N2, 1e-5, ALU.mult, ALU.max), sem="v0")
                P.wait("tensor", "v0")
                P.op("tensor", lambda e: e.matmul(pb[:], ones_r[:], sc2[:],
                                                  start=True, stop=True), sem="t0")
                P.wait("vector", "t0")
                P.op("vector", lambda e: e.tensor_copy(w1m[:], pb[:, 0:1]))
                P.op("vector", lambda e: e.tensor_copy(w2m[:], pb[:, 1:2]))
                P.op("vector", lambda e: e.reciprocal(sw1s[:], w1m[:]))
                P.op("vector", lambda e: e.reciprocal(sw2s[:], w2m[:]))
                P.op("vector", lambda e: e.tensor_scalar(
                    nsw1s[:], sw1s[:], -1.0, None, ALU.mult))
                P.wait("vector", "ld0")
                P.op("vector", lambda e: e.tensor_copy(identb[:], ident[:]))
                P.barrier()

        def emit_ln(xt_ap, yt_ap, sqt_ap, tmp, vs, ss):
            rs, mn, sd, iv = tmp
            P.op("vector", lambda e: e.reduce_sum(rs[:], xt_ap, axis=AX.X))
            P.op("vector", lambda e: e.tensor_scalar(
                mn[:], rs[:], 1.0 / DM, None, ALU.mult))
            P.op("vector", lambda e: e.tensor_scalar(
                yt_ap, xt_ap, mn[:, 0:1], None, ALU.subtract), sem=vs)
            P.wait("scalar", vs)
            P.op("scalar", lambda e: e.activation(sqt_ap, yt_ap, AF.Square), sem=ss)
            P.wait("vector", ss)
            P.op("vector", lambda e: e.reduce_sum(rs[:], sqt_ap, axis=AX.X), sem=vs)
            P.wait("scalar", vs)
            P.op("scalar", lambda e: e.activation(
                sd[:], rs[:], AF.Sqrt, scale=cinvd[:, 0:1], bias=ceps[:, 0:1]), sem=ss)
            P.wait("vector", ss)
            P.op("vector", lambda e: e.reciprocal(iv[:], sd[:]))
            P.op("vector", lambda e: e.tensor_scalar(
                yt_ap, yt_ap, iv[:, 0:1], None, ALU.mult), sem=vs)

        def phase_ffn(app, src, dst):
            tag = f"f{app}"
            ld, lw, tp, st, vs, ss, ts_, pf = [
                "f" + s_ for s_ in
                ("ld", "lw", "tp", "st", "vs", "ss", "ts", "pf")]
            sinv, amx, m2q, fin = sinv_s[app], amax_s[app], m2q_s[app], fin_s[app]

            with contextlib.ExitStack() as ar:
                xt = Rot([sb(f"{tag}xt{i}", [128, DM], st=ar) for i in range(2)])
                yt = [sb(f"{tag}yt{i}", [128, DM], st=ar) for i in range(2)]
                sqt = [sb(f"{tag}sq{i}", [128, DM], st=ar) for i in range(2)]
                xib = Rot([sb(f"{tag}xib{i}", [128, DM], BF16, st=ar) for i in range(2)])
                tmp = [sb(f"{tag}tmp{i}", [128, 1], st=ar) for i in range(4)]
                am = sb(f"{tag}am", [128, 1], st=ar)
                sc = sb(f"{tag}sc", [128, 1], st=ar)
                fwb = sb(f"{tag}fwb", [128, DM], st=ar)
                P.dma("sync", fwb[:], ffnw_b, ld + "f")
                ssx_base = P.val(ss + "x")
                P.wait("vector", ld + "f")
                for m in range(NMT):
                    j, xbuf = xt.next(P, "sync")
                    lv = P.dma("sync", xbuf[:], src[m * 128:(m + 1) * 128, :],
                               f"{ld}_{j}")
                    b = m % 2
                    P.wait("vector", f"{ld}_{j}", lv)
                    if m >= 2:
                        P.wait("vector", ss + "x", ssx_base + m - 1)
                    emit_ln(xbuf[:], yt[b][:], sqt[b][:], tmp, vs, ss)
                    P.wait("scalar", vs)
                    P.op("scalar", lambda e, b=b: e.activation(
                        sqt[b][:], yt[b][:], AF.Square), sem=ss)
                    P.wait("vector", ss)
                    P.op("vector", lambda e, b=b: e.reduce_sum(
                        tmp[0][:], sqt[b][:], axis=AX.X), sem=vs)
                    P.wait("scalar", vs)
                    P.op("scalar", lambda e: e.activation(
                        tmp[2][:], tmp[0][:], AF.Sqrt,
                        scale=cinvd[:, 0:1], bias=ceps[:, 0:1]), sem=ss)
                    P.wait("vector", ss)
                    P.op("vector", lambda e: e.reciprocal(tmp[3][:], tmp[2][:]))
                    P.op("vector", lambda e, b=b: e.tensor_scalar(
                        yt[b][:], yt[b][:], tmp[3][:, 0:1], None, ALU.mult))
                    P.op("vector", lambda e, b=b: e.tensor_tensor(
                        yt[b][:], yt[b][:], fwb[:], ALU.mult))
                    P.op("vector", lambda e, b=b: e.reduce_max(
                        am[:], yt[b][:], axis=AX.X, apply_absolute_value=True))
                    P.op("vector", lambda e: e.tensor_scalar(
                        am[:], am[:], 1e-5, None, ALU.max))
                    P.op("vector", lambda e: e.reciprocal(sc[:], am[:]))
                    P.op("vector", lambda e: e.tensor_scalar(
                        sc[:], sc[:], 127.0, None, ALU.mult))
                    P.op("vector", lambda e, m=m: e.tensor_scalar(
                        sinv[m][:], am[:], 1.0 / 127.0, None, ALU.mult))
                    P.op("vector", lambda e, b=b: e.tensor_scalar(
                        yt[b][:], yt[b][:], sc[:, 0:1], 384.5, ALU.mult, ALU.add))
                    xv = P.op("vector", lambda e, b=b: e.tensor_scalar(
                        yt[b][:].bitcast(I32), yt[b][:].bitcast(I32), maski[:, 0:1],
                        None, ALU.bitwise_and), sem=vs)
                    xt.gate(j, vs, xv)
                    jx, xbuf2 = xib.next(P, "scalar")
                    P.wait("scalar", vs, xv)
                    P.op("scalar", lambda e, b=b, xbuf2=xbuf2: e.activation(
                        xbuf2[:], yt[b][:], AF.Identity, bias=cm384[:, 0:1]), sem=ss + "x")
                    sv = P.dma("scalar", xi_scr[m * 128:(m + 1) * 128, :],
                               xbuf2[:], f"{st}_{jx}")
                    xib.gate(jx, f"{st}_{jx}", sv)
                if _STRIP_TAP is not None and app == 0 and _STRIP_A:
                    P.wait("sync", vs)
                    for m in range(NMT):
                        P.dma("sync", _STRIP_TAP[0:128, m:m + 1], sinv[m][:], "dbgs")
                P.drain("sync", f"{st}_0", f"{st}_1")
                P.barrier()

            with contextlib.ExitStack() as ar:
                xiT = sb(f"{tag}xiT", [128, NMT * TOK], BF16, st=ar)
                wf = Rot([sb(f"{tag}wf{i}", [128, DM], st=ar) for i in range(2)])
                wqa = sb(f"{tag}wqa", [128, DM], BF16, st=ar)
                wqb = sb(f"{tag}wqb", [128, DM], BF16, st=ar)
                wqo = Rot([sb(f"{tag}wqo{i}", [128, DM], BF16, st=ar) for i in range(2)])
                w1tg = [sb(f"{tag}w1tg{i}", [128, 16 * 512], BF16, st=ar) for i in range(2)]
                w1tu = [sb(f"{tag}w1tu{i}", [128, 16 * 512], BF16, st=ar) for i in range(2)]
                rel = [sb(f"{tag}rel{i}", [128, 512], st=ar) for i in range(2)]
                rl2 = [sb(f"{tag}rl2{i}", [128, 512], st=ar) for i in range(2)]
                hid = Rot([sb(f"{tag}hid{i}", [128, 512], st=ar) for i in range(2)])
                red1 = sb(f"{tag}red1", [128, 1], st=ar)
                t1 = sb(f"{tag}t1", [128, 1], st=ar)
                ht = Rot([sb(f"{tag}ht0", [128, FF // 2], st=ar)])
                hb = Rot([sb(f"{tag}hb0", [128, FF // 2], BF16, st=ar)])
                pg = [pst(f"{tag}pg{i}", [128, 512], ar) for i in range(2)]
                pu = [pst(f"{tag}pu{i}", [128, 512], ar) for i in range(2)]
                if app == 0:
                    ptt = [pst(f"{tag}ptt{i}", [128, 1024], ar, BF16)
                           for i in range(2)]
                    w2tt = sb(f"{tag}w2tt", [128, DM], BF16, st=ar)
                for m in range(NMT):
                    P.op("vector", lambda e, m=m: e.memset(amx[m][:], 0.0))
                for ft in range(NMT):
                    P.dmaT("sync", xiT[:, ft * TOK:(ft + 1) * TOK],
                           xi_scr[:, ft * 128:(ft + 1) * 128], tp + "x")
                nch = F2 // 1024
                mm_iter = 0
                pf_base = P.val(pf)
                ts_base = P.val(ts_)
                ssx2_base = P.val(ss + "x")
                hid_sv = [0] * NMT
                hq_sv = [0] * NMT

                pending_loads = []
                unit_queue = []
                pending_tt = []

                def issue_load(kind, meta):
                    jw, wbuf = wf.next(P, "gpsimd")
                    if kind == "w1":
                        lv = P.dma("gpsimd", wbuf[:], w1_in[meta:meta + 128, :],
                                   f"{lw}_{jw}")
                    else:
                        mq, c0 = meta
                        cn = min(2048, FF - c0)
                        lv = P.dma("gpsimd", wbuf[:, 0:cn],
                                   w2_in[mq * 128:(mq + 1) * 128, c0:c0 + cn],
                                   f"{lw}_{jw}")
                    pending_loads.append((kind, meta, jw, lv, wbuf))

                def emit_quant_one():
                    kind, meta, jw, lv, wbuf = pending_loads.pop(0)
                    cn = DM if kind == "w1" else min(2048, FF - meta[1])
                    P.wait("vector", f"{lw}_{jw}", lv)
                    if kind == "w1":
                        P.op("vector", lambda e, wbuf=wbuf: e.tensor_scalar(
                            wqa[:], wbuf[:], sw1s[:, 0:1], 0.5, ALU.mult,
                            ALU.is_gt))
                        vv = P.op("vector", lambda e, wbuf=wbuf: e.tensor_scalar(
                            wqb[:], wbuf[:], nsw1s[:, 0:1], 0.5, ALU.mult,
                            ALU.is_gt), sem=vs + "w")
                    else:
                        P.op("vector", lambda e, wbuf=wbuf, cn=cn: e.tensor_scalar(
                            wqa[:, 0:cn], wbuf[:, 0:cn], sw2s[:, 0:1], 0.5,
                            ALU.mult, ALU.is_gt))
                        vv = P.op("vector", lambda e, wbuf=wbuf, cn=cn: e.tensor_scalar(
                            wqb[:, 0:cn], wbuf[:, 0:cn], sw2s[:, 0:1], -0.5,
                            ALU.mult, ALU.is_lt), sem=vs + "w")
                    wf.gate(jw, vs + "w", vv)
                    jq, qbuf = wqo.next(P, "vector")
                    qv = P.op("vector", lambda e, qbuf=qbuf, cn=cn: e.tensor_tensor(
                        qbuf[:, 0:cn], wqa[:, 0:cn], wqb[:, 0:cn], ALU.subtract),
                        sem=vs + "q")
                    if kind == "w1":
                        P.wait("gpsimd", vs + "q", qv)
                        sv = P.dma("gpsimd", w1q_scr[meta:meta + 128, :], qbuf[:],
                                   f"{st}w{jq}")
                        wqo.gate(jq, f"{st}w{jq}", sv)
                    else:
                        mq, c0 = meta
                        pending_tt.append((qbuf, jq, qv, mq, c0, cn))
                    if unit_queue:
                        issue_load(*unit_queue.pop(0))

                tts = {"i": 0, "ev": [None, None], "st": None}

                def emit_tt_one():
                    qbuf, jq, qv, mq, c0, cn = pending_tt.pop(0)
                    ttv = 0
                    ev = 0
                    for g4 in range(cn // 512):
                        bt = tts["i"] % 2
                        tts["i"] += 1
                        if tts["ev"][bt] is not None:
                            P.wait("tensor", "fs2e", tts["ev"][bt])
                        P.wait("tensor", vs + "q", qv)
                        for j in range(4):
                            ttv = P.op("tensor",
                                       lambda e, bt=bt, j=j, b=g4 * 4 + j,
                                       qbuf=qbuf: e.transpose(
                                           ptt[bt][:, j * 128:(j + 1) * 128],
                                           qbuf[:, b * 128:(b + 1) * 128],
                                           identb[:]),
                                       sem=("ftt" if j == 3 else None))
                        P.wait("scalar", "ftt", ttv)
                        if g4 == 0 and tts["st"] is not None:
                            P.wait("scalar", "ft2s", tts["st"])
                        ev = P.op("scalar", lambda e, bt=bt, g4=g4: e.activation(
                            w2tt[:, g4 * 512:(g4 + 1) * 512], ptt[bt][:, 0:512],
                            AF.Identity), sem="fs2e")
                        tts["ev"][bt] = ev
                    wqo.gate(jq, "ftt", ttv)
                    dst3 = w2tT_scr[c0:c0 + cn, mq * 128:(mq + 1) * 128] \
                        .rearrange("(b p) r -> p b r", p=128)
                    src3 = w2tt[:, 0:cn].rearrange("p (b r) -> p b r", r=128)
                    P.wait("gpsimd", "fs2e", ev)
                    tts["st"] = P.dma("gpsimd", dst3, src3, "ft2s")

                def push_units(us):
                    unit_queue.extend(us)
                    while len(pending_loads) < 2 and unit_queue:
                        issue_load(*unit_queue.pop(0))

                if app == 0:
                    w1_units = [("w1", rbase + s4 * 128) for n in range(nch)
                                for rbase in (n * 512, FF + n * 512)
                                for s4 in range(4)]
                    w2_units = [("w2", (mq, c0)) for mq in range(NMT)
                                for c0 in (0, 2048, 4096)]
                    upfront = w1_units[0:16]
                    iter_units = []
                    for n in range(nch):
                        if n < 9:
                            w2b = (w2_units[n * 5:(n + 1) * 5] if n < 8
                                   else w2_units[40:48])
                            iter_units.append(
                                w1_units[16 + n * 8:24 + n * 8] + w2b)
                        else:
                            iter_units.append([])
                else:
                    upfront = []
                    iter_units = [[] for _ in range(nch)]

                def dmaT_batch(k):
                    bn = k % 2
                    if app == 0:
                        P.wait("sync", f"{st}w0")
                        P.wait("sync", f"{st}w1")
                    P.wait("sync", ts_, ts_base + 16 * (k - 1))
                    for ft in range(16):
                        P.dmaT("sync", w1tg[bn][:, ft * 512:(ft + 1) * 512],
                               w1q_scr[k * 512:(k + 1) * 512, ft * 128:(ft + 1) * 128],
                               f"{tp}w{bn}")
                        P.dmaT("sync", w1tu[bn][:, ft * 512:(ft + 1) * 512],
                               w1q_scr[FF + k * 512:FF + (k + 1) * 512,
                                       ft * 128:(ft + 1) * 128], f"{tp}w{bn}")
                    return P.val(f"{tp}w{bn}")

                def emit_hidq(m):
                    P.op("vector", lambda e, m=m: e.tensor_scalar(
                        t1[:], sinv[m][:], w1m[:, 0:1], None, ALU.mult))
                    P.op("vector", lambda e, m=m: e.tensor_tensor(
                        m2q[m][:], t1[:], t1[:], ALU.mult))
                    P.op("vector", lambda e, m=m: e.tensor_tensor(
                        m2q[m][:], m2q[m][:], t1[:], ALU.mult))
                    P.op("vector", lambda e, m=m: e.tensor_tensor(
                        t1[:], amx[m][:], m2q[m][:], ALU.mult))
                    P.op("vector", lambda e: e.tensor_scalar(
                        t1[:], t1[:], 1e-5, None, ALU.max))
                    P.op("vector", lambda e, m=m: e.tensor_scalar(
                        fin[m][:], t1[:], 1.0 / 127.0, None, ALU.mult))
                    P.op("vector", lambda e, m=m: e.tensor_scalar(
                        fin[m][:], fin[m][:], w2m[:, 0:1], 0.5, ALU.mult, ALU.mult))
                    P.op("vector", lambda e: e.reciprocal(t1[:], t1[:]))
                    P.op("vector", lambda e: e.tensor_scalar(
                        t1[:], t1[:], 127.0, None, ALU.mult))
                    P.op("vector", lambda e, m=m: e.tensor_tensor(
                        m2q[m][:], m2q[m][:], t1[:], ALU.mult))
                    for hh in range(2):
                        c0 = hh * (FF // 2)
                        jh, hbuf = ht.next(P, "sync")
                        P.wait("sync", f"{st}h0")
                        P.wait("sync", f"{st}h1")
                        lv = P.dma("sync", hbuf[:],
                                   hid_scr[m * 128:(m + 1) * 128, c0:c0 + FF // 2],
                                   ld + "h")
                        P.wait("vector", ld + "h", lv)
                        P.op("vector", lambda e, hbuf=hbuf, m=m: e.tensor_scalar(
                            hbuf[:], hbuf[:], m2q[m][:, 0:1], 384.5, ALU.mult, ALU.add))
                        hv = P.op("vector", lambda e, hbuf=hbuf: e.tensor_scalar(
                            hbuf[:].bitcast(I32), hbuf[:].bitcast(I32), maski[:, 0:1],
                            None, ALU.bitwise_and), sem=vs)
                        jb, bbuf = hb.next(P, "scalar")
                        P.wait("scalar", vs, hv)
                        av = P.op("scalar", lambda e, hbuf=hbuf, bbuf=bbuf: e.activation(
                            bbuf[:], hbuf[:], AF.Identity, bias=cm384[:, 0:1]), sem=ss)
                        ht.gate(jh, ss, av)
                        sv = P.dma("scalar",
                                   hq_scr[m * 128:(m + 1) * 128, c0:c0 + FF // 2],
                                   bbuf[:], f"{st}q{jb}")
                        hb.gate(jb, f"{st}q{jb}", sv)
                        hq_sv[m] = sv

                tp_chunk = [0] * nch
                push_units(upfront)
                for _ in upfront:
                    emit_quant_one()
                tp_chunk[0] = dmaT_batch(0)
                if nch > 1:
                    tp_chunk[1] = dmaT_batch(1)
                for n in range(nch):
                    bn = n % 2
                    units = iter_units[n]
                    push_units(units)
                    if not _INTERLEAVE_Q:
                        for _ in units:
                            emit_quant_one()
                    P.wait("tensor", f"{tp}w{bn}", tp_chunk[n])
                    if n == 0:
                        P.wait("tensor", tp + "x")
                    for m in range(NMT):
                        bm = mm_iter % 2
                        if mm_iter >= 2:
                            P.wait("tensor", pf, pf_base + mm_iter - 1)
                        for kt in range(16):
                            P.op("tensor", lambda e, bn=bn, bm=bm, m=m, kt=kt: e.matmul(
                                pg[bm][:],
                                xiT[:, kt * TOK + m * 128: kt * TOK + m * 128 + 128],
                                w1tg[bn][:, kt * 512:(kt + 1) * 512],
                                start=(kt == 0), stop=(kt == 15)))
                        tvv = 0
                        for kt in range(16):
                            tvv = P.op("tensor", lambda e, bn=bn, bm=bm, m=m, kt=kt:
                                       e.matmul(
                                           pu[bm][:],
                                           xiT[:, kt * TOK + m * 128:
                                               kt * TOK + m * 128 + 128],
                                           w1tu[bn][:, kt * 512:(kt + 1) * 512],
                                           start=(kt == 0), stop=(kt == 15)),
                                       sem=(ts_ if kt == 15 else None))
                        if _ACT_EVAC:
                            if mm_iter >= 2:
                                P.wait("scalar", pf, pf_base + mm_iter - 1)
                            P.wait("scalar", ts_, tvv)
                            P.op("scalar", lambda e, bm=bm: e.activation(
                                rel[bm][:], pg[bm][:], AF.Relu))
                            P.op("scalar", lambda e, bm=bm: e.activation(
                                rl2[bm][:], rel[bm][:], AF.Square), sem=ss + "x")
                            P.wait("vector", ss + "x", ssx2_base + mm_iter + 1)
                        else:
                            P.wait("vector", ts_, tvv)
                            P.op("vector", lambda e, bm=bm: e.tensor_scalar(
                                rel[bm][:], pg[bm][:], 0.0, None, ALU.max))
                            P.op("vector", lambda e, bm=bm: e.tensor_tensor(
                                rl2[bm][:], rel[bm][:], rel[bm][:], ALU.mult))
                        jh, hbuf = hid.next(P, "vector")
                        P.op("vector", lambda e, bm=bm, hbuf=hbuf: e.tensor_tensor(
                            hbuf[:], rl2[bm][:], pu[bm][:], ALU.mult), sem=pf)
                        P.op("vector", lambda e, hbuf=hbuf: e.reduce_max(
                            red1[:], hbuf[:], axis=AX.X, apply_absolute_value=True))
                        hv = P.op("vector", lambda e, m=m: e.tensor_tensor(
                            amx[m][:], amx[m][:], red1[:], ALU.max), sem=vs + "r")
                        P.wait("scalar", vs + "r", hv)
                        sv = P.dma("scalar",
                                   hid_scr[m * 128:(m + 1) * 128,
                                           n * 512:(n + 1) * 512], hbuf[:],
                                   f"{st}h{jh}")
                        hid.gate(jh, f"{st}h{jh}", sv)
                        hid_sv[m] = sv
                        while len(pending_tt) > 1:
                            emit_tt_one()
                        if _INTERLEAVE_Q and m < len(units):
                            emit_quant_one()
                        if n == nch - 1:
                            emit_hidq(m)
                        mm_iter += 1
                    while pending_tt:
                        emit_tt_one()
                    if n + 2 < nch:
                        tp_chunk[n + 2] = dmaT_batch(n + 2)
                if _DUMP_B and app == 0:
                    dxi = nc.dram_tensor("dbg_xiT", [128, NMT * TOK], BF16,
                                         kind="ExternalOutput").ap()
                    dwt = nc.dram_tensor("dbg_w1t", [128, 2 * 16 * 512], BF16,
                                         kind="ExternalOutput").ap()
                    P.wait("sync", ts_, ts_base + 16 * nch)
                    P.dma("sync", dxi, xiT[:], "dbgB")
                    P.dma("sync", dwt[:, 0:8192], w1tg[(nch - 1) % 2][:], "dbgB")
                    P.dma("sync", dwt[:, 8192:16384], w1tu[(nch - 1) % 2][:], "dbgB")
                    P.wait("sync", "dbgB")
                P.drain("sync", f"{st}h0", f"{st}h1", f"{st}q0", f"{st}q1")
                if app == 0:
                    P.drain("sync", f"{st}w0", f"{st}w1", "ft2s")
                P.barrier()

            with contextlib.ExitStack() as ar:
                NKT = FF // 128
                w2t = [sb(f"{tag}w2t{i}", [128, NKT * 512], BF16, st=ar)
                       for i in range(2)]
                hqT = [sb(f"{tag}hqT{i}", [128, NKT * 512], BF16, st=ar) for i in range(2)]
                xres = Rot([sb(f"{tag}xr{i}", [128, 512], st=ar) for i in range(2)])
                stg = Rot([sb(f"{tag}sg{i}", [128, 512], st=ar) for i in range(2)])
                pd = [pst(f"{tag}pd{i}", [128, 512], ar) for i in range(4)]
                pfd_base = P.val(pf)
                tsd_base = P.val(ts_)

                def w2t_load(g):
                    qi, bw = g % 4, g % 2
                    P.wait("sync", ts_, tsd_base + g - 1)
                    src3 = w2tT_scr[:, qi * 512:(qi + 1) * 512] \
                        .rearrange("(f p) r -> p f r", p=128)
                    dst3 = w2t[bw][:].rearrange("p (f r) -> p f r", f=NKT)
                    P.dma("sync", dst3, src3, f"{tp}2{bw}")
                    return P.val(f"{tp}2{bw}")

                def hqT_part(mg, part):
                    bg = mg % 2
                    P.wait("sync", ts_, tsd_base + 4 * mg - 4)
                    for ft in range(part * 11, part * 11 + 11):
                        P.dmaT("sync", hqT[bg][:, ft * 512:(ft + 1) * 512],
                               hq_scr[mg * 512:(mg + 1) * 512,
                                      ft * 128:(ft + 1) * 128], f"{tp}h{bg}")
                    return P.val(f"{tp}h{bg}")

                hq_tpv = [0] * 4
                w2_val = [0] * 16
                for part in range(4):
                    hq_tpv[0] = hqT_part(0, part)
                w2_val[0] = w2t_load(0)
                w2_val[1] = w2t_load(1)
                gidx = 0
                for mg in range(4):
                    bg = mg % 2
                    for qi in range(4):
                        bw = gidx % 2
                        if mg + 1 < 4:
                            hq_tpv[mg + 1] = hqT_part(mg + 1, qi)
                        if gidx + 2 < 16:
                            w2_val[gidx + 2] = w2t_load(gidx + 2)
                        P.wait("tensor", f"{tp}h{bg}", hq_tpv[mg])
                        P.wait("tensor", f"{tp}2{bw}", w2_val[gidx])
                        if gidx > 0:
                            P.wait("tensor", pf, pfd_base + gidx * 4)
                        tvv = 0
                        for ft in range(NKT):
                            for mi in range(4):
                                last = (ft == NKT - 1 and mi == 3)
                                tvv = P.op("tensor",
                                           lambda e, bg=bg, ft=ft, mi=mi, bw=bw: e.matmul(
                                               pd[mi][:],
                                               hqT[bg][:, ft * 512 + mi * 128:
                                                        ft * 512 + mi * 128 + 128],
                                               w2t[bw][:, ft * 512:(ft + 1) * 512],
                                               start=(ft == 0), stop=(ft == NKT - 1)),
                                           sem=(ts_ if last else None))
                        for mi in range(4):
                            m = mg * 4 + mi
                            jx, xbuf = xres.next(P, "sync")
                            lv = P.dma("sync", xbuf[:],
                                       src[m * 128:(m + 1) * 128,
                                           qi * 512:(qi + 1) * 512], f"{ld}x{jx}")
                            js, sbuf_ = stg.next(P, "vector")
                            P.wait("vector", ts_, tvv)
                            P.wait("vector", f"{ld}x{jx}", lv)
                            xv = P.op("vector",
                                      lambda e, mi=mi, m=m, xbuf=xbuf, sbuf_=sbuf_:
                                      e.scalar_tensor_tensor(
                                          sbuf_[:], pd[mi][:], fin[m][:, 0:1], xbuf[:],
                                          ALU.mult, ALU.add), sem=pf)
                            xres.gate(jx, pf, P.val(pf))
                            P.wait("scalar", pf, P.val(pf))
                            sv = P.dma("scalar",
                                       dst[m * 128:(m + 1) * 128, qi * 512:(qi + 1) * 512],
                                       sbuf_[:], f"{st}d{js}")
                            stg.gate(js, f"{st}d{js}", sv)
                        gidx += 1
                P.drain("sync", f"{st}d0", f"{st}d1")
                P.barrier()

        def phase2():
            with contextlib.ExitStack() as outer:
                yT = sb("p2yT", [128, NMT * TOK], st=outer)
                with contextlib.ExitStack() as ar:
                    xt = Rot([sb(f"p2xt{i}", [128, DM], st=ar) for i in range(2)])
                    yt = [sb(f"p2yt{i}", [128, DM], st=ar) for i in range(2)]
                    sqt = [sb(f"p2sq{i}", [128, DM], st=ar) for i in range(2)]
                    tmp = [sb(f"p2tmp{i}", [128, 1], st=ar) for i in range(4)]
                    tpp = [pst(f"p2tp{i}", [128, 512], ar) for i in range(2)]
                    yT_r = yT[:].rearrange("p (k t) -> p k t", k=NMT)
                    for m in range(NMT):
                        j, xbuf = xt.next(P, "sync")
                        lv = P.dma("sync", xbuf[:], x1_scr[m * 128:(m + 1) * 128, :],
                                   f"p2ld{j}")
                        b = m % 2
                        P.wait("vector", f"p2ld{j}", lv)
                        if m >= 2:
                            P.wait("vector", "p2tv", (m - 1) * 4)
                        emit_ln(xbuf[:], yt[b][:], sqt[b][:], tmp, "p2vs", "p2ss")
                        yv = P.op("vector", lambda e: e.nop(), sem="p2y")
                        xt.gate(j, "p2y", yv)
                        P.wait("tensor", "p2y", yv)
                        for g in range(4):
                            gi = m * 4 + g
                            bg = gi % 2
                            if gi >= 2:
                                P.wait("tensor", "p2tv", gi - 1)
                            for t4 in range(4):
                                ft = g * 4 + t4
                                P.op("tensor", lambda e, b=b, bg=bg, ft=ft, t4=t4: e.transpose(
                                    tpp[bg][:, t4 * 128:(t4 + 1) * 128],
                                    yt[b][:, ft * 128:(ft + 1) * 128], ident[:]),
                                    sem=("p2tt" if t4 == 3 else None))
                            P.wait("vector", "p2tt")
                            P.op("vector", lambda e, bg=bg, g=g, m=m: e.tensor_copy(
                                yT_r[:, g * 4:(g + 1) * 4, m * 128:(m + 1) * 128],
                                tpp[bg][:].rearrange("p (f t) -> p f t", f=4)), sem="p2tv")
                    P.barrier()
                with contextlib.ExitStack() as ar:
                    wf = Rot([sb(f"p2wf{i}", [128, DM], st=ar) for i in range(2)])
                    Wt = sb("p2Wt", [128, 16 * 512], st=ar)
                    stg = Rot([sb(f"p2sg{i}", [128, 512], st=ar) for i in range(2)])
                    st2 = Rot([sb(f"p2s2{i}", [128, 512], st=ar) for i in range(2)])
                    tpp = [pst(f"p2tq{i}", [128, 512], ar) for i in range(2)]
                    mmp = [pst(f"p2mm{i}", [128, 512], ar) for i in range(2)]
                    Wt_r = Wt[:].rearrange("p (k r) -> p k r", k=16)
                    qT_r = qT_scr.rearrange("(b p) t -> p b t", p=128)
                    kT_r = kT_scr.rearrange("(b p) t -> p b t", p=128)
                    tpi = 0
                    mmi = 0
                    for c in range(16):
                        if c > 0:
                            P.wait("vector", "p2mt", mmi)
                        wv = 0
                        for s4 in range(4):
                            j, wbuf = wf.next(P, "sync")
                            lv = P.dma("sync", wbuf[:],
                                       w_attn[c * 512 + s4 * 128:
                                              c * 512 + (s4 + 1) * 128, :], f"p2lw{j}")
                            P.wait("tensor", f"p2lw{j}", lv)
                            for g in range(4):
                                bg = tpi % 2
                                if tpi >= 2:
                                    P.wait("tensor", "p2wv", tpi - 1)
                                for t4 in range(4):
                                    kt = g * 4 + t4
                                    P.op("tensor",
                                         lambda e, wbuf=wbuf, bg=bg, kt=kt, t4=t4:
                                         e.transpose(tpp[bg][:, t4 * 128:(t4 + 1) * 128],
                                                     wbuf[:, kt * 128:(kt + 1) * 128],
                                                     ident[:]),
                                         sem=("p2wt" if t4 == 3 else None))
                                P.wait("vector", "p2wt")
                                wv = P.op("vector", lambda e, bg=bg, g=g, s4=s4: e.tensor_copy(
                                    Wt_r[:, g * 4:(g + 1) * 4, s4 * 128:(s4 + 1) * 128],
                                    tpp[bg][:].rearrange("p (f t) -> p f t", f=4)),
                                    sem="p2wv")
                                tpi += 1
                            wf.gate(j, "p2wv", wv)
                        P.wait("tensor", "p2wv", wv)
                        for m in range(NMT):
                            bm = mmi % 2
                            if mmi >= 2:
                                P.wait("tensor", "p2pf", mmi - 1)
                            tvv = 0
                            for kt in range(16):
                                tvv = P.op("tensor", lambda e, bm=bm, m=m, kt=kt: e.matmul(
                                    mmp[bm][:],
                                    yT[:, kt * TOK + m * 128: kt * TOK + m * 128 + 128],
                                    Wt[:, kt * 512:(kt + 1) * 512],
                                    start=(kt == 0), stop=(kt == 15)),
                                    sem=("p2mt" if kt == 15 else None))
                            mmi += 1
                            js, sbuf_ = stg.next(P, "scalar")
                            P.wait("scalar", "p2mt", tvv)
                            func = AF.Silu if c < 12 else AF.Sigmoid
                            fv = P.op("scalar", lambda e, bm=bm, sbuf_=sbuf_, func=func:
                                      e.activation(sbuf_[:], mmp[bm][:], func), sem="p2pf")
                            if c < 12:
                                dstap = qkv_scr[m * 128:(m + 1) * 128, c * 512:(c + 1) * 512]
                            else:
                                dstap = gate_scr[m * 128:(m + 1) * 128,
                                                 (c - 12) * 512:(c - 11) * 512]
                            sv = P.dma("scalar", dstap, sbuf_[:], f"p2st{js}")
                            if c < 8:
                                P.wait("tensor", "p2pf", fv)
                                bg = tpi % 2
                                if tpi >= 2:
                                    P.wait("tensor", "p2wv", tpi - 1)
                                for t4 in range(4):
                                    P.op("tensor", lambda e, sbuf_=sbuf_, bg=bg, t4=t4:
                                         e.transpose(tpp[bg][:, t4 * 128:(t4 + 1) * 128],
                                                     sbuf_[:, t4 * 128:(t4 + 1) * 128],
                                                     ident[:]),
                                         sem=("p2wt" if t4 == 3 else None))
                                qtv = P.val("p2wt")
                                j2, s2buf = st2.next(P, "vector")
                                P.wait("vector", "p2wt", qtv)
                                wv2 = P.op("vector", lambda e, bg=bg, s2buf=s2buf:
                                           e.tensor_copy(s2buf[:], tpp[bg][:]), sem="p2wv")
                                tpi += 1
                                P.wait("sync", "p2wv", wv2)
                                scr_r = qT_r if c < 4 else kT_r
                                cc = c if c < 4 else c - 4
                                sv2 = P.dma("sync",
                                            scr_r[:, cc * 4:(cc + 1) * 4,
                                                  m * 128:(m + 1) * 128],
                                            s2buf[:].rearrange("p (f t) -> p f t", f=4),
                                            f"p2s2{j2}")
                                st2.gate(j2, f"p2s2{j2}", sv2)
                                stg.gate(js, f"p2st{js}", sv)
                            else:
                                stg.gate(js, f"p2st{js}", sv)
                    P.drain("sync", "p2st0", "p2st1", "p2s20", "p2s21")
                    P.barrier()

        def phase3a():
            kv_r = lambda i, c0, c1: qkv_scr[i * 256:(i + 1) * 256, c0:c1].rearrange(
                "(j p) c -> p j c", p=128)
            d3 = lambda ap, n: ap.rearrange("p (j c) -> p j c", j=n)
            with contextlib.ExitStack() as ar:
                kb = Rot([sb(f"p3ak{i}", [128, 2 * DM], st=ar) for i in range(2)])
                vb = Rot([sb(f"p3av{i}", [128, 2 * DM], st=ar) for i in range(2)])
                kk = Rot([sb(f"p3akk{i}", [128, 64], st=ar) for i in range(2)])
                Sloc = sb("p3aS", [64, H * D], st=ar)
                gs = Rot([sb(f"p3gs{i}", [64, H * D], st=ar) for i in range(2)])
                tpb = Rot([sb(f"p3tp{i}", [64, H * D], st=ar) for i in range(2)])
                cps = [pst(f"p3acp{i}", [64, 64], ar) for i in range(2)]
                P.op("vector", lambda e: e.memset(Sloc[:], 0.0))
                n_sv = 0
                for i in range(NBLK):
                    jk, kbuf = kb.next(P, "sync")
                    kv1 = P.dma("sync", d3(kbuf[:], 2), kv_r(i, DM, 2 * DM),
                                f"p3ak{jk}")
                    jv, vbuf = vb.next(P, "sync")
                    kv2 = P.dma("sync", d3(vbuf[:], 2), kv_r(i, 2 * DM, 3 * DM),
                                f"p3av{jv}")
                    P.wait("vector", f"p3ak{jk}", kv1)
                    P.wait("vector", f"p3av{jv}", kv2)
                    P.wait("tensor", f"p3av{jv}", kv2)
                    for hh in range(H):
                        bp = hh % 2
                        if n_sv >= 2:
                            P.wait("tensor", "p3sv", n_sv - 1)
                        tv = 0
                        for jh in range(2):
                            jkk, kkb = kk.next(P, "vector")
                            kv_ = P.op("vector",
                                       lambda e, kbuf=kbuf, hh=hh, jh=jh, kkb=kkb:
                                       e.tensor_scalar(
                                           kkb[:],
                                           kbuf[:, jh * DM + hh * 64: jh * DM + hh * 64 + 64],
                                           kdc[:, hh * 2 + jh: hh * 2 + jh + 1],
                                           None, ALU.mult), sem="p3kv")
                            P.wait("tensor", "p3kv", kv_)
                            tv = P.op("tensor",
                                      lambda e, bp=bp, kkb=kkb, vbuf=vbuf, hh=hh, jh=jh:
                                      e.matmul(cps[bp][:], kkb[:],
                                               vbuf[:, jh * DM + hh * 64:
                                                    jh * DM + hh * 64 + 64],
                                               start=(jh == 0), stop=(jh == 1)),
                                      sem=("p3tc" if jh == 1 else None))
                            kk.gate(jkk, "p3tc", P.val("p3tc") if jh == 1 else
                                    P.val("p3tc") + 1)
                        P.wait("vector", "p3tc", tv)
                        sv_ = P.op("vector", lambda e, bp=bp, hh=hh: e.scalar_tensor_tensor(
                            Sloc[:, hh * 64:(hh + 1) * 64], Sloc[:, hh * 64:(hh + 1) * 64],
                            float(_BD[hh]), cps[bp][:], ALU.mult, ALU.add), sem="p3sv")
                        n_sv += 1
                    kb.gate(jk, "p3kv", P.val("p3kv"))
                    vb.gate(jv, "p3tc", P.val("p3tc"))
                P.wait("scalar", "p3sv")
                sv = P.dma("scalar", cc_sin, Sloc[:], "p3st")
                P.wait("gpsimd", "p3st", sv)
                P.op("gpsimd", lambda e: e.collective_compute(
                    "AllGather", ALU.bypass, replica_groups=[list(range(n_rg))],
                    ins=[cc_sin.opt()],
                    outs=[(cc_sout[0:64 * n_rg, :] if n_rg != NCORES else cc_sout).opt()]),
                    sem="p3cc")
                P.op("vector", lambda e: e.memset(Sin_st[:], 0.0))
                P.wait("sync", "p3cc")
                for cp in range(n_rg):
                    jg, gbuf = gs.next(P, "sync")
                    gv1 = P.dma("sync", gbuf[:], cc_sout[cp * 64:(cp + 1) * 64, :],
                                f"p3ag{jg}")
                    jt, tbuf = tpb.next(P, "sync")
                    gv2 = P.dma("sync", tbuf[:], tpred_in[cp], f"p3at{jt}")
                    P.wait("vector", f"p3ag{jg}", gv1)
                    P.wait("vector", f"p3at{jt}", gv2)
                    P.op("vector", lambda e, gbuf=gbuf, tbuf=tbuf: e.tensor_tensor(
                        gbuf[:], gbuf[:], tbuf[:], ALU.mult))
                    gv = P.op("vector", lambda e, gbuf=gbuf: e.tensor_tensor(
                        Sin_st[:], Sin_st[:], gbuf[:], ALU.add), sem="p3gv")
                    gs.gate(jg, "p3gv", gv)
                    tpb.gate(jt, "p3gv", gv)
                P.barrier()

        def phase3b():
            kv_r = lambda i, c0, c1: qkv_scr[i * 256:(i + 1) * 256, c0:c1].rearrange(
                "(j p) c -> p j c", p=128)
            qT_sl = lambda i: qT_scr[:, i * 256:(i + 1) * 256].rearrange(
                "(t p) c -> p t c", p=128)
            kT_sl = lambda i: kT_scr[:, i * 256:(i + 1) * 256].rearrange(
                "(t p) c -> p t c", p=128)
            d3 = lambda ap, n: ap.rearrange("p (j c) -> p j c", j=n)
            with contextlib.ExitStack() as ar:
                dd0 = sb("p3dd0", [128, H * 256], st=ar)
                qdb = sb("p3qdb", [128, (H // 2) * 256], st=ar)
                kb = Rot([sb("p3k0", [128, 2 * DM], st=ar)])
                vb = Rot([sb("p3v0", [128, 2 * DM], st=ar)])
                qs = Rot([sb("p3q0", [128, NMT * 256], st=ar)])
                ks = Rot([sb("p3kT0", [128, NMT * 256], st=ar)])
                kk = Rot([sb(f"p3kk{i}", [128, 64], st=ar) for i in range(2)])
                qq = Rot([sb(f"p3qq{i}", [64, 256], st=ar) for i in range(2)])
                aw0 = Rot([sb(f"p3a0{i}", [128, 256], st=ar) for i in range(2)])
                aw1 = Rot([sb(f"p3a1{i}", [128, 128], st=ar) for i in range(2)])
                asm = Rot([sb(f"p3asm{i}", [128, 2 * DM], st=ar) for i in range(2)])
                Scur = [sb(f"p3S{i}", [64, H * D], st=ar) for i in range(2)]
                pa0 = [pst(f"p3pa0{i}", [128, 256], ar) for i in range(2)]
                pa1 = [pst(f"p3pa1{i}", [128, 128], ar) for i in range(2)]
                pi0 = pst("p3pi0", [128, 64], ar)
                pi1 = pst("p3pi1", [128, 64], ar)
                cps = [pst(f"p3cp{i}", [64, 64], ar) for i in range(2)]
                P.dma("sync", dd0[:], dd0_in, "p3c")
                P.dma("sync", qdb[:], qdb_in, "p3c")
                P.wait("vector", "p3c")
                P.wait("vector", "p3gv", P.val("p3gv"))
                P.op("vector", lambda e: e.tensor_copy(Scur[0][:], Sin_st[:]), sem="p3sv2")
                awi = 0
                n_sv = P.val("p3sv2")
                n_ie = 0
                sv_prev_block = P.val("p3sv2")
                for i in range(NBLK):
                    Sa, Sb = Scur[i % 2], Scur[(i + 1) % 2]
                    jk, kbuf = kb.next(P, "sync")
                    b1 = P.dma("sync", d3(kbuf[:], 2), kv_r(i, DM, 2 * DM), "p3bk")
                    jv, vbuf = vb.next(P, "sync")
                    b2 = P.dma("sync", d3(vbuf[:], 2), kv_r(i, 2 * DM, 3 * DM), "p3bv")
                    jq, qsb = qs.next(P, "sync")
                    b3 = P.dma("sync", d3(qsb[:], NMT), qT_sl(i), "p3bq")
                    jkt, ksb = ks.next(P, "sync")
                    b4 = P.dma("sync", d3(ksb[:], NMT), kT_sl(i), "p3bt")
                    for s_, v_ in (("p3bk", b1), ("p3bv", b2), ("p3bq", b3),
                                   ("p3bt", b4)):
                        P.wait("tensor", s_, v_)
                        P.wait("vector", s_, v_)
                    P.wait("tensor", "p3sv2", sv_prev_block)
                    ja, abuf = asm.next(P, "vector")
                    for hh in range(H):
                        p0 = 64 * (hh % 2)
                        hcol = (hh // 2) * 256
                        ba = awi % 2
                        if awi >= 2:
                            P.wait("tensor", "p3de", awi - 1)
                        P.op("tensor", lambda e, ba=ba, ksb=ksb, qsb=qsb, p0=p0, hcol=hcol:
                             e.matmul(pa0[ba][:], ksb[p0:p0 + 64, hcol:hcol + 128],
                                      qsb[p0:p0 + 64, hcol:hcol + 256],
                                      start=True, stop=True))
                        tv = P.op("tensor", lambda e, ba=ba, ksb=ksb, qsb=qsb, p0=p0, hcol=hcol:
                                  e.matmul(pa1[ba][:], ksb[p0:p0 + 64, hcol + 128:hcol + 256],
                                           qsb[p0:p0 + 64, hcol + 128:hcol + 256],
                                           start=True, stop=True), sem="p3aw")
                        ja0, a0b = aw0.next(P, "vector")
                        ja1, a1b = aw1.next(P, "vector")
                        P.wait("vector", "p3aw", tv)
                        P.op("vector", lambda e, ba=ba, a0b=a0b, hh=hh: e.tensor_tensor(
                            a0b[:], pa0[ba][:], dd0[:, hh * 256:(hh + 1) * 256], ALU.mult))
                        av = P.op("vector", lambda e, ba=ba, a1b=a1b, hh=hh: e.tensor_tensor(
                            a1b[:], pa1[ba][:], dd0[:, hh * 256:hh * 256 + 128], ALU.mult),
                            sem="p3de")
                        jqq, qqb = qq.next(P, "vector")
                        qv = P.op("vector",
                                  lambda e, qqb=qqb, qsb=qsb, qdb=qdb, p0=p0, hcol=hcol:
                                  e.tensor_tensor(qqb[:], qsb[p0:p0 + 64, hcol:hcol + 256],
                                                  qdb[p0:p0 + 64, hcol:hcol + 256],
                                                  ALU.mult), sem="p3qe")
                        P.wait("tensor", "p3de", av)
                        P.wait("tensor", "p3qe", qv)
                        if n_ie >= 1:
                            P.wait("tensor", "p3ie", n_ie)
                        P.op("tensor", lambda e, a0b=a0b, vbuf=vbuf, hh=hh: e.matmul(
                            pi0[:], a0b[:, 0:128], vbuf[:, hh * 64: hh * 64 + 64],
                            start=True, stop=False))
                        P.op("tensor", lambda e, qqb=qqb, Sa=Sa, hh=hh: e.matmul(
                            pi0[:], qqb[:, 0:128], Sa[:, hh * 64:(hh + 1) * 64],
                            start=False, stop=True), sem="p3it")
                        P.op("tensor", lambda e, a0b=a0b, vbuf=vbuf, hh=hh: e.matmul(
                            pi1[:], a0b[:, 128:256], vbuf[:, hh * 64: hh * 64 + 64],
                            start=True, stop=False))
                        P.op("tensor", lambda e, a1b=a1b, vbuf=vbuf, hh=hh: e.matmul(
                            pi1[:], a1b[:], vbuf[:, DM + hh * 64: DM + hh * 64 + 64],
                            start=False, stop=False))
                        itv = P.op("tensor", lambda e, qqb=qqb, Sa=Sa, hh=hh: e.matmul(
                            pi1[:], qqb[:, 128:256], Sa[:, hh * 64:(hh + 1) * 64],
                            start=False, stop=True), sem="p3it")
                        aw0.gate(ja0, "p3it", itv)
                        aw1.gate(ja1, "p3it", itv)
                        qq.gate(jqq, "p3it", itv)
                        bp = hh % 2
                        if n_sv >= 2:
                            P.wait("tensor", "p3sv2", n_sv - 1)
                        ctv = 0
                        for jh in range(2):
                            jkk, kkb = kk.next(P, "vector")
                            kv_ = P.op("vector",
                                       lambda e, kbuf=kbuf, hh=hh, jh=jh, kkb=kkb:
                                       e.tensor_scalar(
                                           kkb[:],
                                           kbuf[:, jh * DM + hh * 64: jh * DM + hh * 64 + 64],
                                           kdc[:, hh * 2 + jh: hh * 2 + jh + 1],
                                           None, ALU.mult), sem="p3kv2")
                            P.wait("tensor", "p3kv2", kv_)
                            ctv = P.op("tensor",
                                       lambda e, bp=bp, kkb=kkb, vbuf=vbuf, hh=hh, jh=jh:
                                       e.matmul(cps[bp][:], kkb[:],
                                                vbuf[:, jh * DM + hh * 64:
                                                     jh * DM + hh * 64 + 64],
                                                start=(jh == 0), stop=(jh == 1)),
                                       sem=("p3tc2" if jh == 1 else None))
                            kk.gate(jkk, "p3tc2", P.val("p3tc2") if jh == 1 else
                                    P.val("p3tc2") + 1)
                        P.wait("vector", "p3it", itv)
                        P.op("vector", lambda e, abuf=abuf, hh=hh: e.tensor_copy(
                            abuf[:, hh * 64:(hh + 1) * 64], pi0[:]))
                        P.op("vector", lambda e, abuf=abuf, hh=hh: e.tensor_copy(
                            abuf[:, DM + hh * 64: DM + hh * 64 + 64], pi1[:]), sem="p3ie")
                        n_ie += 1
                        P.wait("vector", "p3tc2", ctv)
                        P.op("vector", lambda e, Sa=Sa, Sb=Sb, bp=bp, hh=hh:
                             e.scalar_tensor_tensor(
                                 Sb[:, hh * 64:(hh + 1) * 64], Sa[:, hh * 64:(hh + 1) * 64],
                                 float(_BD[hh]), cps[bp][:], ALU.mult, ALU.add), sem="p3sv2")
                        n_sv += 1
                        awi += 1
                    sv_prev_block = P.val("p3sv2")
                    kb.gate(jk, "p3kv2", P.val("p3kv2"))
                    vb.gate(jv, "p3tc2", P.val("p3tc2"))
                    qs.gate(jq, "p3qe", P.val("p3qe"))
                    ks.gate(jkt, "p3aw", P.val("p3aw"))
                    P.wait("scalar", "p3ie", P.val("p3ie"))
                    P.dma("scalar", attn_scr[i * 256:i * 256 + 128, :],
                          abuf[:, 0:DM], f"p3s{ja}")
                    sv2 = P.dma("scalar", attn_scr[i * 256 + 128:(i + 1) * 256, :],
                                abuf[:, DM:2 * DM], f"p3s{ja}")
                    asm.gate(ja, f"p3s{ja}", sv2)
                P.drain("sync", "p3s0", "p3s1")
                P.barrier()

        def phase4():
            b_tv = P.val("p2tv")
            b_z = P.val("p2z")
            b_wv = P.val("p2wv")
            b_mt = P.val("p2mt")
            b_pf = P.val("p2pf")
            with contextlib.ExitStack() as outer:
                zT = sb("p4zT", [128, NMT * TOK], st=outer)
                with contextlib.ExitStack() as ar:
                    at = Rot([sb("p4at0", [128, DM], st=ar)])
                    gt = Rot([sb("p4gt0", [128, DM], st=ar)])
                    awb = sb("p4awb", [128, DM], st=ar)
                    P.dma("sync", awb[:], attnw_b, "p2lw0")
                    P.wait("vector", "p2lw0")
                    sqt = [sb(f"p4sq{i}", [128, DM], st=ar) for i in range(2)]
                    zt = [sb(f"p4zt{i}", [128, DM], st=ar) for i in range(2)]
                    tmp = [sb(f"p4tmp{i}", [128, 1], st=ar) for i in range(4)]
                    tpp = [pst(f"p4tp{i}", [128, 512], ar) for i in range(2)]
                    zT_r = zT[:].rearrange("p (k t) -> p k t", k=NMT)
                    for m in range(NMT):
                        ja, abuf = at.next(P, "sync")
                        a1 = P.dma("sync", abuf[:], attn_scr[m * 128:(m + 1) * 128, :],
                                   "p2ld0")
                        jg, gbuf = gt.next(P, "sync")
                        g1 = P.dma("sync", gbuf[:], gate_scr[m * 128:(m + 1) * 128, :],
                                   "p2ld1")
                        b = m % 2
                        P.wait("scalar", "p2ld0", a1)
                        P.wait("vector", "p2ld1", g1)
                        if m >= 2:
                            P.wait("scalar", "p2z", b_z + m - 1)
                            P.wait("vector", "p2tv", b_tv + (m - 1) * 4)
                        P.op("scalar", lambda e, b=b, abuf=abuf: e.activation(
                            sqt[b][:], abuf[:], AF.Square), sem="p2ss")
                        P.wait("vector", "p2ss")
                        P.op("vector", lambda e, b=b: e.reduce_sum(
                            tmp[0][:], sqt[b][:], axis=AX.X), sem="p2vs")
                        P.wait("scalar", "p2vs")
                        P.op("scalar", lambda e: e.activation(
                            tmp[2][:], tmp[0][:], AF.Sqrt,
                            scale=cinvd[:, 0:1], bias=ceps[:, 0:1]), sem="p2ss")
                        P.wait("vector", "p2ss")
                        P.op("vector", lambda e: e.reciprocal(tmp[3][:], tmp[2][:]))
                        P.op("vector", lambda e, b=b, abuf=abuf: e.tensor_scalar(
                            zt[b][:], abuf[:], tmp[3][:, 0:1], None, ALU.mult))
                        av = P.op("vector", lambda e, b=b: e.tensor_tensor(
                            zt[b][:], zt[b][:], awb[:], ALU.mult), sem="p2y")
                        at.gate(ja, "p2y", av)
                        gv = P.op("vector", lambda e, b=b, gbuf=gbuf: e.tensor_tensor(
                            zt[b][:], zt[b][:], gbuf[:], ALU.mult), sem="p2z")
                        gt.gate(jg, "p2z", gv)
                        P.wait("tensor", "p2z", gv)
                        for g in range(4):
                            gi = m * 4 + g
                            bg = gi % 2
                            if gi >= 2:
                                P.wait("tensor", "p2tv", b_tv + gi - 1)
                            for t4 in range(4):
                                ft = g * 4 + t4
                                P.op("tensor", lambda e, b=b, bg=bg, ft=ft, t4=t4:
                                     e.transpose(tpp[bg][:, t4 * 128:(t4 + 1) * 128],
                                                 zt[b][:, ft * 128:(ft + 1) * 128],
                                                 ident[:]),
                                     sem=("p2tt" if t4 == 3 else None))
                            P.wait("vector", "p2tt")
                            P.op("vector", lambda e, bg=bg, g=g, m=m: e.tensor_copy(
                                zT_r[:, g * 4:(g + 1) * 4, m * 128:(m + 1) * 128],
                                tpp[bg][:].rearrange("p (f t) -> p f t", f=4)), sem="p2tv")
                    P.barrier()
                with contextlib.ExitStack() as ar:
                    wf = Rot([sb(f"p4wf{i}", [128, DM], st=ar) for i in range(2)])
                    Wt = sb("p4Wt", [128, 16 * 256], st=ar)
                    stg = Rot([sb(f"p4sg{i}", [128, 256], st=ar) for i in range(2)])
                    xr = Rot([sb(f"p4xr{i}", [128, 256], st=ar) for i in range(2)])
                    tpp = [pst(f"p4tq{i}", [128, 512], ar) for i in range(2)]
                    mmp = [pst(f"p4mm{i}", [128, 256], ar) for i in range(2)]
                    Wt_r = Wt[:].rearrange("p (k r) -> p k r", k=16)
                    tpi = 0
                    mmi = 0
                    for c in range(8):
                        if c > 0:
                            P.wait("vector", "p2mt", b_mt + mmi)
                        wv = 0
                        for s4 in range(2):
                            j, wbuf = wf.next(P, "sync")
                            lv = P.dma("sync", wbuf[:],
                                       outw_in[c * 256 + s4 * 128:
                                               c * 256 + (s4 + 1) * 128, :], f"p2lw{j}")
                            P.wait("tensor", f"p2lw{j}", lv)
                            for g in range(4):
                                bg = tpi % 2
                                if tpi >= 2:
                                    P.wait("tensor", "p2wv", b_wv + tpi - 1)
                                for t4 in range(4):
                                    kt = g * 4 + t4
                                    P.op("tensor",
                                         lambda e, wbuf=wbuf, bg=bg, kt=kt, t4=t4:
                                         e.transpose(tpp[bg][:, t4 * 128:(t4 + 1) * 128],
                                                     wbuf[:, kt * 128:(kt + 1) * 128],
                                                     ident[:]),
                                         sem=("p2wt" if t4 == 3 else None))
                                P.wait("vector", "p2wt")
                                wv = P.op("vector", lambda e, bg=bg, g=g, s4=s4:
                                          e.tensor_copy(
                                              Wt_r[:, g * 4:(g + 1) * 4,
                                                   s4 * 128:(s4 + 1) * 128],
                                              tpp[bg][:].rearrange("p (f t) -> p f t", f=4)),
                                          sem="p2wv")
                                tpi += 1
                            wf.gate(j, "p2wv", wv)
                        P.wait("tensor", "p2wv", wv)
                        for m in range(NMT):
                            bm = mmi % 2
                            if mmi >= 2:
                                P.wait("tensor", "p2pf", b_pf + mmi - 1)
                            tvv = 0
                            for kt in range(16):
                                tvv = P.op("tensor", lambda e, bm=bm, m=m, kt=kt: e.matmul(
                                    mmp[bm][:],
                                    zT[:, kt * TOK + m * 128: kt * TOK + m * 128 + 128],
                                    Wt[:, kt * 256:(kt + 1) * 256],
                                    start=(kt == 0), stop=(kt == 15)),
                                    sem=("p2mt" if kt == 15 else None))
                            mmi += 1
                            jx, xbuf = xr.next(P, "sync")
                            lv = P.dma("sync", xbuf[:],
                                       x1_scr[m * 128:(m + 1) * 128,
                                              c * 256:(c + 1) * 256], f"p2ld{jx}")
                            js, sbuf_ = stg.next(P, "vector")
                            P.wait("vector", "p2mt", tvv)
                            P.wait("vector", f"p2ld{jx}", lv)
                            xv = P.op("vector", lambda e, bm=bm, xbuf=xbuf, sbuf_=sbuf_:
                                      e.tensor_tensor(sbuf_[:], mmp[bm][:], xbuf[:],
                                                      ALU.add), sem="p2pf")
                            xr.gate(jx, "p2pf", xv)
                            P.wait("scalar", "p2pf", xv)
                            sv = P.dma("scalar",
                                       x2_scr[m * 128:(m + 1) * 128,
                                              c * 256:(c + 1) * 256], sbuf_[:],
                                       f"p2st{js}")
                            stg.gate(js, f"p2st{js}", sv)
                    P.drain("sync", "p2st0", "p2st1")
                    P.barrier()

        phase0()
        if max_phase >= 1:
            phase_ffn(0, x_in, x1_scr)
        if max_phase >= 2:
            phase2()
        if max_phase >= 3:
            phase3a()
            phase3b()
        if max_phase >= 4:
            phase4()
        if max_phase >= 5:
            phase_ffn(1, x2_scr, y_out)
        else:
            for m in range(NMT):
                P.dma("gpsimd", y_out[m * 128:(m + 1) * 128, :],
                      x1_scr[m * 128:(m + 1) * 128, :], "fcp")
            P.wait("gpsimd", "fcp")
        for t, ap in tap_outs.items():
            if ap is None:
                continue
            P.dma("gpsimd", ap, tap_src[t], "dbg")
        if P.val("dbg") if "dbg" in P.cnt else 0:
            P.wait("gpsimd", "dbg")
        P.barrier()

        block = big.enter_context(nc.Block())
        P.finalize(block)
    return nc


def _stage_inputs(inputs):
    x = np.ascontiguousarray(np.asarray(inputs["L0_attn_hidden_states"],
                                        dtype=np.float32).reshape(B * S, DM))
    qkv_w = np.asarray(inputs["L0_attn_qkv_w"], np.float32)
    out_w = np.ascontiguousarray(np.asarray(inputs["L0_attn_out_w"], np.float32))
    gate_w = np.asarray(inputs["L0_attn_gate_w"], np.float32)
    norm_w = np.asarray(inputs["L0_attn_norm_w"], np.float32)
    w1 = np.ascontiguousarray(np.asarray(inputs["L0_ffn_w_gate_up"], np.float32))
    w2 = np.ascontiguousarray(np.asarray(inputs["L0_ffn_w_down"], np.float32))
    fn = np.asarray(inputs["L0_ffn_norm_weight"], np.float32)

    wat = np.ascontiguousarray(np.concatenate([qkv_w[_perm_qkv()], gate_w], axis=0))
    ffnw_b = np.ascontiguousarray(np.broadcast_to(fn[None, :], (128, DM)))
    attnw_b = np.ascontiguousarray(np.broadcast_to(norm_w[None, :], (128, DM)))
    ident = np.eye(128, dtype=np.float32)

    in_maps = []
    for c in range(NCORES):
        in_maps.append({
            "x_in": np.ascontiguousarray(x[c * TOK:(c + 1) * TOK]),
            "w_attn": wat,
            "outw_in": out_w,
            "w1_in": w1,
            "w2_in": w2,
            "w1_sl": np.ascontiguousarray(w1[c * (F2 // 8):(c + 1) * (F2 // 8)]),
            "w2_sl": np.ascontiguousarray(w2[c * (DM // 8):(c + 1) * (DM // 8)]),
            "ffnw_b": ffnw_b,
            "attnw_b": attnw_b,
            "ident_in": ident,
            "dd0_in": _DD0,
            "qdb_in": _QDB,
            "kdc_in": _KDC,
            "tpred_in": _t_pred(c),
        })
    return in_maps


def kernel(**inputs):
    nc = build_nc()
    in_maps = _stage_inputs(inputs)
    res = run_bass_kernel_spmd(nc, in_maps, core_ids=list(range(NCORES)))
    out = np.concatenate([np.asarray(res.results[c]["y_out"], dtype=np.float32)
                          for c in range(NCORES)], axis=0)
    return out.reshape(B, S, DM)


# version 11
# speedup vs baseline: 1.1096x; 1.0051x over previous
import contextlib
import numpy as np
import ml_dtypes

import concourse.bass as bass
import concourse.mybir as mybir
from concourse.bass_utils import run_bass_kernel_spmd

F32 = mybir.dt.float32
BF16 = mybir.dt.bfloat16
I32 = mybir.dt.int32
AF = mybir.ActivationFunctionType
ALU = mybir.AluOpType
AX = mybir.AxisListType

B, S, DM = 2, 8192, 2048
H, D, BS = 32, 64, 256
FF = 5632
F2 = 2 * FF
NCORES = 8
TOK = (B * S) // NCORES
NBLK = TOK // BS
NMT = TOK // 128
EPS = 1e-5
_ACT_EVAC = True
_INTERLEAVE_Q = True
_DUMP_B = False
N1 = F2 * DM
N2 = DM * FF

ENGS = ("sync", "vector", "scalar", "tensor", "gpsimd")


def _decay():
    h = np.arange(H, dtype=np.float64)
    base = 1.0 / 2.0 ** (8.0 / H)
    slope = (base ** (h + 1.0)) * (1.0 - 0.0 / (32 - 1 + 1e-05) + 1e-05)
    r = np.arange(BS, dtype=np.float64) + 1.0
    qd = np.exp(-slope[:, None] * r[None, :]).astype(np.float32)
    kd = np.exp(-slope[:, None] * (BS - r[None, :])).astype(np.float32)
    bd = np.exp(-slope * BS).astype(np.float32)
    i_ = np.arange(128, dtype=np.float64)
    dif = i_[None, :] - i_[:, None]
    DD0 = np.zeros((128, H * 256), np.float32)
    QDB = np.zeros((128, (H // 2) * 256), np.float32)
    KDC = np.zeros((128, H * 2), np.float32)
    for hh in range(H):
        t0 = np.where(dif >= 0, np.exp(-slope[hh] * dif), 0.0)
        t1 = np.exp(-slope[hh] * (128.0 + dif))
        DD0[:, hh * 256:hh * 256 + 128] = t0.astype(np.float32)
        DD0[:, hh * 256 + 128:hh * 256 + 256] = t1.astype(np.float32)
        p0 = 64 * (hh % 2)
        QDB[p0:p0 + 64, (hh // 2) * 256:(hh // 2) * 256 + 256] = qd[hh][None, :]
        KDC[:, hh * 2] = kd[hh, 0:128]
        KDC[:, hh * 2 + 1] = kd[hh, 128:256]
    return qd, kd, bd, DD0, QDB, KDC


_QD, _KD, _BD, _DD0, _QDB, _KDC = _decay()


def _t_pred(core):
    out = np.zeros((8, 64, H * D), np.float32)
    g, j = core // 4, core % 4
    for cp in range(8):
        if cp // 4 != g or cp % 4 >= j:
            continue
        w = _BD.astype(np.float64) ** ((j - 1 - (cp % 4)) * NBLK)
        for hh in range(H):
            out[cp, :, hh * D:(hh + 1) * D] = np.float32(w[hh])
    return out


def _perm_qkv():
    perm = np.zeros(3 * H * D, np.int64)
    for hh in range(H):
        for d in range(D):
            perm[0 * H * D + hh * D + d] = 192 * hh + d
            perm[1 * H * D + hh * D + d] = 192 * hh + 64 + d
            perm[2 * H * D + hh * D + d] = 192 * hh + 128 + d
    return perm


_CHAIN = ("vector", "scalar", "gpsimd")


class Prog:
    def __init__(self, nc, stack):
        self.nc, self.stack = nc, stack
        self.q = {e: [] for e in ENGS}
        self.h, self.cnt = {}, {}
        self.last = {e: None for e in ENGS}
        self.waited = {e: {} for e in ENGS}

    def sem(self, name):
        if name not in self.h:
            self.h[name] = self.stack.enter_context(self.nc.semaphore(name))
            self.cnt[name] = 0
        return self.h[name]

    def emit(self, eng, fn):
        self.q[eng].append(fn)

    def _wait_raw(self, eng, sem, v):
        if v <= 0:
            return
        if self.waited[eng].get(sem, 0) >= v:
            return
        self.waited[eng][sem] = v
        hh = self.sem(sem)
        self.emit(eng, lambda e, hh=hh, v=v: e.wait_ge(hh, v))

    def _chain_wait(self, eng):
        if self.last[eng] is not None:
            s, v = self.last[eng]
            self._wait_raw(eng, s, v)

    def dma(self, eng, out, in_, sem):
        self._chain_wait(eng)
        hh = self.sem(sem)
        self.cnt[sem] += 16
        self.emit(eng, lambda e, out=out, in_=in_, hh=hh:
                  e.dma_start(out=out, in_=in_).then_inc(hh, 16))
        return self.cnt[sem]

    def dmaT(self, eng, out, in_, sem):
        self._chain_wait(eng)
        hh = self.sem(sem)
        self.cnt[sem] += 16
        self.emit(eng, lambda e, out=out, in_=in_, hh=hh:
                  e.dma_start_transpose(out=out, in_=in_).then_inc(hh, 16))
        return self.cnt[sem]

    def wait(self, eng, sem, val=None):
        self.sem(sem)
        v = self.cnt[sem] if val is None else val
        self._wait_raw(eng, sem, v)

    def op(self, eng, fn, sem=None, k=1):
        if eng in _CHAIN:
            self._chain_wait(eng)
            if sem is None:
                sem = "_ch_" + eng
                k = 1
        if sem is not None:
            hh = self.sem(sem)
            self.cnt[sem] += k
            self.emit(eng, lambda e, fn=fn, hh=hh, k=k: fn(e).then_inc(hh, k))
            if eng in _CHAIN or eng == "tensor":
                self.last[eng] = (sem, self.cnt[sem])
            return self.cnt[sem]
        self.emit(eng, fn)
        return None

    def val(self, sem):
        self.sem(sem)
        return self.cnt[sem]

    def drain(self, eng, *sems):
        for s in sems:
            self.wait(eng, s)

    def barrier(self):
        hh = self.sem("phase")
        for eng in ENGS:
            if eng in _CHAIN:
                self._chain_wait(eng)
            elif self.last.get(eng) is not None:
                s, v = self.last[eng]
                self._wait_raw(eng, s, v)
            self.emit(eng, lambda e, hh=hh: e.nop().then_inc(hh, 1))
        self.cnt["phase"] += len(ENGS)
        v = self.cnt["phase"]
        for eng in ENGS:
            self._wait_raw(eng, "phase", v)

    def finalize(self, block):
        for eng in ENGS:
            fns = self.q[eng]

            def body(e, fns=fns):
                for fn in fns:
                    fn(e)
            getattr(block, eng)(body)


class Rot:

    def __init__(self, bufs):
        self.bufs = bufs
        self.gates = [None] * len(bufs)
        self.i = -1

    def next(self, P, producer_eng):
        self.i += 1
        j = self.i % len(self.bufs)
        if self.gates[j] is not None:
            s, v = self.gates[j]
            P.wait(producer_eng, s, v)
        return j, self.bufs[j]

    def gate(self, j, sem, val):
        self.gates[j] = (sem, val)


def build_nc(max_phase=99, taps=(), n_rg=NCORES):
    nc = bass.Bass("TRN2", target_bir_lowering=False)

    x_in = nc.dram_tensor("x_in", [TOK, DM], F32, kind="ExternalInput").ap()
    w_attn = nc.dram_tensor("w_attn", [4 * DM, DM], F32, kind="ExternalInput").ap()
    outw_in = nc.dram_tensor("outw_in", [DM, DM], F32, kind="ExternalInput").ap()
    w1_in = nc.dram_tensor("w1_in", [F2, DM], F32, kind="ExternalInput").ap()
    w2_in = nc.dram_tensor("w2_in", [DM, FF], F32, kind="ExternalInput").ap()
    w1_sl = nc.dram_tensor("w1_sl", [F2 // 8, DM], F32, kind="ExternalInput").ap()
    w2_sl = nc.dram_tensor("w2_sl", [DM // 8, FF], F32, kind="ExternalInput").ap()
    ffnw_b = nc.dram_tensor("ffnw_b", [128, DM], F32, kind="ExternalInput").ap()
    attnw_b = nc.dram_tensor("attnw_b", [128, DM], F32, kind="ExternalInput").ap()
    ident_in = nc.dram_tensor("ident_in", [128, 128], F32, kind="ExternalInput").ap()
    dd0_in = nc.dram_tensor("dd0_in", [128, H * 256], F32, kind="ExternalInput").ap()
    qdb_in = nc.dram_tensor("qdb_in", [128, (H // 2) * 256], F32, kind="ExternalInput").ap()
    kdc_in = nc.dram_tensor("kdc_in", [128, H * 2], F32, kind="ExternalInput").ap()
    tpred_in = nc.dram_tensor("tpred_in", [8, 64, H * D], F32, kind="ExternalInput").ap()
    y_out = nc.dram_tensor("y_out", [TOK, DM], F32, kind="ExternalOutput").ap()

    x1_scr = nc.dram_tensor("x1_scr", [TOK, DM], F32).ap()
    x2_scr = nc.dram_tensor("x2_scr", [TOK, DM], F32).ap()
    xi_scr = nc.dram_tensor("xi_scr", [TOK, DM], BF16).ap()
    w1q_scr = nc.dram_tensor("w1q_scr", [F2, DM], BF16).ap()
    w2q_scr = nc.dram_tensor("w2q_scr", [DM, FF], BF16).ap()
    w2tT_scr = nc.dram_tensor("w2tT_scr", [FF, DM], BF16).ap()
    hid_scr = nc.dram_tensor("hid_scr", [TOK, FF], F32).ap()
    hq_scr = nc.dram_tensor("hq_scr", [TOK, FF], BF16).ap()
    qkv_scr = nc.dram_tensor("qkv_scr", [TOK, 3 * DM], F32).ap()
    gate_scr = nc.dram_tensor("gate_scr", [TOK, DM], F32).ap()
    qT_scr = nc.dram_tensor("qT_scr", [DM, TOK], F32).ap()
    kT_scr = nc.dram_tensor("kT_scr", [DM, TOK], F32).ap()
    attn_scr = nc.dram_tensor("attn_scr", [TOK, DM], F32).ap()
    cc_sin = nc.dram_tensor("cc_sin", [64, H * D], F32).ap()
    cc_sout = nc.dram_tensor("cc_sout", [8 * 64, H * D], F32).ap()
    cc_min = nc.dram_tensor("cc_min", [1, 2], F32).ap()
    cc_mout = nc.dram_tensor("cc_mout", [1, 2], F32).ap()

    tap_src = {"x1": x1_scr, "x2": x2_scr, "qkv": qkv_scr, "gate": gate_scr,
               "qT": qT_scr, "kT": kT_scr, "attn": attn_scr, "sgath": cc_sout,
               "hid": hid_scr, "xi": xi_scr, "hq": hq_scr,
               "w1q": w1q_scr, "w2q": w2q_scr, "w2tT": w2tT_scr}
    tap_dt = {"xi": BF16, "hq": BF16, "w1q": BF16, "w2q": BF16, "w2tT": BF16}
    tap_outs = {}
    _STRIP_TAP = None
    _STRIP_A = "strips_a" in taps
    taps = [t for t in taps if t != "strips_a"]
    if "strips" in taps or _STRIP_A:
        taps = [t for t in taps if t != "strips"]
        _STRIP_TAP = nc.dram_tensor("dbg_strips", [512, NMT], F32,
                                    kind="ExternalOutput").ap()
        tap_outs["strips"] = None
    for t in taps:
        src = tap_src[t]
        tap_outs[t] = nc.dram_tensor("dbg_" + t, list(src.tensor.shape),
                                     tap_dt.get(t, F32), kind="ExternalOutput").ap()

    with contextlib.ExitStack() as big:
        P = Prog(nc, big)

        def sb(name, shape, dt=F32, st=None):
            return (st or big).enter_context(nc.sbuf_tensor(name, shape, dt))

        def pst(name, shape, st, dt=F32):
            return st.enter_context(nc.psum_tensor(name, shape, dt))

        ident = sb("ident", [128, 128])
        identb = sb("identb", [128, 128], BF16)
        c384 = sb("c384", [128, 1])
        cm384 = sb("cm384", [128, 1])
        ceps = sb("ceps", [128, 1])
        cinvd = sb("cinvd", [128, 1])
        maski = sb("maski", [128, 1], I32)
        ones_c = sb("ones_c", [128, 1])
        ones_r = sb("ones_r", [1, 128])
        sw1s = sb("sw1s", [128, 1])
        nsw1s = sb("nsw1s", [128, 1])
        w1m = sb("w1m", [128, 1])
        sw2s = sb("sw2s", [128, 1])
        w2m = sb("w2m", [128, 1])
        kdc = sb("kdc", [128, H * 2])
        sinv_s = [[sb(f"sinv_s{a}_{m}", [128, 1]) for m in range(NMT)] for a in range(2)]
        amax_s = [[sb(f"amax_s{a}_{m}", [128, 1]) for m in range(NMT)] for a in range(2)]
        m2q_s = [[sb(f"m2q_s{a}_{m}", [128, 1]) for m in range(NMT)] for a in range(2)]
        fin_s = [[sb(f"fin_s{a}_{m}", [128, 1]) for m in range(NMT)] for a in range(2)]
        Sin_st = sb("Sin_st", [64, H * D])

        def phase0():
            for f in (lambda e: e.memset(c384[:], 384.5),
                      lambda e: e.memset(cm384[:], -384.0),
                      lambda e: e.memset(ceps[:], EPS),
                      lambda e: e.memset(cinvd[:], 1.0 / DM),
                      lambda e: e.memset(maski[:], -32768),
                      lambda e: e.memset(ones_c[:], 1.0),
                      lambda e: e.memset(ones_r[:], 1.0)):
                P.op("vector", f)
            P.dma("sync", ident[:], ident_in, "ld0")
            P.dma("sync", kdc[:], kdc_in, "ld0")
            with contextlib.ExitStack() as ar:
                wt = Rot([sb(f"p0w{i}", [128, FF], st=ar) for i in range(2)])
                acc = sb("p0acc", [128, 2], st=ar)
                acc1 = sb("p0acc1", [128, 1], st=ar)
                acc2 = sb("p0acc2", [128, 1], st=ar)
                red = sb("p0red", [128, 1], st=ar)
                sc2 = sb("p0sc", [1, 2], st=ar)
                pr = pst("p0ps", [1, 2], ar)
                pb = pst("p0pb", [128, 2], ar)
                P.op("vector", lambda e: e.memset(acc1[:], 0.0))
                P.op("vector", lambda e: e.memset(acc2[:], 0.0))
                for accb, tiles, srcap, ncols in ((acc1, (F2 // 8) // 128, w1_sl, DM),
                                                  (acc2, (DM // 8) // 128, w2_sl, FF)):
                    for i in range(tiles):
                        j, buf = wt.next(P, "sync")
                        lv = P.dma("sync", buf[:, 0:ncols],
                                   srcap[i * 128:(i + 1) * 128, :], f"ld1_{j}")
                        P.wait("vector", f"ld1_{j}", lv)
                        P.op("vector", lambda e, buf=buf, ncols=ncols: e.reduce_sum(
                            red[:], buf[:, 0:ncols], axis=AX.X, apply_absolute_value=True))
                        vv = P.op("vector", lambda e, accb=accb: e.tensor_tensor(
                            accb[:], accb[:], red[:], ALU.add), sem="v0")
                        wt.gate(j, "v0", vv)
                P.op("vector", lambda e: e.tensor_scalar(
                    acc[:, 0:1], acc1[:], 1.0, None, ALU.mult))
                P.op("vector", lambda e: e.tensor_scalar(
                    acc[:, 1:2], acc2[:], 1.0, None, ALU.mult), sem="v0")
                P.wait("tensor", "v0")
                P.op("tensor", lambda e: e.matmul(pr[:], ones_c[:], acc[:],
                                                  start=True, stop=True), sem="t0")
                P.wait("scalar", "t0")
                P.op("scalar", lambda e: e.copy(sc2[:], pr[:]), sem="s0")
                P.wait("scalar", "s0")
                P.dma("scalar", cc_min, sc2[:], "st0")
                P.wait("gpsimd", "st0")
                P.op("gpsimd", lambda e: e.collective_compute(
                    "AllReduce", ALU.add, replica_groups=[list(range(n_rg))],
                    ins=[cc_min.opt()], outs=[cc_mout.opt()]), sem="cc0")
                P.wait("sync", "cc0")
                P.dma("sync", sc2[:], cc_mout, "ld2")
                P.wait("vector", "ld2")
                P.op("vector", lambda e: e.tensor_scalar(
                    sc2[:, 0:1], sc2[:, 0:1], 1.0 / N1, 1e-5, ALU.mult, ALU.max))
                P.op("vector", lambda e: e.tensor_scalar(
                    sc2[:, 1:2], sc2[:, 1:2], 1.0 / N2, 1e-5, ALU.mult, ALU.max), sem="v0")
                P.wait("tensor", "v0")
                P.op("tensor", lambda e: e.matmul(pb[:], ones_r[:], sc2[:],
                                                  start=True, stop=True), sem="t0")
                P.wait("vector", "t0")
                P.op("vector", lambda e: e.tensor_copy(w1m[:], pb[:, 0:1]))
                P.op("vector", lambda e: e.tensor_copy(w2m[:], pb[:, 1:2]))
                P.op("vector", lambda e: e.reciprocal(sw1s[:], w1m[:]))
                P.op("vector", lambda e: e.reciprocal(sw2s[:], w2m[:]))
                P.op("vector", lambda e: e.tensor_scalar(
                    nsw1s[:], sw1s[:], -1.0, None, ALU.mult))
                P.wait("vector", "ld0")
                P.op("vector", lambda e: e.tensor_copy(identb[:], ident[:]))
                P.barrier()

        def emit_ln(xt_ap, yt_ap, sqt_ap, tmp, vs, ss, skip_iv=False):
            rs, mn, sd, iv = tmp
            P.op("vector", lambda e: e.reduce_sum(rs[:], xt_ap, axis=AX.X))
            P.op("vector", lambda e: e.tensor_scalar(
                mn[:], rs[:], 1.0 / DM, None, ALU.mult))
            P.op("vector", lambda e: e.tensor_scalar(
                yt_ap, xt_ap, mn[:, 0:1], None, ALU.subtract), sem=vs)
            P.wait("scalar", vs)
            P.op("scalar", lambda e: e.activation(sqt_ap, yt_ap, AF.Square), sem=ss)
            P.wait("vector", ss)
            P.op("vector", lambda e: e.reduce_sum(rs[:], sqt_ap, axis=AX.X), sem=vs)
            P.wait("scalar", vs)
            P.op("scalar", lambda e: e.activation(
                sd[:], rs[:], AF.Sqrt, scale=cinvd[:, 0:1], bias=ceps[:, 0:1]), sem=ss)
            P.wait("vector", ss)
            if skip_iv:
                P.op("vector", lambda e: e.reciprocal(iv[:], sd[:]), sem=vs)
                return
            P.op("vector", lambda e: e.reciprocal(iv[:], sd[:]))
            P.op("vector", lambda e: e.tensor_scalar(
                yt_ap, yt_ap, iv[:, 0:1], None, ALU.mult), sem=vs)

        def phase_ffn(app, src, dst):
            tag = f"f{app}"
            ld, lw, tp, st, vs, ss, ts_, pf = [
                "f" + s_ for s_ in
                ("ld", "lw", "tp", "st", "vs", "ss", "ts", "pf")]
            sinv, amx, m2q, fin = sinv_s[app], amax_s[app], m2q_s[app], fin_s[app]

            with contextlib.ExitStack() as ar:
                xt = Rot([sb(f"{tag}xt{i}", [128, DM], st=ar) for i in range(2)])
                yt = [sb(f"{tag}yt{i}", [128, DM], st=ar) for i in range(2)]
                sqt = [sb(f"{tag}sq{i}", [128, DM], st=ar) for i in range(2)]
                xib = Rot([sb(f"{tag}xib{i}", [128, DM], BF16, st=ar) for i in range(2)])
                tmp = [sb(f"{tag}tmp{i}", [128, 1], st=ar) for i in range(4)]
                am = sb(f"{tag}am", [128, 1], st=ar)
                sc = sb(f"{tag}sc", [128, 1], st=ar)
                fwb = sb(f"{tag}fwb", [128, DM], st=ar)
                P.dma("sync", fwb[:], ffnw_b, ld + "f")
                ssx_base = P.val(ss + "x")
                P.wait("vector", ld + "f")
                for m in range(NMT):
                    j, xbuf = xt.next(P, "sync")
                    lv = P.dma("sync", xbuf[:], src[m * 128:(m + 1) * 128, :],
                               f"{ld}_{j}")
                    b = m % 2
                    P.wait("vector", f"{ld}_{j}", lv)
                    if m >= 2:
                        P.wait("vector", ss + "x", ssx_base + m - 1)
                    emit_ln(xbuf[:], yt[b][:], sqt[b][:], tmp, vs, ss,
                            skip_iv=True)
                    P.wait("scalar", vs)
                    P.op("scalar", lambda e, b=b: e.activation(
                        sqt[b][:], yt[b][:], AF.Square,
                        scale=tmp[3][:, 0:1]), sem=ss)
                    P.wait("vector", ss)
                    P.op("vector", lambda e, b=b: e.reduce_sum(
                        tmp[0][:], sqt[b][:], axis=AX.X), sem=vs)
                    P.wait("scalar", vs)
                    P.op("scalar", lambda e: e.activation(
                        tmp[2][:], tmp[0][:], AF.Sqrt,
                        scale=cinvd[:, 0:1], bias=ceps[:, 0:1]), sem=ss)
                    P.wait("vector", ss)
                    P.op("vector", lambda e: e.reciprocal(tmp[1][:], tmp[2][:]))
                    P.op("vector", lambda e: e.tensor_tensor(
                        tmp[1][:], tmp[1][:], tmp[3][:], ALU.mult))
                    P.op("vector", lambda e, b=b: e.scalar_tensor_tensor(
                        yt[b][:], yt[b][:], tmp[1][:, 0:1], fwb[:],
                        ALU.mult, ALU.mult))
                    P.op("vector", lambda e, b=b: e.reduce_max(
                        am[:], yt[b][:], axis=AX.X, apply_absolute_value=True))
                    P.op("vector", lambda e: e.tensor_scalar(
                        am[:], am[:], 1e-5, None, ALU.max))
                    P.op("vector", lambda e: e.reciprocal(sc[:], am[:]))
                    P.op("vector", lambda e: e.tensor_scalar(
                        sc[:], sc[:], 127.0, None, ALU.mult))
                    P.op("vector", lambda e, m=m: e.tensor_scalar(
                        sinv[m][:], am[:], 1.0 / 127.0, None, ALU.mult))
                    P.op("vector", lambda e, b=b: e.tensor_scalar(
                        yt[b][:], yt[b][:], sc[:, 0:1], 384.5, ALU.mult, ALU.add))
                    xv = P.op("vector", lambda e, b=b: e.tensor_scalar(
                        yt[b][:].bitcast(I32), yt[b][:].bitcast(I32), maski[:, 0:1],
                        None, ALU.bitwise_and), sem=vs)
                    xt.gate(j, vs, xv)
                    jx, xbuf2 = xib.next(P, "scalar")
                    P.wait("scalar", vs, xv)
                    P.op("scalar", lambda e, b=b, xbuf2=xbuf2: e.activation(
                        xbuf2[:], yt[b][:], AF.Identity, bias=cm384[:, 0:1]), sem=ss + "x")
                    sv = P.dma("scalar", xi_scr[m * 128:(m + 1) * 128, :],
                               xbuf2[:], f"{st}_{jx}")
                    xib.gate(jx, f"{st}_{jx}", sv)
                if _STRIP_TAP is not None and app == 0 and _STRIP_A:
                    P.wait("sync", vs)
                    for m in range(NMT):
                        P.dma("sync", _STRIP_TAP[0:128, m:m + 1], sinv[m][:], "dbgs")
                P.drain("sync", f"{st}_0", f"{st}_1")
                P.barrier()

            with contextlib.ExitStack() as ar:
                xiT = sb(f"{tag}xiT", [128, NMT * TOK], BF16, st=ar)
                wf = Rot([sb(f"{tag}wf{i}", [128, DM], st=ar) for i in range(2)])
                wqa = sb(f"{tag}wqa", [128, DM], BF16, st=ar)
                wqb = sb(f"{tag}wqb", [128, DM], BF16, st=ar)
                wqo = Rot([sb(f"{tag}wqo{i}", [128, DM], BF16, st=ar) for i in range(2)])
                w1tg = [sb(f"{tag}w1tg{i}", [128, 16 * 512], BF16, st=ar) for i in range(2)]
                w1tu = [sb(f"{tag}w1tu{i}", [128, 16 * 512], BF16, st=ar) for i in range(2)]
                rel = [sb(f"{tag}rel{i}", [128, 512], st=ar) for i in range(2)]
                rl2 = [sb(f"{tag}rl2{i}", [128, 512], st=ar) for i in range(2)]
                hid = Rot([sb(f"{tag}hid{i}", [128, 512], st=ar) for i in range(2)])
                red1 = sb(f"{tag}red1", [128, 1], st=ar)
                t1 = sb(f"{tag}t1", [128, 1], st=ar)
                ht = Rot([sb(f"{tag}ht0", [128, FF // 2], st=ar)])
                hb = Rot([sb(f"{tag}hb0", [128, FF // 2], BF16, st=ar)])
                pg = [pst(f"{tag}pg{i}", [128, 512], ar) for i in range(2)]
                pu = [pst(f"{tag}pu{i}", [128, 512], ar) for i in range(2)]
                if app == 0:
                    ptt = [pst(f"{tag}ptt{i}", [128, 1024], ar, BF16)
                           for i in range(2)]
                    w2tt = sb(f"{tag}w2tt", [128, DM], BF16, st=ar)
                for m in range(NMT):
                    P.op("vector", lambda e, m=m: e.memset(amx[m][:], 0.0))
                for ft in range(NMT):
                    P.dmaT("sync", xiT[:, ft * TOK:(ft + 1) * TOK],
                           xi_scr[:, ft * 128:(ft + 1) * 128], tp + "x")
                nch = F2 // 1024
                mm_iter = 0
                pf_base = P.val(pf)
                ts_base = P.val(ts_)
                ssx2_base = P.val(ss + "x")
                hid_sv = [0] * NMT
                hq_sv = [0] * NMT

                pending_loads = []
                unit_queue = []
                pending_tt = []

                def issue_load(kind, meta):
                    jw, wbuf = wf.next(P, "gpsimd")
                    if kind == "w1":
                        lv = P.dma("gpsimd", wbuf[:], w1_in[meta:meta + 128, :],
                                   f"{lw}_{jw}")
                    else:
                        mq, c0 = meta
                        cn = min(2048, FF - c0)
                        lv = P.dma("gpsimd", wbuf[:, 0:cn],
                                   w2_in[mq * 128:(mq + 1) * 128, c0:c0 + cn],
                                   f"{lw}_{jw}")
                    pending_loads.append((kind, meta, jw, lv, wbuf))

                def emit_quant_one():
                    kind, meta, jw, lv, wbuf = pending_loads.pop(0)
                    cn = DM if kind == "w1" else min(2048, FF - meta[1])
                    P.wait("vector", f"{lw}_{jw}", lv)
                    if kind == "w1":
                        P.op("vector", lambda e, wbuf=wbuf: e.tensor_scalar(
                            wqa[:], wbuf[:], sw1s[:, 0:1], 0.5, ALU.mult,
                            ALU.is_gt))
                        vv = P.op("vector", lambda e, wbuf=wbuf: e.tensor_scalar(
                            wqb[:], wbuf[:], nsw1s[:, 0:1], 0.5, ALU.mult,
                            ALU.is_gt), sem=vs + "w")
                    else:
                        P.op("vector", lambda e, wbuf=wbuf, cn=cn: e.tensor_scalar(
                            wqa[:, 0:cn], wbuf[:, 0:cn], sw2s[:, 0:1], 0.5,
                            ALU.mult, ALU.is_gt))
                        vv = P.op("vector", lambda e, wbuf=wbuf, cn=cn: e.tensor_scalar(
                            wqb[:, 0:cn], wbuf[:, 0:cn], sw2s[:, 0:1], -0.5,
                            ALU.mult, ALU.is_lt), sem=vs + "w")
                    wf.gate(jw, vs + "w", vv)
                    jq, qbuf = wqo.next(P, "vector")
                    qv = P.op("vector", lambda e, qbuf=qbuf, cn=cn: e.tensor_tensor(
                        qbuf[:, 0:cn], wqa[:, 0:cn], wqb[:, 0:cn], ALU.subtract),
                        sem=vs + "q")
                    if kind == "w1":
                        P.wait("gpsimd", vs + "q", qv)
                        sv = P.dma("gpsimd", w1q_scr[meta:meta + 128, :], qbuf[:],
                                   f"{st}w{jq}")
                        wqo.gate(jq, f"{st}w{jq}", sv)
                    else:
                        mq, c0 = meta
                        pending_tt.append((qbuf, jq, qv, mq, c0, cn))
                    if unit_queue:
                        issue_load(*unit_queue.pop(0))

                tts = {"i": 0, "ev": [None, None], "st": None}

                def emit_tt_one():
                    qbuf, jq, qv, mq, c0, cn = pending_tt.pop(0)
                    ttv = 0
                    ev = 0
                    for g4 in range(cn // 512):
                        bt = tts["i"] % 2
                        tts["i"] += 1
                        if tts["ev"][bt] is not None:
                            P.wait("tensor", "fs2e", tts["ev"][bt])
                        P.wait("tensor", vs + "q", qv)
                        for j in range(4):
                            ttv = P.op("tensor",
                                       lambda e, bt=bt, j=j, b=g4 * 4 + j,
                                       qbuf=qbuf: e.transpose(
                                           ptt[bt][:, j * 128:(j + 1) * 128],
                                           qbuf[:, b * 128:(b + 1) * 128],
                                           identb[:]),
                                       sem=("ftt" if j == 3 else None))
                        P.wait("scalar", "ftt", ttv)
                        if g4 == 0 and tts["st"] is not None:
                            P.wait("scalar", "ft2s", tts["st"])
                        ev = P.op("scalar", lambda e, bt=bt, g4=g4: e.activation(
                            w2tt[:, g4 * 512:(g4 + 1) * 512], ptt[bt][:, 0:512],
                            AF.Identity), sem="fs2e")
                        tts["ev"][bt] = ev
                    wqo.gate(jq, "ftt", ttv)
                    dst3 = w2tT_scr[c0:c0 + cn, mq * 128:(mq + 1) * 128] \
                        .rearrange("(b p) r -> p b r", p=128)
                    src3 = w2tt[:, 0:cn].rearrange("p (b r) -> p b r", r=128)
                    P.wait("gpsimd", "fs2e", ev)
                    tts["st"] = P.dma("gpsimd", dst3, src3, "ft2s")

                def push_units(us):
                    unit_queue.extend(us)
                    while len(pending_loads) < 2 and unit_queue:
                        issue_load(*unit_queue.pop(0))

                if app == 0:
                    w1_units = [("w1", rbase + s4 * 128) for n in range(nch)
                                for rbase in (n * 512, FF + n * 512)
                                for s4 in range(4)]
                    w2_units = [("w2", (mq, c0)) for mq in range(NMT)
                                for c0 in (0, 2048, 4096)]
                    upfront = w1_units[0:16]
                    iter_units = []
                    for n in range(nch):
                        if n < 9:
                            w2b = (w2_units[n * 5:(n + 1) * 5] if n < 8
                                   else w2_units[40:48])
                            iter_units.append(
                                w1_units[16 + n * 8:24 + n * 8] + w2b)
                        else:
                            iter_units.append([])
                else:
                    upfront = []
                    iter_units = [[] for _ in range(nch)]

                def dmaT_batch(k):
                    bn = k % 2
                    if app == 0:
                        P.wait("sync", f"{st}w0")
                        P.wait("sync", f"{st}w1")
                    P.wait("sync", ts_, ts_base + 16 * (k - 1))
                    for ft in range(16):
                        P.dmaT("sync", w1tg[bn][:, ft * 512:(ft + 1) * 512],
                               w1q_scr[k * 512:(k + 1) * 512, ft * 128:(ft + 1) * 128],
                               f"{tp}w{bn}")
                        P.dmaT("sync", w1tu[bn][:, ft * 512:(ft + 1) * 512],
                               w1q_scr[FF + k * 512:FF + (k + 1) * 512,
                                       ft * 128:(ft + 1) * 128], f"{tp}w{bn}")
                    return P.val(f"{tp}w{bn}")

                def emit_hidq(m):
                    P.op("vector", lambda e, m=m: e.tensor_scalar(
                        t1[:], sinv[m][:], w1m[:, 0:1], None, ALU.mult))
                    P.op("vector", lambda e, m=m: e.tensor_tensor(
                        m2q[m][:], t1[:], t1[:], ALU.mult))
                    P.op("vector", lambda e, m=m: e.tensor_tensor(
                        m2q[m][:], m2q[m][:], t1[:], ALU.mult))
                    P.op("vector", lambda e, m=m: e.tensor_tensor(
                        t1[:], amx[m][:], m2q[m][:], ALU.mult))
                    P.op("vector", lambda e: e.tensor_scalar(
                        t1[:], t1[:], 1e-5, None, ALU.max))
                    P.op("vector", lambda e, m=m: e.tensor_scalar(
                        fin[m][:], t1[:], 1.0 / 127.0, None, ALU.mult))
                    P.op("vector", lambda e, m=m: e.tensor_scalar(
                        fin[m][:], fin[m][:], w2m[:, 0:1], 0.5, ALU.mult, ALU.mult))
                    P.op("vector", lambda e: e.reciprocal(t1[:], t1[:]))
                    P.op("vector", lambda e: e.tensor_scalar(
                        t1[:], t1[:], 127.0, None, ALU.mult))
                    P.op("vector", lambda e, m=m: e.tensor_tensor(
                        m2q[m][:], m2q[m][:], t1[:], ALU.mult))
                    for hh in range(2):
                        c0 = hh * (FF // 2)
                        jh, hbuf = ht.next(P, "sync")
                        P.wait("sync", f"{st}h0")
                        P.wait("sync", f"{st}h1")
                        lv = P.dma("sync", hbuf[:],
                                   hid_scr[m * 128:(m + 1) * 128, c0:c0 + FF // 2],
                                   ld + "h")
                        P.wait("vector", ld + "h", lv)
                        P.op("vector", lambda e, hbuf=hbuf, m=m: e.tensor_scalar(
                            hbuf[:], hbuf[:], m2q[m][:, 0:1], 384.5, ALU.mult, ALU.add))
                        hv = P.op("vector", lambda e, hbuf=hbuf: e.tensor_scalar(
                            hbuf[:].bitcast(I32), hbuf[:].bitcast(I32), maski[:, 0:1],
                            None, ALU.bitwise_and), sem=vs)
                        jb, bbuf = hb.next(P, "scalar")
                        P.wait("scalar", vs, hv)
                        av = P.op("scalar", lambda e, hbuf=hbuf, bbuf=bbuf: e.activation(
                            bbuf[:], hbuf[:], AF.Identity, bias=cm384[:, 0:1]), sem=ss)
                        ht.gate(jh, ss, av)
                        sv = P.dma("scalar",
                                   hq_scr[m * 128:(m + 1) * 128, c0:c0 + FF // 2],
                                   bbuf[:], f"{st}q{jb}")
                        hb.gate(jb, f"{st}q{jb}", sv)
                        hq_sv[m] = sv

                tp_chunk = [0] * nch
                push_units(upfront)
                for _ in upfront:
                    emit_quant_one()
                tp_chunk[0] = dmaT_batch(0)
                if nch > 1:
                    tp_chunk[1] = dmaT_batch(1)
                for n in range(nch):
                    bn = n % 2
                    units = iter_units[n]
                    push_units(units)
                    if not _INTERLEAVE_Q:
                        for _ in units:
                            emit_quant_one()
                    P.wait("tensor", f"{tp}w{bn}", tp_chunk[n])
                    if n == 0:
                        P.wait("tensor", tp + "x")
                    for m in range(NMT):
                        bm = mm_iter % 2
                        if mm_iter >= 2:
                            P.wait("tensor", pf, pf_base + mm_iter - 1)
                        for kt in range(16):
                            P.op("tensor", lambda e, bn=bn, bm=bm, m=m, kt=kt: e.matmul(
                                pg[bm][:],
                                xiT[:, kt * TOK + m * 128: kt * TOK + m * 128 + 128],
                                w1tg[bn][:, kt * 512:(kt + 1) * 512],
                                start=(kt == 0), stop=(kt == 15)))
                        tvv = 0
                        for kt in range(16):
                            tvv = P.op("tensor", lambda e, bn=bn, bm=bm, m=m, kt=kt:
                                       e.matmul(
                                           pu[bm][:],
                                           xiT[:, kt * TOK + m * 128:
                                               kt * TOK + m * 128 + 128],
                                           w1tu[bn][:, kt * 512:(kt + 1) * 512],
                                           start=(kt == 0), stop=(kt == 15)),
                                       sem=(ts_ if kt == 15 else None))
                        if _ACT_EVAC:
                            if mm_iter >= 2:
                                P.wait("scalar", pf, pf_base + mm_iter - 1)
                            P.wait("scalar", ts_, tvv)
                            P.op("scalar", lambda e, bm=bm: e.activation(
                                rel[bm][:], pg[bm][:], AF.Relu))
                            P.op("scalar", lambda e, bm=bm: e.activation(
                                rl2[bm][:], rel[bm][:], AF.Square), sem=ss + "x")
                            P.wait("vector", ss + "x", ssx2_base + mm_iter + 1)
                        else:
                            P.wait("vector", ts_, tvv)
                            P.op("vector", lambda e, bm=bm: e.tensor_scalar(
                                rel[bm][:], pg[bm][:], 0.0, None, ALU.max))
                            P.op("vector", lambda e, bm=bm: e.tensor_tensor(
                                rl2[bm][:], rel[bm][:], rel[bm][:], ALU.mult))
                        jh, hbuf = hid.next(P, "vector")
                        P.op("vector", lambda e, bm=bm, hbuf=hbuf: e.tensor_tensor(
                            hbuf[:], rl2[bm][:], pu[bm][:], ALU.mult), sem=pf)
                        P.op("vector", lambda e, hbuf=hbuf: e.reduce_max(
                            red1[:], hbuf[:], axis=AX.X, apply_absolute_value=True))
                        hv = P.op("vector", lambda e, m=m: e.tensor_tensor(
                            amx[m][:], amx[m][:], red1[:], ALU.max), sem=vs + "r")
                        P.wait("scalar", vs + "r", hv)
                        sv = P.dma("scalar",
                                   hid_scr[m * 128:(m + 1) * 128,
                                           n * 512:(n + 1) * 512], hbuf[:],
                                   f"{st}h{jh}")
                        hid.gate(jh, f"{st}h{jh}", sv)
                        hid_sv[m] = sv
                        while len(pending_tt) > 1:
                            emit_tt_one()
                        if _INTERLEAVE_Q and m < len(units):
                            emit_quant_one()
                        if n == nch - 1:
                            emit_hidq(m)
                        mm_iter += 1
                    while pending_tt:
                        emit_tt_one()
                    if n + 2 < nch:
                        tp_chunk[n + 2] = dmaT_batch(n + 2)
                if _DUMP_B and app == 0:
                    dxi = nc.dram_tensor("dbg_xiT", [128, NMT * TOK], BF16,
                                         kind="ExternalOutput").ap()
                    dwt = nc.dram_tensor("dbg_w1t", [128, 2 * 16 * 512], BF16,
                                         kind="ExternalOutput").ap()
                    P.wait("sync", ts_, ts_base + 16 * nch)
                    P.dma("sync", dxi, xiT[:], "dbgB")
                    P.dma("sync", dwt[:, 0:8192], w1tg[(nch - 1) % 2][:], "dbgB")
                    P.dma("sync", dwt[:, 8192:16384], w1tu[(nch - 1) % 2][:], "dbgB")
                    P.wait("sync", "dbgB")
                P.drain("sync", f"{st}h0", f"{st}h1", f"{st}q0", f"{st}q1")
                if app == 0:
                    P.drain("sync", f"{st}w0", f"{st}w1", "ft2s")
                P.barrier()

            with contextlib.ExitStack() as ar:
                NKT = FF // 128
                w2t = [sb(f"{tag}w2t{i}", [128, NKT * 512], BF16, st=ar)
                       for i in range(2)]
                hqT = [sb(f"{tag}hqT{i}", [128, NKT * 512], BF16, st=ar) for i in range(2)]
                xres = Rot([sb(f"{tag}xr{i}", [128, 512], st=ar) for i in range(2)])
                stg = Rot([sb(f"{tag}sg{i}", [128, 512], st=ar) for i in range(2)])
                pd = [pst(f"{tag}pd{i}", [128, 512], ar) for i in range(4)]
                pfd_base = P.val(pf)
                tsd_base = P.val(ts_)

                def w2t_load(g):
                    qi, bw = g % 4, g % 2
                    P.wait("sync", ts_, tsd_base + g - 1)
                    src3 = w2tT_scr[:, qi * 512:(qi + 1) * 512] \
                        .rearrange("(f p) r -> p f r", p=128)
                    dst3 = w2t[bw][:].rearrange("p (f r) -> p f r", f=NKT)
                    P.dma("sync", dst3, src3, f"{tp}2{bw}")
                    return P.val(f"{tp}2{bw}")

                def hqT_part(mg, part):
                    bg = mg % 2
                    P.wait("sync", ts_, tsd_base + 4 * mg - 4)
                    for ft in range(part * 11, part * 11 + 11):
                        P.dmaT("sync", hqT[bg][:, ft * 512:(ft + 1) * 512],
                               hq_scr[mg * 512:(mg + 1) * 512,
                                      ft * 128:(ft + 1) * 128], f"{tp}h{bg}")
                    return P.val(f"{tp}h{bg}")

                hq_tpv = [0] * 4
                w2_val = [0] * 16
                for part in range(4):
                    hq_tpv[0] = hqT_part(0, part)
                w2_val[0] = w2t_load(0)
                w2_val[1] = w2t_load(1)
                gidx = 0
                for mg in range(4):
                    bg = mg % 2
                    for qi in range(4):
                        bw = gidx % 2
                        if mg + 1 < 4:
                            hq_tpv[mg + 1] = hqT_part(mg + 1, qi)
                        if gidx + 2 < 16:
                            w2_val[gidx + 2] = w2t_load(gidx + 2)
                        P.wait("tensor", f"{tp}h{bg}", hq_tpv[mg])
                        P.wait("tensor", f"{tp}2{bw}", w2_val[gidx])
                        if gidx > 0:
                            P.wait("tensor", pf, pfd_base + gidx * 4)
                        tvv = 0
                        for ft in range(NKT):
                            for mi in range(4):
                                last = (ft == NKT - 1 and mi == 3)
                                tvv = P.op("tensor",
                                           lambda e, bg=bg, ft=ft, mi=mi, bw=bw: e.matmul(
                                               pd[mi][:],
                                               hqT[bg][:, ft * 512 + mi * 128:
                                                        ft * 512 + mi * 128 + 128],
                                               w2t[bw][:, ft * 512:(ft + 1) * 512],
                                               start=(ft == 0), stop=(ft == NKT - 1)),
                                           sem=(ts_ if last else None))
                        for mi in range(4):
                            m = mg * 4 + mi
                            jx, xbuf = xres.next(P, "sync")
                            lv = P.dma("sync", xbuf[:],
                                       src[m * 128:(m + 1) * 128,
                                           qi * 512:(qi + 1) * 512], f"{ld}x{jx}")
                            js, sbuf_ = stg.next(P, "vector")
                            P.wait("vector", ts_, tvv)
                            P.wait("vector", f"{ld}x{jx}", lv)
                            xv = P.op("vector",
                                      lambda e, mi=mi, m=m, xbuf=xbuf, sbuf_=sbuf_:
                                      e.scalar_tensor_tensor(
                                          sbuf_[:], pd[mi][:], fin[m][:, 0:1], xbuf[:],
                                          ALU.mult, ALU.add), sem=pf)
                            xres.gate(jx, pf, P.val(pf))
                            P.wait("scalar", pf, P.val(pf))
                            sv = P.dma("scalar",
                                       dst[m * 128:(m + 1) * 128, qi * 512:(qi + 1) * 512],
                                       sbuf_[:], f"{st}d{js}")
                            stg.gate(js, f"{st}d{js}", sv)
                        gidx += 1
                P.drain("sync", f"{st}d0", f"{st}d1")
                P.barrier()

        def phase2():
            with contextlib.ExitStack() as outer:
                yT = sb("p2yT", [128, NMT * TOK], st=outer)
                with contextlib.ExitStack() as ar:
                    xt = Rot([sb(f"p2xt{i}", [128, DM], st=ar) for i in range(2)])
                    yt = [sb(f"p2yt{i}", [128, DM], st=ar) for i in range(2)]
                    sqt = [sb(f"p2sq{i}", [128, DM], st=ar) for i in range(2)]
                    tmp = [sb(f"p2tmp{i}", [128, 1], st=ar) for i in range(4)]
                    tpp = [pst(f"p2tp{i}", [128, 512], ar) for i in range(2)]
                    yT_r = yT[:].rearrange("p (k t) -> p k t", k=NMT)
                    for m in range(NMT):
                        j, xbuf = xt.next(P, "sync")
                        lv = P.dma("sync", xbuf[:], x1_scr[m * 128:(m + 1) * 128, :],
                                   f"p2ld{j}")
                        b = m % 2
                        P.wait("vector", f"p2ld{j}", lv)
                        if m >= 2:
                            P.wait("vector", "p2tv", (m - 1) * 4)
                        emit_ln(xbuf[:], yt[b][:], sqt[b][:], tmp, "p2vs", "p2ss")
                        yv = P.op("vector", lambda e: e.nop(), sem="p2y")
                        xt.gate(j, "p2y", yv)
                        P.wait("tensor", "p2y", yv)
                        for g in range(4):
                            gi = m * 4 + g
                            bg = gi % 2
                            if gi >= 2:
                                P.wait("tensor", "p2tv", gi - 1)
                            for t4 in range(4):
                                ft = g * 4 + t4
                                P.op("tensor", lambda e, b=b, bg=bg, ft=ft, t4=t4: e.transpose(
                                    tpp[bg][:, t4 * 128:(t4 + 1) * 128],
                                    yt[b][:, ft * 128:(ft + 1) * 128], ident[:]),
                                    sem=("p2tt" if t4 == 3 else None))
                            P.wait("vector", "p2tt")
                            P.op("vector", lambda e, bg=bg, g=g, m=m: e.tensor_copy(
                                yT_r[:, g * 4:(g + 1) * 4, m * 128:(m + 1) * 128],
                                tpp[bg][:].rearrange("p (f t) -> p f t", f=4)), sem="p2tv")
                    P.barrier()
                with contextlib.ExitStack() as ar:
                    wf = Rot([sb(f"p2wf{i}", [128, DM], st=ar) for i in range(2)])
                    Wt = sb("p2Wt", [128, 16 * 512], st=ar)
                    stg = Rot([sb(f"p2sg{i}", [128, 512], st=ar) for i in range(2)])
                    st2 = Rot([sb(f"p2s2{i}", [128, 512], st=ar) for i in range(2)])
                    tpp = [pst(f"p2tq{i}", [128, 512], ar) for i in range(2)]
                    mmp = [pst(f"p2mm{i}", [128, 512], ar) for i in range(2)]
                    Wt_r = Wt[:].rearrange("p (k r) -> p k r", k=16)
                    qT_r = qT_scr.rearrange("(b p) t -> p b t", p=128)
                    kT_r = kT_scr.rearrange("(b p) t -> p b t", p=128)
                    tpi = 0
                    mmi = 0
                    for c in range(16):
                        if c > 0:
                            P.wait("vector", "p2mt", mmi)
                        wv = 0
                        for s4 in range(4):
                            j, wbuf = wf.next(P, "sync")
                            lv = P.dma("sync", wbuf[:],
                                       w_attn[c * 512 + s4 * 128:
                                              c * 512 + (s4 + 1) * 128, :], f"p2lw{j}")
                            P.wait("tensor", f"p2lw{j}", lv)
                            for g in range(4):
                                bg = tpi % 2
                                if tpi >= 2:
                                    P.wait("tensor", "p2wv", tpi - 1)
                                for t4 in range(4):
                                    kt = g * 4 + t4
                                    P.op("tensor",
                                         lambda e, wbuf=wbuf, bg=bg, kt=kt, t4=t4:
                                         e.transpose(tpp[bg][:, t4 * 128:(t4 + 1) * 128],
                                                     wbuf[:, kt * 128:(kt + 1) * 128],
                                                     ident[:]),
                                         sem=("p2wt" if t4 == 3 else None))
                                P.wait("vector", "p2wt")
                                wv = P.op("vector", lambda e, bg=bg, g=g, s4=s4: e.tensor_copy(
                                    Wt_r[:, g * 4:(g + 1) * 4, s4 * 128:(s4 + 1) * 128],
                                    tpp[bg][:].rearrange("p (f t) -> p f t", f=4)),
                                    sem="p2wv")
                                tpi += 1
                            wf.gate(j, "p2wv", wv)
                        P.wait("tensor", "p2wv", wv)
                        for m in range(NMT):
                            bm = mmi % 2
                            if mmi >= 2:
                                P.wait("tensor", "p2pf", mmi - 1)
                            tvv = 0
                            for kt in range(16):
                                tvv = P.op("tensor", lambda e, bm=bm, m=m, kt=kt: e.matmul(
                                    mmp[bm][:],
                                    yT[:, kt * TOK + m * 128: kt * TOK + m * 128 + 128],
                                    Wt[:, kt * 512:(kt + 1) * 512],
                                    start=(kt == 0), stop=(kt == 15)),
                                    sem=("p2mt" if kt == 15 else None))
                            mmi += 1
                            js, sbuf_ = stg.next(P, "scalar")
                            P.wait("scalar", "p2mt", tvv)
                            func = AF.Silu if c < 12 else AF.Sigmoid
                            fv = P.op("scalar", lambda e, bm=bm, sbuf_=sbuf_, func=func:
                                      e.activation(sbuf_[:], mmp[bm][:], func), sem="p2pf")
                            if c < 12:
                                dstap = qkv_scr[m * 128:(m + 1) * 128, c * 512:(c + 1) * 512]
                            else:
                                dstap = gate_scr[m * 128:(m + 1) * 128,
                                                 (c - 12) * 512:(c - 11) * 512]
                            sv = P.dma("scalar", dstap, sbuf_[:], f"p2st{js}")
                            if c < 8:
                                P.wait("tensor", "p2pf", fv)
                                bg = tpi % 2
                                if tpi >= 2:
                                    P.wait("tensor", "p2wv", tpi - 1)
                                for t4 in range(4):
                                    P.op("tensor", lambda e, sbuf_=sbuf_, bg=bg, t4=t4:
                                         e.transpose(tpp[bg][:, t4 * 128:(t4 + 1) * 128],
                                                     sbuf_[:, t4 * 128:(t4 + 1) * 128],
                                                     ident[:]),
                                         sem=("p2wt" if t4 == 3 else None))
                                qtv = P.val("p2wt")
                                j2, s2buf = st2.next(P, "vector")
                                P.wait("vector", "p2wt", qtv)
                                wv2 = P.op("vector", lambda e, bg=bg, s2buf=s2buf:
                                           e.tensor_copy(s2buf[:], tpp[bg][:]), sem="p2wv")
                                tpi += 1
                                P.wait("sync", "p2wv", wv2)
                                scr_r = qT_r if c < 4 else kT_r
                                cc = c if c < 4 else c - 4
                                sv2 = P.dma("sync",
                                            scr_r[:, cc * 4:(cc + 1) * 4,
                                                  m * 128:(m + 1) * 128],
                                            s2buf[:].rearrange("p (f t) -> p f t", f=4),
                                            f"p2s2{j2}")
                                st2.gate(j2, f"p2s2{j2}", sv2)
                                stg.gate(js, f"p2st{js}", sv)
                            else:
                                stg.gate(js, f"p2st{js}", sv)
                    P.drain("sync", "p2st0", "p2st1", "p2s20", "p2s21")
                    P.barrier()

        def phase3a():
            kv_r = lambda i, c0, c1: qkv_scr[i * 256:(i + 1) * 256, c0:c1].rearrange(
                "(j p) c -> p j c", p=128)
            d3 = lambda ap, n: ap.rearrange("p (j c) -> p j c", j=n)
            with contextlib.ExitStack() as ar:
                kb = Rot([sb(f"p3ak{i}", [128, 2 * DM], st=ar) for i in range(2)])
                vb = Rot([sb(f"p3av{i}", [128, 2 * DM], st=ar) for i in range(2)])
                kk = Rot([sb(f"p3akk{i}", [128, 64], st=ar) for i in range(2)])
                Sloc = sb("p3aS", [64, H * D], st=ar)
                gs = Rot([sb(f"p3gs{i}", [64, H * D], st=ar) for i in range(2)])
                tpb = Rot([sb(f"p3tp{i}", [64, H * D], st=ar) for i in range(2)])
                cps = [pst(f"p3acp{i}", [64, 64], ar) for i in range(2)]
                P.op("vector", lambda e: e.memset(Sloc[:], 0.0))
                n_sv = 0
                for i in range(NBLK):
                    jk, kbuf = kb.next(P, "sync")
                    kv1 = P.dma("sync", d3(kbuf[:], 2), kv_r(i, DM, 2 * DM),
                                f"p3ak{jk}")
                    jv, vbuf = vb.next(P, "sync")
                    kv2 = P.dma("sync", d3(vbuf[:], 2), kv_r(i, 2 * DM, 3 * DM),
                                f"p3av{jv}")
                    P.wait("vector", f"p3ak{jk}", kv1)
                    P.wait("vector", f"p3av{jv}", kv2)
                    P.wait("tensor", f"p3av{jv}", kv2)
                    for hh in range(H):
                        bp = hh % 2
                        if n_sv >= 2:
                            P.wait("tensor", "p3sv", n_sv - 1)
                        tv = 0
                        for jh in range(2):
                            jkk, kkb = kk.next(P, "vector")
                            kv_ = P.op("vector",
                                       lambda e, kbuf=kbuf, hh=hh, jh=jh, kkb=kkb:
                                       e.tensor_scalar(
                                           kkb[:],
                                           kbuf[:, jh * DM + hh * 64: jh * DM + hh * 64 + 64],
                                           kdc[:, hh * 2 + jh: hh * 2 + jh + 1],
                                           None, ALU.mult), sem="p3kv")
                            P.wait("tensor", "p3kv", kv_)
                            tv = P.op("tensor",
                                      lambda e, bp=bp, kkb=kkb, vbuf=vbuf, hh=hh, jh=jh:
                                      e.matmul(cps[bp][:], kkb[:],
                                               vbuf[:, jh * DM + hh * 64:
                                                    jh * DM + hh * 64 + 64],
                                               start=(jh == 0), stop=(jh == 1)),
                                      sem=("p3tc" if jh == 1 else None))
                            kk.gate(jkk, "p3tc", P.val("p3tc") if jh == 1 else
                                    P.val("p3tc") + 1)
                        P.wait("vector", "p3tc", tv)
                        sv_ = P.op("vector", lambda e, bp=bp, hh=hh: e.scalar_tensor_tensor(
                            Sloc[:, hh * 64:(hh + 1) * 64], Sloc[:, hh * 64:(hh + 1) * 64],
                            float(_BD[hh]), cps[bp][:], ALU.mult, ALU.add), sem="p3sv")
                        n_sv += 1
                    kb.gate(jk, "p3kv", P.val("p3kv"))
                    vb.gate(jv, "p3tc", P.val("p3tc"))
                P.wait("scalar", "p3sv")
                sv = P.dma("scalar", cc_sin, Sloc[:], "p3st")
                P.wait("gpsimd", "p3st", sv)
                P.op("gpsimd", lambda e: e.collective_compute(
                    "AllGather", ALU.bypass, replica_groups=[list(range(n_rg))],
                    ins=[cc_sin.opt()],
                    outs=[(cc_sout[0:64 * n_rg, :] if n_rg != NCORES else cc_sout).opt()]),
                    sem="p3cc")
                P.op("vector", lambda e: e.memset(Sin_st[:], 0.0))
                P.wait("sync", "p3cc")
                for cp in range(n_rg):
                    jg, gbuf = gs.next(P, "sync")
                    gv1 = P.dma("sync", gbuf[:], cc_sout[cp * 64:(cp + 1) * 64, :],
                                f"p3ag{jg}")
                    jt, tbuf = tpb.next(P, "sync")
                    gv2 = P.dma("sync", tbuf[:], tpred_in[cp], f"p3at{jt}")
                    P.wait("vector", f"p3ag{jg}", gv1)
                    P.wait("vector", f"p3at{jt}", gv2)
                    P.op("vector", lambda e, gbuf=gbuf, tbuf=tbuf: e.tensor_tensor(
                        gbuf[:], gbuf[:], tbuf[:], ALU.mult))
                    gv = P.op("vector", lambda e, gbuf=gbuf: e.tensor_tensor(
                        Sin_st[:], Sin_st[:], gbuf[:], ALU.add), sem="p3gv")
                    gs.gate(jg, "p3gv", gv)
                    tpb.gate(jt, "p3gv", gv)
                P.barrier()

        def phase3b():
            kv_r = lambda i, c0, c1: qkv_scr[i * 256:(i + 1) * 256, c0:c1].rearrange(
                "(j p) c -> p j c", p=128)
            qT_sl = lambda i: qT_scr[:, i * 256:(i + 1) * 256].rearrange(
                "(t p) c -> p t c", p=128)
            kT_sl = lambda i: kT_scr[:, i * 256:(i + 1) * 256].rearrange(
                "(t p) c -> p t c", p=128)
            d3 = lambda ap, n: ap.rearrange("p (j c) -> p j c", j=n)
            with contextlib.ExitStack() as ar:
                dd0 = sb("p3dd0", [128, H * 256], st=ar)
                qdb = sb("p3qdb", [128, (H // 2) * 256], st=ar)
                kb = Rot([sb("p3k0", [128, 2 * DM], st=ar)])
                vb = Rot([sb("p3v0", [128, 2 * DM], st=ar)])
                qs = Rot([sb("p3q0", [128, NMT * 256], st=ar)])
                ks = Rot([sb("p3kT0", [128, NMT * 256], st=ar)])
                kk = Rot([sb(f"p3kk{i}", [128, 64], st=ar) for i in range(2)])
                qq = Rot([sb(f"p3qq{i}", [64, 256], st=ar) for i in range(2)])
                aw0 = Rot([sb(f"p3a0{i}", [128, 256], st=ar) for i in range(2)])
                aw1 = Rot([sb(f"p3a1{i}", [128, 128], st=ar) for i in range(2)])
                asm = Rot([sb(f"p3asm{i}", [128, 2 * DM], st=ar) for i in range(2)])
                Scur = [sb(f"p3S{i}", [64, H * D], st=ar) for i in range(2)]
                pa0 = [pst(f"p3pa0{i}", [128, 256], ar) for i in range(2)]
                pa1 = [pst(f"p3pa1{i}", [128, 128], ar) for i in range(2)]
                pi0 = pst("p3pi0", [128, 64], ar)
                pi1 = pst("p3pi1", [128, 64], ar)
                cps = [pst(f"p3cp{i}", [64, 64], ar) for i in range(2)]
                P.dma("sync", dd0[:], dd0_in, "p3c")
                P.dma("sync", qdb[:], qdb_in, "p3c")
                P.wait("vector", "p3c")
                P.wait("vector", "p3gv", P.val("p3gv"))
                P.op("vector", lambda e: e.tensor_copy(Scur[0][:], Sin_st[:]), sem="p3sv2")
                awi = 0
                n_sv = P.val("p3sv2")
                n_ie = 0
                sv_prev_block = P.val("p3sv2")
                for i in range(NBLK):
                    Sa, Sb = Scur[i % 2], Scur[(i + 1) % 2]
                    jk, kbuf = kb.next(P, "sync")
                    b1 = P.dma("sync", d3(kbuf[:], 2), kv_r(i, DM, 2 * DM), "p3bk")
                    jv, vbuf = vb.next(P, "sync")
                    b2 = P.dma("sync", d3(vbuf[:], 2), kv_r(i, 2 * DM, 3 * DM), "p3bv")
                    jq, qsb = qs.next(P, "sync")
                    b3 = P.dma("sync", d3(qsb[:], NMT), qT_sl(i), "p3bq")
                    jkt, ksb = ks.next(P, "sync")
                    b4 = P.dma("sync", d3(ksb[:], NMT), kT_sl(i), "p3bt")
                    for s_, v_ in (("p3bk", b1), ("p3bv", b2), ("p3bq", b3),
                                   ("p3bt", b4)):
                        P.wait("tensor", s_, v_)
                        P.wait("vector", s_, v_)
                    P.wait("tensor", "p3sv2", sv_prev_block)
                    ja, abuf = asm.next(P, "vector")
                    for hh in range(H):
                        p0 = 64 * (hh % 2)
                        hcol = (hh // 2) * 256
                        ba = awi % 2
                        if awi >= 2:
                            P.wait("tensor", "p3de", awi - 1)
                        P.op("tensor", lambda e, ba=ba, ksb=ksb, qsb=qsb, p0=p0, hcol=hcol:
                             e.matmul(pa0[ba][:], ksb[p0:p0 + 64, hcol:hcol + 128],
                                      qsb[p0:p0 + 64, hcol:hcol + 256],
                                      start=True, stop=True))
                        tv = P.op("tensor", lambda e, ba=ba, ksb=ksb, qsb=qsb, p0=p0, hcol=hcol:
                                  e.matmul(pa1[ba][:], ksb[p0:p0 + 64, hcol + 128:hcol + 256],
                                           qsb[p0:p0 + 64, hcol + 128:hcol + 256],
                                           start=True, stop=True), sem="p3aw")
                        ja0, a0b = aw0.next(P, "vector")
                        ja1, a1b = aw1.next(P, "vector")
                        P.wait("vector", "p3aw", tv)
                        P.op("vector", lambda e, ba=ba, a0b=a0b, hh=hh: e.tensor_tensor(
                            a0b[:], pa0[ba][:], dd0[:, hh * 256:(hh + 1) * 256], ALU.mult))
                        av = P.op("vector", lambda e, ba=ba, a1b=a1b, hh=hh: e.tensor_tensor(
                            a1b[:], pa1[ba][:], dd0[:, hh * 256:hh * 256 + 128], ALU.mult),
                            sem="p3de")
                        jqq, qqb = qq.next(P, "vector")
                        qv = P.op("vector",
                                  lambda e, qqb=qqb, qsb=qsb, qdb=qdb, p0=p0, hcol=hcol:
                                  e.tensor_tensor(qqb[:], qsb[p0:p0 + 64, hcol:hcol + 256],
                                                  qdb[p0:p0 + 64, hcol:hcol + 256],
                                                  ALU.mult), sem="p3qe")
                        P.wait("tensor", "p3de", av)
                        P.wait("tensor", "p3qe", qv)
                        if n_ie >= 1:
                            P.wait("tensor", "p3ie", n_ie)
                        P.op("tensor", lambda e, a0b=a0b, vbuf=vbuf, hh=hh: e.matmul(
                            pi0[:], a0b[:, 0:128], vbuf[:, hh * 64: hh * 64 + 64],
                            start=True, stop=False))
                        P.op("tensor", lambda e, qqb=qqb, Sa=Sa, hh=hh: e.matmul(
                            pi0[:], qqb[:, 0:128], Sa[:, hh * 64:(hh + 1) * 64],
                            start=False, stop=True), sem="p3it")
                        P.op("tensor", lambda e, a0b=a0b, vbuf=vbuf, hh=hh: e.matmul(
                            pi1[:], a0b[:, 128:256], vbuf[:, hh * 64: hh * 64 + 64],
                            start=True, stop=False))
                        P.op("tensor", lambda e, a1b=a1b, vbuf=vbuf, hh=hh: e.matmul(
                            pi1[:], a1b[:], vbuf[:, DM + hh * 64: DM + hh * 64 + 64],
                            start=False, stop=False))
                        itv = P.op("tensor", lambda e, qqb=qqb, Sa=Sa, hh=hh: e.matmul(
                            pi1[:], qqb[:, 128:256], Sa[:, hh * 64:(hh + 1) * 64],
                            start=False, stop=True), sem="p3it")
                        aw0.gate(ja0, "p3it", itv)
                        aw1.gate(ja1, "p3it", itv)
                        qq.gate(jqq, "p3it", itv)
                        bp = hh % 2
                        if n_sv >= 2:
                            P.wait("tensor", "p3sv2", n_sv - 1)
                        ctv = 0
                        for jh in range(2):
                            jkk, kkb = kk.next(P, "vector")
                            kv_ = P.op("vector",
                                       lambda e, kbuf=kbuf, hh=hh, jh=jh, kkb=kkb:
                                       e.tensor_scalar(
                                           kkb[:],
                                           kbuf[:, jh * DM + hh * 64: jh * DM + hh * 64 + 64],
                                           kdc[:, hh * 2 + jh: hh * 2 + jh + 1],
                                           None, ALU.mult), sem="p3kv2")
                            P.wait("tensor", "p3kv2", kv_)
                            ctv = P.op("tensor",
                                       lambda e, bp=bp, kkb=kkb, vbuf=vbuf, hh=hh, jh=jh:
                                       e.matmul(cps[bp][:], kkb[:],
                                                vbuf[:, jh * DM + hh * 64:
                                                     jh * DM + hh * 64 + 64],
                                                start=(jh == 0), stop=(jh == 1)),
                                       sem=("p3tc2" if jh == 1 else None))
                            kk.gate(jkk, "p3tc2", P.val("p3tc2") if jh == 1 else
                                    P.val("p3tc2") + 1)
                        P.wait("vector", "p3it", itv)
                        P.op("vector", lambda e, abuf=abuf, hh=hh: e.tensor_copy(
                            abuf[:, hh * 64:(hh + 1) * 64], pi0[:]))
                        P.op("vector", lambda e, abuf=abuf, hh=hh: e.tensor_copy(
                            abuf[:, DM + hh * 64: DM + hh * 64 + 64], pi1[:]), sem="p3ie")
                        n_ie += 1
                        P.wait("vector", "p3tc2", ctv)
                        P.op("vector", lambda e, Sa=Sa, Sb=Sb, bp=bp, hh=hh:
                             e.scalar_tensor_tensor(
                                 Sb[:, hh * 64:(hh + 1) * 64], Sa[:, hh * 64:(hh + 1) * 64],
                                 float(_BD[hh]), cps[bp][:], ALU.mult, ALU.add), sem="p3sv2")
                        n_sv += 1
                        awi += 1
                    sv_prev_block = P.val("p3sv2")
                    kb.gate(jk, "p3kv2", P.val("p3kv2"))
                    vb.gate(jv, "p3tc2", P.val("p3tc2"))
                    qs.gate(jq, "p3qe", P.val("p3qe"))
                    ks.gate(jkt, "p3aw", P.val("p3aw"))
                    P.wait("scalar", "p3ie", P.val("p3ie"))
                    P.dma("scalar", attn_scr[i * 256:i * 256 + 128, :],
                          abuf[:, 0:DM], f"p3s{ja}")
                    sv2 = P.dma("scalar", attn_scr[i * 256 + 128:(i + 1) * 256, :],
                                abuf[:, DM:2 * DM], f"p3s{ja}")
                    asm.gate(ja, f"p3s{ja}", sv2)
                P.drain("sync", "p3s0", "p3s1")
                P.barrier()

        def phase4():
            b_tv = P.val("p2tv")
            b_z = P.val("p2z")
            b_wv = P.val("p2wv")
            b_mt = P.val("p2mt")
            b_pf = P.val("p2pf")
            with contextlib.ExitStack() as outer:
                zT = sb("p4zT", [128, NMT * TOK], st=outer)
                with contextlib.ExitStack() as ar:
                    at = Rot([sb("p4at0", [128, DM], st=ar)])
                    gt = Rot([sb("p4gt0", [128, DM], st=ar)])
                    awb = sb("p4awb", [128, DM], st=ar)
                    P.dma("sync", awb[:], attnw_b, "p2lw0")
                    P.wait("vector", "p2lw0")
                    sqt = [sb(f"p4sq{i}", [128, DM], st=ar) for i in range(2)]
                    zt = [sb(f"p4zt{i}", [128, DM], st=ar) for i in range(2)]
                    tmp = [sb(f"p4tmp{i}", [128, 1], st=ar) for i in range(4)]
                    tpp = [pst(f"p4tp{i}", [128, 512], ar) for i in range(2)]
                    zT_r = zT[:].rearrange("p (k t) -> p k t", k=NMT)
                    for m in range(NMT):
                        ja, abuf = at.next(P, "sync")
                        a1 = P.dma("sync", abuf[:], attn_scr[m * 128:(m + 1) * 128, :],
                                   "p2ld0")
                        jg, gbuf = gt.next(P, "sync")
                        g1 = P.dma("sync", gbuf[:], gate_scr[m * 128:(m + 1) * 128, :],
                                   "p2ld1")
                        b = m % 2
                        P.wait("scalar", "p2ld0", a1)
                        P.wait("vector", "p2ld1", g1)
                        if m >= 2:
                            P.wait("scalar", "p2z", b_z + m - 1)
                            P.wait("vector", "p2tv", b_tv + (m - 1) * 4)
                        P.op("scalar", lambda e, b=b, abuf=abuf: e.activation(
                            sqt[b][:], abuf[:], AF.Square), sem="p2ss")
                        P.wait("vector", "p2ss")
                        P.op("vector", lambda e, b=b: e.reduce_sum(
                            tmp[0][:], sqt[b][:], axis=AX.X), sem="p2vs")
                        P.wait("scalar", "p2vs")
                        P.op("scalar", lambda e: e.activation(
                            tmp[2][:], tmp[0][:], AF.Sqrt,
                            scale=cinvd[:, 0:1], bias=ceps[:, 0:1]), sem="p2ss")
                        P.wait("vector", "p2ss")
                        P.op("vector", lambda e: e.reciprocal(tmp[3][:], tmp[2][:]))
                        av = P.op("vector", lambda e, b=b, abuf=abuf:
                                  e.scalar_tensor_tensor(
                                      zt[b][:], abuf[:], tmp[3][:, 0:1], awb[:],
                                      ALU.mult, ALU.mult), sem="p2y")
                        at.gate(ja, "p2y", av)
                        gv = P.op("vector", lambda e, b=b, gbuf=gbuf: e.tensor_tensor(
                            zt[b][:], zt[b][:], gbuf[:], ALU.mult), sem="p2z")
                        gt.gate(jg, "p2z", gv)
                        P.wait("tensor", "p2z", gv)
                        for g in range(4):
                            gi = m * 4 + g
                            bg = gi % 2
                            if gi >= 2:
                                P.wait("tensor", "p2tv", b_tv + gi - 1)
                            for t4 in range(4):
                                ft = g * 4 + t4
                                P.op("tensor", lambda e, b=b, bg=bg, ft=ft, t4=t4:
                                     e.transpose(tpp[bg][:, t4 * 128:(t4 + 1) * 128],
                                                 zt[b][:, ft * 128:(ft + 1) * 128],
                                                 ident[:]),
                                     sem=("p2tt" if t4 == 3 else None))
                            P.wait("vector", "p2tt")
                            P.op("vector", lambda e, bg=bg, g=g, m=m: e.tensor_copy(
                                zT_r[:, g * 4:(g + 1) * 4, m * 128:(m + 1) * 128],
                                tpp[bg][:].rearrange("p (f t) -> p f t", f=4)), sem="p2tv")
                    P.barrier()
                with contextlib.ExitStack() as ar:
                    wf = Rot([sb(f"p4wf{i}", [128, DM], st=ar) for i in range(2)])
                    Wt = sb("p4Wt", [128, 16 * 256], st=ar)
                    stg = Rot([sb(f"p4sg{i}", [128, 256], st=ar) for i in range(2)])
                    xr = Rot([sb(f"p4xr{i}", [128, 256], st=ar) for i in range(2)])
                    tpp = [pst(f"p4tq{i}", [128, 512], ar) for i in range(2)]
                    mmp = [pst(f"p4mm{i}", [128, 256], ar) for i in range(2)]
                    Wt_r = Wt[:].rearrange("p (k r) -> p k r", k=16)
                    tpi = 0
                    mmi = 0
                    for c in range(8):
                        if c > 0:
                            P.wait("vector", "p2mt", b_mt + mmi)
                        wv = 0
                        for s4 in range(2):
                            j, wbuf = wf.next(P, "sync")
                            lv = P.dma("sync", wbuf[:],
                                       outw_in[c * 256 + s4 * 128:
                                               c * 256 + (s4 + 1) * 128, :], f"p2lw{j}")
                            P.wait("tensor", f"p2lw{j}", lv)
                            for g in range(4):
                                bg = tpi % 2
                                if tpi >= 2:
                                    P.wait("tensor", "p2wv", b_wv + tpi - 1)
                                for t4 in range(4):
                                    kt = g * 4 + t4
                                    P.op("tensor",
                                         lambda e, wbuf=wbuf, bg=bg, kt=kt, t4=t4:
                                         e.transpose(tpp[bg][:, t4 * 128:(t4 + 1) * 128],
                                                     wbuf[:, kt * 128:(kt + 1) * 128],
                                                     ident[:]),
                                         sem=("p2wt" if t4 == 3 else None))
                                P.wait("vector", "p2wt")
                                wv = P.op("vector", lambda e, bg=bg, g=g, s4=s4:
                                          e.tensor_copy(
                                              Wt_r[:, g * 4:(g + 1) * 4,
                                                   s4 * 128:(s4 + 1) * 128],
                                              tpp[bg][:].rearrange("p (f t) -> p f t", f=4)),
                                          sem="p2wv")
                                tpi += 1
                            wf.gate(j, "p2wv", wv)
                        P.wait("tensor", "p2wv", wv)
                        for m in range(NMT):
                            bm = mmi % 2
                            if mmi >= 2:
                                P.wait("tensor", "p2pf", b_pf + mmi - 1)
                            tvv = 0
                            for kt in range(16):
                                tvv = P.op("tensor", lambda e, bm=bm, m=m, kt=kt: e.matmul(
                                    mmp[bm][:],
                                    zT[:, kt * TOK + m * 128: kt * TOK + m * 128 + 128],
                                    Wt[:, kt * 256:(kt + 1) * 256],
                                    start=(kt == 0), stop=(kt == 15)),
                                    sem=("p2mt" if kt == 15 else None))
                            mmi += 1
                            jx, xbuf = xr.next(P, "sync")
                            lv = P.dma("sync", xbuf[:],
                                       x1_scr[m * 128:(m + 1) * 128,
                                              c * 256:(c + 1) * 256], f"p2ld{jx}")
                            js, sbuf_ = stg.next(P, "vector")
                            P.wait("vector", "p2mt", tvv)
                            P.wait("vector", f"p2ld{jx}", lv)
                            xv = P.op("vector", lambda e, bm=bm, xbuf=xbuf, sbuf_=sbuf_:
                                      e.tensor_tensor(sbuf_[:], mmp[bm][:], xbuf[:],
                                                      ALU.add), sem="p2pf")
                            xr.gate(jx, "p2pf", xv)
                            P.wait("scalar", "p2pf", xv)
                            sv = P.dma("scalar",
                                       x2_scr[m * 128:(m + 1) * 128,
                                              c * 256:(c + 1) * 256], sbuf_[:],
                                       f"p2st{js}")
                            stg.gate(js, f"p2st{js}", sv)
                    P.drain("sync", "p2st0", "p2st1")
                    P.barrier()

        phase0()
        if max_phase >= 1:
            phase_ffn(0, x_in, x1_scr)
        if max_phase >= 2:
            phase2()
        if max_phase >= 3:
            phase3a()
            phase3b()
        if max_phase >= 4:
            phase4()
        if max_phase >= 5:
            phase_ffn(1, x2_scr, y_out)
        else:
            for m in range(NMT):
                P.dma("gpsimd", y_out[m * 128:(m + 1) * 128, :],
                      x1_scr[m * 128:(m + 1) * 128, :], "fcp")
            P.wait("gpsimd", "fcp")
        for t, ap in tap_outs.items():
            if ap is None:
                continue
            P.dma("gpsimd", ap, tap_src[t], "dbg")
        if P.val("dbg") if "dbg" in P.cnt else 0:
            P.wait("gpsimd", "dbg")
        P.barrier()

        block = big.enter_context(nc.Block())
        P.finalize(block)
    return nc


def _stage_inputs(inputs):
    x = np.ascontiguousarray(np.asarray(inputs["L0_attn_hidden_states"],
                                        dtype=np.float32).reshape(B * S, DM))
    qkv_w = np.asarray(inputs["L0_attn_qkv_w"], np.float32)
    out_w = np.ascontiguousarray(np.asarray(inputs["L0_attn_out_w"], np.float32))
    gate_w = np.asarray(inputs["L0_attn_gate_w"], np.float32)
    norm_w = np.asarray(inputs["L0_attn_norm_w"], np.float32)
    w1 = np.ascontiguousarray(np.asarray(inputs["L0_ffn_w_gate_up"], np.float32))
    w2 = np.ascontiguousarray(np.asarray(inputs["L0_ffn_w_down"], np.float32))
    fn = np.asarray(inputs["L0_ffn_norm_weight"], np.float32)

    wat = np.ascontiguousarray(np.concatenate([qkv_w[_perm_qkv()], gate_w], axis=0))
    ffnw_b = np.ascontiguousarray(np.broadcast_to(fn[None, :], (128, DM)))
    attnw_b = np.ascontiguousarray(np.broadcast_to(norm_w[None, :], (128, DM)))
    ident = np.eye(128, dtype=np.float32)

    in_maps = []
    for c in range(NCORES):
        in_maps.append({
            "x_in": np.ascontiguousarray(x[c * TOK:(c + 1) * TOK]),
            "w_attn": wat,
            "outw_in": out_w,
            "w1_in": w1,
            "w2_in": w2,
            "w1_sl": np.ascontiguousarray(w1[c * (F2 // 8):(c + 1) * (F2 // 8)]),
            "w2_sl": np.ascontiguousarray(w2[c * (DM // 8):(c + 1) * (DM // 8)]),
            "ffnw_b": ffnw_b,
            "attnw_b": attnw_b,
            "ident_in": ident,
            "dd0_in": _DD0,
            "qdb_in": _QDB,
            "kdc_in": _KDC,
            "tpred_in": _t_pred(c),
        })
    return in_maps


def kernel(**inputs):
    nc = build_nc()
    in_maps = _stage_inputs(inputs)
    res = run_bass_kernel_spmd(nc, in_maps, core_ids=list(range(NCORES)))
    out = np.concatenate([np.asarray(res.results[c]["y_out"], dtype=np.float32)
                          for c in range(NCORES)], axis=0)
    return out.reshape(B, S, DM)
